# version 01
# baseline (speedup 1.0000x reference)
import os
import numpy as np
import ml_dtypes
from contextlib import ExitStack

import concourse.bass as bass
import concourse.mybir as mybir
from concourse.bass_utils import run_bass_kernel_spmd

F32 = mybir.dt.float32
BF16 = mybir.dt.bfloat16
AF = mybir.ActivationFunctionType
ALU = mybir.AluOpType
AX = mybir.AxisListType
BF = ml_dtypes.bfloat16

ENGINES = ("sync", "tensor", "vector", "scalar", "gpsimd")

D = 1024
L = 2048
NSEQ = 2
T = NSEQ * L
NC = D // 128
NCORE = 8

IDENTF = np.eye(128, dtype=np.float32)
IDENTB = np.eye(128, dtype=BF)


class Dep:
    __slots__ = ("sem", "val", "eng")

    def __init__(self, sem, val, eng):
        self.sem = sem
        self.val = val
        self.eng = eng


class Buf:
    __slots__ = ("name", "ap", "last_write", "reads")

    def __init__(self, name, ap):
        self.name = name
        self.ap = ap
        self.last_write = None
        self.reads = []

    def __getitem__(self, idx):
        return self.ap.__getitem__(idx)


class Ctx:

    def __init__(self, name="prog"):
        self.nc = bass.Bass("TRN2", target_bir_lowering=False)
        self.stack = ExitStack()
        self.sems = {}
        self.semval = {}
        self.thunks = {e: [] for e in ENGINES}
        self.dma_sems = {}
        self._buf_names = set()
        self.constvals = []
        self._constmap = {}
        self.cconst = None
        self.ident_f = None
        self.ident_b = None
        self.waited = {e: {} for e in ENGINES}
        self._banks = {}
        self._dmaring = {}
        self._dmaslot = {}
        self.DMA_RING = 12

    def sem(self, name):
        if name not in self.sems:
            self.sems[name] = self.stack.enter_context(self.nc.semaphore(name))
            self.semval[name] = 0
        return self.sems[name]

    def sbuf(self, name, shape, dtype):
        assert name not in self._buf_names, name
        self._buf_names.add(name)
        h = self.stack.enter_context(self.nc.sbuf_tensor(name, shape, dtype))
        return Buf(name, h[:])

    def psum(self, name, shape, dtype=F32):
        assert name not in self._buf_names, name
        self._buf_names.add(name)
        h = self.stack.enter_context(self.nc.psum_tensor(name, shape, dtype))
        return Buf(name, h[:])

    def vfence(self, buf):
        if not hasattr(self, "_junk"):
            self._junk = self.sbuf("junk_fence", [1, 8], F32)
        return self.rop("vector", lambda e: e.memset(self._junk.ap, 0.0),
                        reads=[], writes=[buf, self._junk])

    def _psum_all(self):
        if not hasattr(self, "_psall"):
            self._psall = self.stack.enter_context(
                self.nc.psum_tensor("psum_all", [128, 8, 512], F32))
        return self._psall

    def bank(self, i):
        if i not in self._banks:
            assert 0 <= i < 8
            self._banks[i] = Buf(f"pbank{i}", self._psum_all()[:][:, i, :])
        return self._banks[i]

    def pspan(self, i, k):
        return self._psum_all()[:][:, i:i + k, :].rearrange("p a b -> p (a b)")

    def dram(self, name, shape, dtype, kind="Internal"):
        ap = self.nc.dram_tensor(name, shape, dtype, kind=kind).ap()
        return Buf(name, ap)

    def _wait(self, engine, sem_name, val):
        if val <= 0:
            return
        w = self.waited[engine]
        if w.get(sem_name, 0) >= val:
            return
        w[sem_name] = val
        self.sem(sem_name)
        sems = self.sems
        self.thunks[engine].append(lambda eng: eng.wait_ge(sems[sem_name], val))

    def _emit(self, engine, build, inc):
        name, n = inc
        self.sem(name)
        self.semval[name] += n
        tgt = self.semval[name]
        sems = self.sems

        def thunk(eng, build=build, name=name, n=n):
            build(eng).then_inc(sems[name], n)
        self.thunks[engine].append(thunk)
        return tgt

    def rop(self, engine, build, reads=(), writes=(), dma=False, sem=None):
        if sem is None:
            if dma:
                ring = self._dmaring.setdefault(engine, [0] * self.DMA_RING)
                slot = self._dmaslot.get(engine, 0)
                self._dmaslot[engine] = (slot + 1) % self.DMA_RING
                sem = f"d_{engine}_{slot}"
                if ring[slot] > 0:
                    self._wait(engine, sem, 16 * ring[slot])
                ring[slot] += 1
            else:
                sem = "c_" + engine
        if dma:
            self.dma_sems[sem] = engine
        n = 16 if dma else 1
        deps = []
        for b in reads:
            if b.last_write is not None:
                deps.append(b.last_write)
        for b in writes:
            if b.last_write is not None:
                deps.append(b.last_write)
            deps.extend(b.reads)
        for d in deps:
            if (d.eng == engine == "tensor" and not dma
                    and not d.sem.startswith("d_")):
                continue
            self._wait(engine, d.sem, d.val)
        tgt = self._emit(engine, build, (sem, n))
        dep = Dep(sem, tgt, engine)
        for b in reads:
            b.reads.append(dep)
        for b in writes:
            b.last_write = dep
            b.reads = []
        return dep

    def dma(self, engine, dst, dst_ap, src, src_ap, **kw):
        reads = [src] if src is not None else []
        writes = [dst] if dst is not None else []
        return self.rop(engine,
                        lambda e: e.dma_start(out=dst_ap, in_=src_ap, **kw),
                        reads=reads, writes=writes, dma=True)

    def constcol(self, v):
        v = float(v)
        if v not in self._constmap:
            self._constmap[v] = len(self.constvals)
            self.constvals.append(v)
            assert len(self.constvals) <= 16
        i = self._constmap[v]
        return self.cconst[:, i:i + 1]

    def barrier(self):
        for sem, eng in self.dma_sems.items():
            self._wait(eng, sem, self.semval[sem])
        self.sem("bar")
        for e in ENGINES:
            self.semval["bar"] += 1
            sems = self.sems
            self.thunks[e].append(lambda eng, s=sems: eng.sem_inc(s["bar"], 1))
        tgt = self.semval["bar"]
        for e in ENGINES:
            self._wait(e, "bar", tgt)

    def build(self):
        with self.stack:
            with self.nc.Block() as block:
                for e in ENGINES:
                    ths = self.thunks[e]
                    getattr(block, e)(lambda eng, ths=ths: [t(eng) for t in ths])
        return self.nc


class Launch:
    def __init__(self, name, inputs, outputs, emit):
        self.name = name
        self.inputs = inputs
        self.outputs = outputs
        self.emit = emit
        self._nc = None

    def build(self):
        if self._nc is not None:
            return self._nc
        ctx = Ctx(self.name)
        self._ctx = ctx
        aps = {}
        std = [("cconst", (128, 16), F32, None),
               ("identf", (128, 128), F32, None),
               ("identb", (128, 128), BF16, None)]
        for (n, shape, dt, _src) in std + list(self.inputs):
            aps[n] = ctx.dram(n, list(shape), dt, kind="ExternalInput")
        for (n, shape, dt) in self.outputs:
            aps[n] = ctx.dram(n, list(shape), dt, kind="ExternalOutput")
        ctx.cconst = ctx.sbuf("cconst_sb", [128, 16], F32)
        idf = ctx.sbuf("identf_sb", [128, 128], F32)
        idb = ctx.sbuf("identb_sb", [128, 128], BF16)
        ctx.ident_f = idf
        ctx.ident_b = idb
        ctx.dma("sync", ctx.cconst, ctx.cconst.ap, aps["cconst"], aps["cconst"].ap)
        ctx.dma("sync", idf, idf.ap, aps["identf"], aps["identf"].ap)
        ctx.dma("sync", idb, idb.ap, aps["identb"], aps["identb"].ap)
        self.emit(ctx, aps)
        ctx.barrier()
        self._nc = ctx.build()
        return self._nc

    def run(self, states, **kw):
        nc = self.build()
        cvals = np.zeros((128, 16), np.float32)
        for i, v in enumerate(self._ctx.constvals):
            cvals[:, i] = v
        in_maps = []
        for st in states:
            m = {"cconst": cvals, "identf": IDENTF, "identb": IDENTB}
            for (n, shape, dt, src) in self.inputs:
                a = src(st)
                assert tuple(a.shape) == tuple(shape), (self.name, n, a.shape, shape)
                m[n] = a
            in_maps.append(m)
        res = run_bass_kernel_spmd(nc, in_maps, core_ids=list(range(len(states))), **kw)
        for st, r in zip(states, res.results):
            for (n, shape, dt) in self.outputs:
                st[n] = r[n]
        return res


def emit_stage_A(ctx, xin, x_fm, xn0_fm):
    NT = T // 128
    xt = [ctx.sbuf(f"A_xt{i}", [128, D], F32) for i in range(2)]
    sq = [ctx.sbuf(f"A_sq{i}", [128, D], F32) for i in range(2)]
    xnt = [ctx.sbuf(f"A_xn{i}", [128, D], F32) for i in range(2)]
    st = [ctx.sbuf(f"A_st{i}", [128, 8], F32) for i in range(2)]
    xev = [ctx.sbuf(f"A_xe{i}", [128, D], F32) for i in range(2)]
    xnev = [ctx.sbuf(f"A_xne{i}", [128, D], F32) for i in range(2)]
    pss = [ctx.bank(0), ctx.bank(1)]

    x_fm_v = x_fm.ap.rearrange("(c p) t -> p c t", p=128)
    xn_fm_v = xn0_fm.ap.rearrange("(c p) t -> p c t", p=128)
    epsc = ctx.constcol(1e-5)

    for ti in range(NT):
        b = ti % 2
        X, SQ, XN, S = xt[b], sq[b], xnt[b], st[b]
        ctx.dma("sync", X, X.ap, xin, xin.ap[ti * 128:(ti + 1) * 128, :])
        ctx.rop("vector", lambda e, X=X, SQ=SQ: e.tensor_mul(SQ.ap, X.ap, X.ap),
                reads=[X], writes=[SQ])
        ctx.rop("vector", lambda e, X=X, S=S: e.reduce_sum(S[:, 0:1], X.ap, AX.X),
                reads=[X], writes=[S])
        ctx.rop("vector", lambda e, SQ=SQ, S=S: e.reduce_sum(S[:, 1:2], SQ.ap, AX.X),
                reads=[SQ], writes=[S])
        ctx.rop("vector", lambda e, S=S: e.tensor_scalar_mul(S[:, 2:3], S[:, 0:1], 1.0 / D),
                reads=[S], writes=[S])
        ctx.rop("vector", lambda e, S=S: e.tensor_scalar_mul(S[:, 3:4], S[:, 1:2], 1.0 / D),
                reads=[S], writes=[S])
        ctx.rop("vector", lambda e, S=S: e.tensor_mul(S[:, 4:5], S[:, 2:3], S[:, 2:3]),
                reads=[S], writes=[S])
        ctx.rop("vector", lambda e, S=S: e.tensor_sub(S[:, 5:6], S[:, 3:4], S[:, 4:5]),
                reads=[S], writes=[S])
        ctx.rop("scalar", lambda e, S=S, epsc=epsc: e.activation(S[:, 6:7], S[:, 5:6], AF.Sqrt, bias=epsc),
                reads=[S, ctx.cconst], writes=[S])
        ctx.rop("vector", lambda e, S=S: e.reciprocal(S[:, 7:8], S[:, 6:7]),
                reads=[S], writes=[S])
        ctx.rop("vector", lambda e, X=X, XN=XN, S=S: e.tensor_scalar(
            XN.ap, X.ap, S[:, 2:3], S[:, 7:8], ALU.subtract, ALU.mult),
            reads=[X, S], writes=[XN])
        for g in range(4):
            pb = (ti * 4 + g) % 2
            srcbuf = X if g < 2 else XN
            P = pss[pb]
            for k in range(4):
                dc = (g % 2) * 4 + k
                ctx.rop("tensor", lambda e, sb=srcbuf, dc=dc, P=P, k=k: e.transpose(
                    P[:, k * 128:(k + 1) * 128], sb.ap[:, dc * 128:(dc + 1) * 128],
                    ctx.ident_f.ap),
                    reads=[srcbuf, ctx.ident_f], writes=[P])
            dstbuf = xev[b] if g < 2 else xnev[b]
            off = (g % 2) * 512
            ctx.rop("vector", lambda e, db=dstbuf, off=off, P=P: e.tensor_copy(
                db.ap[:, off:off + 512], P.ap), reads=[P], writes=[dstbuf])
        ctx.dma("sync", x_fm, x_fm_v[:, :, ti * 128:(ti + 1) * 128],
                xev[b], xev[b].ap.rearrange("p (c t) -> p c t", t=128))
        ctx.dma("sync", xn0_fm, xn_fm_v[:, :, ti * 128:(ti + 1) * 128],
                xnev[b], xnev[b].ap.rearrange("p (c t) -> p c t", t=128))


def launch_A():
    return Launch(
        "A",
        inputs=[("xin", (T, D), F32, lambda st: st["xin"])],
        outputs=[("x_fm", (D, T), F32), ("xn0_fm", (D, T), F32)],
        emit=lambda ctx, aps: emit_stage_A(ctx, aps["xin"], aps["x_fm"], aps["xn0_fm"]),
    )


def shard_inputs(inputs):
    x = np.ascontiguousarray(inputs["L0_attn_x"])
    states = []
    for c in range(NCORE):
        xs = x[c * NSEQ:(c + 1) * NSEQ].reshape(T, D)
        states.append({"xin": np.ascontiguousarray(xs), "weights": inputs})
    return states





def load_param_cols(ctx, tag, src_buf, n=D):
    nc_ = n // 128
    stage = ctx.sbuf(f"{tag}_pst", [nc_, 128], F32)
    ps = ctx.bank(0)
    out = ctx.sbuf(f"{tag}_pc", [128, nc_], F32)
    ctx.dma("sync", stage, stage.ap,
            src_buf, src_buf.ap.rearrange("(c p) -> c p", p=128))
    ctx.rop("tensor", lambda e: e.transpose(ps.ap[:, 0:nc_], stage.ap, ctx.ident_f.ap[0:nc_, 0:nc_]),
            reads=[stage, ctx.ident_f], writes=[ps])
    ctx.rop("vector", lambda e: e.tensor_copy(out.ap, ps.ap[:, 0:nc_]), reads=[ps], writes=[out])
    return out


def load_wT_bf(ctx, tag, w_buf, M, K):
    mt, kt = M // 128, K // 128
    wT = ctx.sbuf(f"{tag}_wT", [128, kt, mt, 128], BF16)
    stg = [ctx.sbuf(f"{tag}_ws{i}", [128, 128], BF16) for i in range(2)]
    pss = [ctx.bank(0), ctx.bank(1)]
    i = 0
    for mc in range(mt):
        for kc in range(kt):
            b = i % 2
            i += 1
            S, P = stg[b], pss[b]
            pv = P.ap.bitcast(BF16)[:, 0:128]
            ctx.dma("gpsimd", S, S.ap,
                    w_buf, w_buf.ap[mc * 128:(mc + 1) * 128, kc * 128:(kc + 1) * 128])
            ctx.rop("tensor", lambda e, S=S, pv=pv: e.transpose(pv, S.ap, ctx.ident_b.ap),
                    reads=[S, ctx.ident_b], writes=[P])
            ctx.rop("vector", lambda e, pv=pv, kc=kc, mc=mc: e.tensor_copy(
                wT.ap[:, kc, mc, :], pv), reads=[P], writes=[wT])
    return wT


def make_ones(ctx):
    onesb = ctx.sbuf("onesb", [128, 1], BF16)
    onesf = ctx.sbuf("onesf_row", [1, 128], F32)
    ctx.rop("vector", lambda e: e.memset(onesb.ap, 1.0), writes=[onesb])
    ctx.rop("vector", lambda e: e.memset(onesf.ap, 1.0), writes=[onesf])
    return onesb, onesf


def emit_stage_B(ctx, aps):
    xn_fm = aps["xn0_fm"]
    s0_fm = aps["s0_fm"]
    wrT = load_wT_bf(ctx, "wr", aps["W_r"], D, D)
    wiT = load_wT_bf(ctx, "wi", aps["W_i"], D, D)
    woT = load_wT_bf(ctx, "wo", aps["W_out"], D, D)
    cw = ctx.sbuf("B_cw", [128, NC, 4], F32)
    for c in range(NC):
        ctx.dma("sync", cw, cw.ap[:, c, :], aps["conv_w"],
                aps["conv_w"].ap[c * 128:(c + 1) * 128, :])
    cb = load_param_cols(ctx, "B_cb", aps["conv_b"])
    brc = load_param_cols(ctx, "B_br", aps["b_r"])
    bic = load_param_cols(ctx, "B_bi", aps["b_i"])
    lac = load_param_cols(ctx, "B_la", aps["log_a"])
    gac = load_param_cols(ctx, "B_ga", aps["gamma"])
    onesb, onesf = make_ones(ctx)

    la8 = ctx.sbuf("B_la8", [128, NC], F32)
    ctx.rop("scalar", lambda e: e.activation(la8.ap, lac.ap, AF.Sigmoid),
            reads=[lac], writes=[la8])
    ctx.rop("vector", lambda e: e.tensor_scalar_max(la8.ap, la8.ap, 1e-38),
            reads=[la8], writes=[la8])
    ctx.rop("scalar", lambda e: e.activation(la8.ap, la8.ap, AF.Ln),
            reads=[la8], writes=[la8])
    ctx.rop("vector", lambda e: e.tensor_scalar_mul(la8.ap, la8.ap, 8.0),
            reads=[la8], writes=[la8])

    xcv = ctx.sbuf("B_xcv", [128, NC, T], BF16)
    xnp = [ctx.sbuf(f"B_xnp{i}", [128, L + 3], BF16) for i in range(2)]
    xn_fm_v = xn_fm.ap.rearrange("(c p) t -> p c t", p=128)
    for c in range(NC):
        for s in range(NSEQ):
            b = (c * NSEQ + s) % 2
            XP = xnp[b]
            ctx.rop("vector", lambda e, XP=XP: e.memset(XP.ap[:, 0:3], 0.0), writes=[XP])
            ctx.dma("gpsimd", XP, XP.ap[:, 3:L + 3],
                    xn_fm, xn_fm_v[:, c, s * L:(s + 1) * L])
            tacc = ctx.sbuf(f"B_cvt{c}_{s}", [128, L], F32) if c == 0 and s == 0 else tacc
            ctx.rop("vector", lambda e, XP=XP, c=c: e.tensor_scalar(
                tacc.ap, XP.ap[:, 0:L], cw.ap[:, c, 0:1], None, ALU.mult),
                reads=[XP, cw], writes=[tacc])
            for j in range(1, 4):
                ctx.rop("vector", lambda e, XP=XP, c=c, j=j: e.scalar_tensor_tensor(
                    tacc.ap, XP.ap[:, j:j + L], cw.ap[:, c, j:j + 1], tacc.ap,
                    ALU.mult, ALU.add), reads=[XP, cw, tacc], writes=[tacc])
            ctx.rop("vector", lambda e, c=c, s=s: e.tensor_scalar_add(
                xcv.ap[:, c, s * L:(s + 1) * L], tacc.ap, cb.ap[:, c:c + 1]),
                reads=[tacc, cb], writes=[xcv])
            if "dbg_xcv" in aps:
                ctx.dma("sync", aps["dbg_xcv"],
                        aps["dbg_xcv"].ap.rearrange("(c p) t -> p c t", p=128)[:, c, s * L:(s + 1) * L],
                        xcv, xcv.ap[:, c, s * L:(s + 1) * L])

    h_dram = aps.get("dbg_h") or ctx.dram("B_h_dram", [D, T], BF16)
    h_dram_v = h_dram.ap.rearrange("(c p) t -> p c t", p=128)
    hs = [ctx.sbuf("B_hs0", [128, L], BF16)]
    zr_banks = [ctx.bank(i) for i in range(4)]
    zi_banks = [ctx.bank(4 + i) for i in range(4)]
    rt = ctx.sbuf("B_rt", [128, L], F32)
    it = ctx.sbuf("B_it", [128, L], F32)
    at = ctx.sbuf("B_at", [128, L], F32)
    t1 = ctx.sbuf("B_t1", [128, L], F32)
    t2 = rt
    for oc in range(NC):
        for s in range(NSEQ):
            for (banks, wT) in ((zr_banks, wrT), (zi_banks, wiT)):
                for nb in range(4):
                    ps = banks[nb]
                    for kc in range(NC):
                        ctx.rop("tensor", lambda e, ps=ps, wT=wT, nb=nb, kc=kc, oc=oc, s=s: e.matmul(
                            ps.ap,
                            wT.ap[:, kc, oc, :],
                            xcv.ap[:, kc, s * L + nb * 512: s * L + (nb + 1) * 512],
                            start=(kc == 0), stop=(kc == NC - 1)),
                            reads=[wT, xcv], writes=[ps])
            for nb in range(4):
                ctx.rop("scalar", lambda e, oc=oc, nb=nb: e.activation(
                    rt.ap[:, nb * 512:(nb + 1) * 512], zr_banks[nb].ap, AF.Sigmoid,
                    bias=brc.ap[:, oc:oc + 1]),
                    reads=[zr_banks[nb], brc], writes=[rt])
                ctx.rop("scalar", lambda e, oc=oc, nb=nb: e.activation(
                    it.ap[:, nb * 512:(nb + 1) * 512], zi_banks[nb].ap, AF.Sigmoid,
                    bias=bic.ap[:, oc:oc + 1]),
                    reads=[zi_banks[nb], bic], writes=[it])
            ctx.rop("scalar", lambda e, oc=oc: e.activation(at.ap, rt.ap, AF.Exp,
                                                            scale=la8.ap[:, oc:oc + 1]),
                    reads=[rt, la8], writes=[at])
            ctx.rop("vector", lambda e: e.tensor_mul(t1.ap, at.ap, at.ap),
                    reads=[at], writes=[t1])
            ctx.rop("vector", lambda e: e.tensor_scalar(t1.ap, t1.ap, -1.0, 1.0, ALU.mult, ALU.add),
                    reads=[t1], writes=[t1])
            ctx.rop("vector", lambda e: e.tensor_scalar_max(t1.ap, t1.ap, 0.0),
                    reads=[t1], writes=[t1])
            ctx.rop("scalar", lambda e: e.activation(t1.ap, t1.ap, AF.Sqrt),
                    reads=[t1], writes=[t1])
            ctx.rop("vector", lambda e, oc=oc, s=s: e.tensor_copy(
                t2.ap, xcv.ap[:, oc, s * L:(s + 1) * L]), reads=[xcv], writes=[t2])
            ctx.rop("vector", lambda e: e.tensor_mul(t1.ap, t1.ap, it.ap),
                    reads=[t1, it], writes=[t1])
            ctx.rop("vector", lambda e: e.tensor_mul(t2.ap, t1.ap, t2.ap),
                    reads=[t1, t2], writes=[t2])
            hb = hs[0]
            ctx.rop("vector", lambda e, hb=hb: e.tensor_tensor_scan(
                hb.ap, at.ap, t2.ap, 0.0, ALU.mult, ALU.add),
                reads=[at, t2], writes=[hb])
            ctx.dma("sync", h_dram, h_dram_v[:, oc, s * L:(s + 1) * L], hb, hb.ap)

    if "dbg_pc" in aps:
        ctx.dma("sync", aps["dbg_pc"], aps["dbg_pc"].ap[:, 0:NC], gac, gac.ap)
        ctx.dma("sync", aps["dbg_pc"], aps["dbg_pc"].ap[:, 8:8 + NC], la8, la8.ap)
        ctx.dma("sync", aps["dbg_pc"], aps["dbg_pc"].ap[:, 16:16 + NC], brc, brc.ap)
        ctx.dma("sync", aps["dbg_pc"], aps["dbg_pc"].ap[:, 24:24 + NC], cb, cb.ap)
    if "dbg_wT" in aps:
        ctx.dma("sync", aps["dbg_wT"], aps["dbg_wT"].ap.rearrange("p a b q -> p (a b q)"),
                woT, woT.ap.rearrange("p a b q -> p (a b q)"))
    ht = [ctx.sbuf("B_ht0", [128, NC, 512], BF16)]
    o_f = ctx.sbuf("B_of", [128, NC, 512], F32)
    osq = ctx.sbuf("B_osq", [128, NC, 512], BF16)
    ps_o = [ctx.bank(0), ctx.bank(1)]
    ps_r = ctx.bank(2)
    ps_b = ctx.bank(3)
    rowA = ctx.sbuf("B_rowA", [1, 512], F32)
    rrep = ctx.sbuf("B_rrep", [128, 512], F32)
    sout = ctx.sbuf("B_sout", [128, 512], F32)
    s0_fm_v = s0_fm.ap.rearrange("(c p) t -> p c t", p=128)
    for tc in range(T // 512):
        H = ht[0]
        ctx.dma("sync", H, H.ap, h_dram, h_dram_v[:, :, tc * 512:(tc + 1) * 512])
        for mc in range(NC):
            P = ps_o[mc % 2]
            for kc in range(NC):
                ctx.rop("tensor", lambda e, P=P, H=H, mc=mc, kc=kc: e.matmul(
                    P.ap, woT.ap[:, kc, mc, :], H.ap[:, kc, :],
                    start=(kc == 0), stop=(kc == NC - 1)),
                    reads=[woT, H], writes=[P])
            ctx.rop("vector", lambda e, P=P, mc=mc: e.tensor_copy(o_f.ap[:, mc, :], P.ap),
                    reads=[P], writes=[o_f])
            ctx.vfence(P)
            if "dbg_o" in aps:
                ctx.dma("sync", aps["dbg_o"],
                        aps["dbg_o"].ap.rearrange("(c p) t -> p c t", p=128)[:, mc, tc * 512:(tc + 1) * 512],
                        o_f, o_f.ap[:, mc, :])
            ctx.rop("scalar", lambda e, mc=mc: e.square(osq.ap[:, mc, :], o_f.ap[:, mc, :]),
                    reads=[o_f], writes=[osq])
        for mc in range(NC):
            ctx.rop("tensor", lambda e, mc=mc: e.matmul(
                ps_r.ap[0:1, :], onesb.ap, osq.ap[:, mc, :],
                start=(mc == 0), stop=(mc == NC - 1)),
                reads=[onesb, osq], writes=[ps_r])
        ctx.rop("vector", lambda e: e.tensor_scalar_mul(rowA.ap, ps_r.ap[0:1, :], 1.0 / D),
                reads=[ps_r], writes=[rowA])
        ctx.rop("scalar", lambda e: e.activation(rowA.ap, rowA.ap, AF.Sqrt,
                                                 bias=ctx.constcol(1e-6)[0:1, :]),
                reads=[rowA, ctx.cconst], writes=[rowA])
        ctx.rop("vector", lambda e: e.reciprocal(rowA.ap, rowA.ap),
                reads=[rowA], writes=[rowA])
        ctx.rop("tensor", lambda e: e.matmul(ps_b.ap, onesf.ap, rowA.ap,
                                             start=True, stop=True),
                reads=[onesf, rowA], writes=[ps_b])
        ctx.rop("vector", lambda e: e.tensor_copy(rrep.ap, ps_b.ap),
                reads=[ps_b], writes=[rrep])
        for mc in range(NC):
            ctx.rop("vector", lambda e, mc=mc: e.scalar_tensor_tensor(
                sout.ap, o_f.ap[:, mc, :], gac.ap[:, mc:mc + 1], rrep.ap,
                ALU.mult, ALU.mult), reads=[o_f, gac, rrep], writes=[sout])
            ctx.dma("sync", s0_fm, s0_fm_v[:, mc, tc * 512:(tc + 1) * 512],
                    sout, sout.ap)


def launch_B(dbg=False):
    W = lambda name: (lambda st: np.ascontiguousarray(st["weights"][name]))
    extra = [("dbg_xcv", (D, T), BF16), ("dbg_h", (D, T), BF16), ("dbg_o", (D, T), F32),
             ("dbg_wT", (128, 8, 8, 128), BF16), ("dbg_pc", (128, 32), F32)] if dbg else []
    return Launch(
        "Bdbg" if dbg else "B",
        inputs=[
            ("xn0_fm", (D, T), F32, lambda st: st["xn0_fm"]),
            ("conv_w", (D, 4), F32, lambda st: np.ascontiguousarray(
                st["weights"]["L0_attn_conv_w"].reshape(D, 4))),
            ("conv_b", (D,), F32, W("L0_attn_conv_b")),
            ("W_r", (D, D), F32, W("L0_attn_W_r")),
            ("b_r", (D,), F32, W("L0_attn_b_r")),
            ("W_i", (D, D), F32, W("L0_attn_W_i")),
            ("b_i", (D,), F32, W("L0_attn_b_i")),
            ("log_a", (D,), F32, W("L0_attn_log_a")),
            ("W_out", (D, D), F32, W("L0_attn_W_out")),
            ("gamma", (D,), F32, W("L0_attn_gamma")),
        ],
        outputs=[("s0_fm", (D, T), F32)] + extra,
        emit=lambda ctx, aps: emit_stage_B(ctx, aps),
    )


def emit_ln_fm2(ctx, tag, get_tile, onesb, onesf, murep, rstdrep, row4, scr,
                banks=(6, 7), eps=1e-5):
    prow = ctx.bank(banks[0])
    pbc = ctx.bank(banks[1])
    for kc in range(NC):
        b, ap = get_tile(kc)
        ctx.rop("tensor", lambda e, ap=ap, kc=kc: e.matmul(
            prow.ap[0:1, :], onesb.ap, ap, start=(kc == 0), stop=(kc == NC - 1)),
            reads=[onesb, b], writes=[prow])
    ctx.rop("vector", lambda e: e.tensor_scalar_mul(row4.ap[0:1, 2, :], prow.ap[0:1, :], 1.0 / D),
            reads=[prow], writes=[row4])
    for kc in range(NC):
        b, ap = get_tile(kc)
        ctx.rop("scalar", lambda e, ap=ap: e.square(scr.ap, ap), reads=[b], writes=[scr])
        ctx.rop("tensor", lambda e, kc=kc: e.matmul(
            prow.ap[0:1, :], onesb.ap, scr.ap, start=(kc == 0), stop=(kc == NC - 1)),
            reads=[onesb, scr], writes=[prow])
    ctx.rop("vector", lambda e: e.tensor_scalar_mul(row4.ap[0:1, 1, :], prow.ap[0:1, :], 1.0 / D),
            reads=[prow], writes=[row4])
    ctx.rop("vector", lambda e: e.tensor_mul(row4.ap[0:1, 0, :], row4.ap[0:1, 2, :], row4.ap[0:1, 2, :]),
            reads=[row4], writes=[row4])
    ctx.rop("vector", lambda e: e.tensor_sub(row4.ap[0:1, 0, :], row4.ap[0:1, 1, :], row4.ap[0:1, 0, :]),
            reads=[row4], writes=[row4])
    ctx.rop("scalar", lambda e: e.activation(row4.ap[0:1, 0, :], row4.ap[0:1, 0, :], AF.Sqrt,
                                             bias=ctx.constcol(eps)[0:1, :]),
            reads=[row4, ctx.cconst], writes=[row4])
    ctx.rop("vector", lambda e: e.reciprocal(row4.ap[0:1, 3, :], row4.ap[0:1, 0, :]),
            reads=[row4], writes=[row4])
    ctx.rop("tensor", lambda e: e.matmul(pbc.ap, onesf.ap, row4.ap[0:1, 2, :],
                                         start=True, stop=True),
            reads=[onesf, row4], writes=[pbc])
    ctx.rop("vector", lambda e: e.tensor_copy(murep.ap, pbc.ap), reads=[pbc], writes=[murep])
    ctx.rop("tensor", lambda e: e.matmul(pbc.ap, onesf.ap, row4.ap[0:1, 3, :],
                                         start=True, stop=True),
            reads=[onesf, row4], writes=[pbc])
    ctx.rop("vector", lambda e: e.tensor_copy(rstdrep.ap, pbc.ap), reads=[pbc], writes=[rstdrep])


def emit_stage_C(ctx, aps, qkvT_dram):
    xn_fm = aps["xn0_fm"]
    xn_fm_v = xn_fm.ap.rearrange("(c p) t -> p c t", p=128)
    qkv_v = qkvT_dram.ap.rearrange("(c p) t -> p c t", p=128)
    wq = ctx.sbuf("C_wq", [128, NC, 24, 128], BF16)
    for kc in range(NC):
        ctx.dma("gpsimd", wq, wq.ap[:, kc, :, :].rearrange("p a b -> p (a b)"),
                aps["w_qkv"], aps["w_qkv"].ap[kc * 128:(kc + 1) * 128, :])
    bq = load_param_cols(ctx, "C_bq", aps["b_qkv"], n=3 * D)
    g1 = load_param_cols(ctx, "C_g1", aps["ln1_g"])
    b1 = load_param_cols(ctx, "C_b1", aps["ln1_b"])
    onesb, onesf = make_ones(ctx)

    xnt = [ctx.sbuf(f"C_xnt{i}", [128, NC, 512], BF16) for i in range(2)]
    h1t = [ctx.sbuf(f"C_h1t{i}", [128, NC, 512], BF16) for i in range(2)]
    murep = ctx.sbuf("C_mu", [128, 512], F32)
    rstdrep = ctx.sbuf("C_rs", [128, 512], F32)
    row4 = ctx.sbuf("C_row4", [1, 4, 512], F32)
    scr = ctx.sbuf("C_scr", [128, 512], BF16)
    tf = ctx.sbuf("C_tf", [128, 512], F32)
    qev = [ctx.sbuf(f"C_qev{i}", [128, 512], BF16) for i in range(2)]

    for tc in range(T // 512):
        X = xnt[tc % 2]
        H = h1t[tc % 2]
        ctx.dma("gpsimd", X, X.ap, xn_fm, xn_fm_v[:, :, tc * 512:(tc + 1) * 512])
        emit_ln_fm2(ctx, "C", lambda kc: (X, X.ap[:, kc, :]), onesb, onesf,
                    murep, rstdrep, row4, scr)
        for kc in range(NC):
            ctx.rop("vector", lambda e, X=X, kc=kc: e.tensor_sub(tf.ap, X.ap[:, kc, :], murep.ap),
                    reads=[X, murep], writes=[tf])
            ctx.rop("vector", lambda e: e.tensor_mul(tf.ap, tf.ap, rstdrep.ap),
                    reads=[tf, rstdrep], writes=[tf])
            ctx.rop("vector", lambda e, H=H, kc=kc: e.tensor_scalar(
                H.ap[:, kc, :], tf.ap, g1.ap[:, kc:kc + 1], b1.ap[:, kc:kc + 1],
                ALU.mult, ALU.add), reads=[tf, g1, b1], writes=[H])
        for mc in range(24):
            P = ctx.bank(mc % 2)
            for kc in range(NC):
                ctx.rop("tensor", lambda e, P=P, H=H, mc=mc, kc=kc: e.matmul(
                    P.ap, wq.ap[:, kc, mc, :], H.ap[:, kc, :],
                    start=(kc == 0), stop=(kc == NC - 1)),
                    reads=[wq, H], writes=[P])
            Q = qev[mc % 2]
            ctx.rop("vector", lambda e, P=P, Q=Q, mc=mc: e.tensor_scalar_add(
                Q.ap, P.ap, bq.ap[:, mc:mc + 1]),
                reads=[P, bq], writes=[Q])
            ctx.dma("sync", qkvT_dram, qkv_v[:, mc, tc * 512:(tc + 1) * 512], Q, Q.ap)


def launch_C():
    W = lambda name: (lambda st: np.ascontiguousarray(st["weights"][name]))
    return Launch(
        "C",
        inputs=[
            ("xn0_fm", (D, T), F32, lambda st: st["xn0_fm"]),
            ("ln1_g", (D,), F32, W("L0_ffn_ln1_g")),
            ("ln1_b", (D,), F32, W("L0_ffn_ln1_b")),
            ("w_qkv", (D, 3 * D), F32, W("L0_ffn_w_qkv")),
            ("b_qkv", (3 * D,), F32, W("L0_ffn_b_qkv")),
        ],
        outputs=[("qkvT", (3 * D, T), BF16)],
        emit=lambda ctx, aps: emit_stage_C(ctx, aps, aps["qkvT"]),
    )


def emit_attention(ctx, tag, qkvT, oT_dram, nheads=16, dk=64, scale=0.125):
    qv = qkvT.ap
    onesb, onesf = make_ones(ctx) if not hasattr(ctx, "_ones") else ctx._ones
    ctx._ones = (onesb, onesf)
    qaug = ctx.sbuf(f"{tag}_qaug", [65, L], BF16)
    kaug = ctx.sbuf(f"{tag}_kaug", [65, L], BF16)
    vln = ctx.sbuf(f"{tag}_vln", [64, L], BF16)
    vtok = ctx.sbuf(f"{tag}_vtok", [128, 16, 64], BF16)
    expT = ctx.sbuf(f"{tag}_expT", [128, 16, 512], BF16)
    maxc = ctx.sbuf(f"{tag}_maxc", [128, 16], F32)
    mrow = ctx.sbuf(f"{tag}_mrow", [16, 128], BF16)
    maxb = ctx.sbuf(f"{tag}_maxb", [128, 16], BF16)
    sumr = ctx.sbuf(f"{tag}_sumr", [1, 512], F32)
    mrow_dram = ctx.dram(f"{tag}_mrowd", [16, 128], BF16)
    rrep = ctx.sbuf(f"{tag}_rrep", [64, 512], F32)
    oev = ctx.sbuf(f"{tag}_oev", [64, 512], BF16)
    oT_v = oT_dram.ap.rearrange("(c p) t -> p c t", p=128)

    for s in range(NSEQ):
        for h in range(nheads):
            t0 = s * L
            ctx.dma("sync", qaug, qaug.ap[0:64, :], qkvT, qv[h * dk:(h + 1) * dk, t0:t0 + L])
            ctx.dma("sync", kaug, kaug.ap[0:64, :], qkvT, qv[D + h * dk:D + (h + 1) * dk, t0:t0 + L])
            ctx.dma("sync", vln, vln.ap, qkvT, qv[2 * D + h * dk:2 * D + (h + 1) * dk, t0:t0 + L])
            ctx.rop("vector", lambda e: e.memset(kaug.ap[64:65, :], 1.0), writes=[kaug])
            for kb in range(16):
                P = ctx.bank(4 + kb % 2)
                pv = P.ap.bitcast(BF16)[:, 0:64]
                ctx.rop("tensor", lambda e, kb=kb, pv=pv: e.transpose(
                    pv, vln.ap[:, kb * 128:(kb + 1) * 128], ctx.ident_b.ap[0:64, 0:64]),
                    reads=[vln, ctx.ident_b], writes=[P])
                ctx.rop("vector", lambda e, kb=kb, pv=pv: e.tensor_copy(
                    vtok.ap[:, kb, :], pv), reads=[P], writes=[vtok])
            for qt in range(16):
                pb = [ctx.bank(i) for i in range(4)]
                for nb in range(4):
                    ctx.rop("tensor", lambda e, qt=qt, nb=nb: e.matmul(
                        pb[nb].ap, qaug.ap[0:64, qt * 128:(qt + 1) * 128],
                        kaug.ap[0:64, nb * 512:(nb + 1) * 512],
                        start=True, stop=True),
                        reads=[qaug, kaug], writes=[pb[nb]])
                ctx.rop("vector", lambda e, qt=qt: e.reduce_max(
                    maxc.ap[:, qt:qt + 1], ctx.pspan(0, 4), AX.X),
                    reads=[ctx.bank(0), ctx.bank(1), ctx.bank(2), ctx.bank(3)],
                    writes=[maxc])
            ctx.rop("vector", lambda e: e.tensor_scalar_mul(maxb.ap, maxc.ap, -scale),
                    reads=[maxc], writes=[maxb])
            P = ctx.bank(4)
            pv = P.ap.bitcast(BF16)[0:16, 0:128]
            ctx.rop("tensor", lambda e, pv=pv: e.transpose(pv, maxb.ap, ctx.ident_b.ap),
                    reads=[maxb, ctx.ident_b], writes=[P])
            ctx.rop("vector", lambda e, pv=pv: e.tensor_copy(mrow.ap, pv),
                    reads=[P], writes=[mrow])
            ctx.dma("gpsimd", mrow_dram, mrow_dram.ap, mrow, mrow.ap)
            ctx.dma("gpsimd", qaug, qaug.ap[64:65, :],
                    mrow_dram, mrow_dram.ap.rearrange("j t -> () (j t)"))
            for nb in range(4):
                q0 = nb * 512
                for kb in range(16):
                    P = ctx.bank(kb % 2)
                    ctx.rop("tensor", lambda e, P=P, kb=kb, q0=q0: e.matmul(
                        P.ap, kaug.ap[:, kb * 128:(kb + 1) * 128],
                        qaug.ap[:, t0 * 0 + q0:q0 + 512],
                        start=True, stop=True),
                        reads=[kaug, qaug], writes=[P])
                    ctx.rop("scalar", lambda e, P=P, kb=kb: e.activation(
                        expT.ap[:, kb, :], P.ap, AF.Exp, scale=scale),
                        reads=[P], writes=[expT])
                    ctx.rop("tensor", lambda e, kb=kb: e.matmul(
                        ctx.bank(3).ap[0:1, :], onesb.ap, expT.ap[:, kb, :],
                        start=(kb == 0), stop=(kb == 15)),
                        reads=[onesb, expT], writes=[ctx.bank(3)])
                    ctx.rop("tensor", lambda e, kb=kb: e.matmul(
                        ctx.bank(2).ap[0:64, :], vtok.ap[:, kb, :], expT.ap[:, kb, :],
                        start=(kb == 0), stop=(kb == 15)),
                        reads=[vtok, expT], writes=[ctx.bank(2)])
                ctx.rop("vector", lambda e: e.reciprocal(sumr.ap, ctx.bank(3).ap[0:1, :]),
                        reads=[ctx.bank(3)], writes=[sumr])
                ctx.rop("tensor", lambda e: e.matmul(
                    ctx.bank(5).ap[0:64, :], onesf.ap[:, 0:64], sumr.ap,
                    start=True, stop=True),
                    reads=[onesf, sumr], writes=[ctx.bank(5)])
                ctx.rop("vector", lambda e: e.tensor_copy(rrep.ap, ctx.bank(5).ap[0:64, :]),
                        reads=[ctx.bank(5)], writes=[rrep])
                ctx.rop("vector", lambda e: e.tensor_mul(oev.ap, ctx.bank(2).ap[0:64, :], rrep.ap),
                        reads=[ctx.bank(2), rrep], writes=[oev])
                ctx.dma("sync", oT_dram, oT_dram.ap[h * dk:(h + 1) * dk, t0 + q0:t0 + q0 + 512],
                        oev, oev.ap)


def emit_stage_C2(ctx, aps):
    qkvT = aps["qkvT"]
    x2_fm = aps["x2_fm"]
    oT_dram = ctx.dram("C2_oT", [D, T], BF16)
    emit_attention(ctx, "C2", qkvT, oT_dram)
    wo = ctx.sbuf("C2_wo", [128, NC, NC, 128], BF16)
    for kc in range(NC):
        ctx.dma("gpsimd", wo, wo.ap[:, kc, :, :].rearrange("p a b -> p (a b)"),
                aps["w_o"], aps["w_o"].ap[kc * 128:(kc + 1) * 128, :])
    bo = load_param_cols(ctx, "C2_bo", aps["b_o"])
    oT_v = oT_dram.ap.rearrange("(c p) t -> p c t", p=128)
    xn_v = aps["xn0_fm"].ap.rearrange("(c p) t -> p c t", p=128)
    x2_v = x2_fm.ap.rearrange("(c p) t -> p c t", p=128)
    ot = [ctx.sbuf(f"C2_ot{i}", [128, NC, 512], BF16) for i in range(2)]
    xnt = [ctx.sbuf(f"C2_xnt{i}", [128, 512], F32) for i in range(2)]
    x2t = [ctx.sbuf(f"C2_x2t{i}", [128, 512], F32) for i in range(2)]
    for tc in range(T // 512):
        O = ot[tc % 2]
        ctx.dma("sync", O, O.ap, oT_dram, oT_v[:, :, tc * 512:(tc + 1) * 512])
        for mc in range(NC):
            P = ctx.bank(6 + mc % 2)
            for kc in range(NC):
                ctx.rop("tensor", lambda e, P=P, O=O, mc=mc, kc=kc: e.matmul(
                    P.ap, wo.ap[:, kc, mc, :], O.ap[:, kc, :],
                    start=(kc == 0), stop=(kc == NC - 1)),
                    reads=[wo, O], writes=[P])
            XN = xnt[mc % 2]
            X2 = x2t[mc % 2]
            ctx.dma("gpsimd", XN, XN.ap, aps["xn0_fm"], xn_v[:, mc, tc * 512:(tc + 1) * 512])
            ctx.rop("vector", lambda e, P=P, XN=XN, X2=X2, mc=mc: e.scalar_tensor_tensor(
                X2.ap, P.ap, bo.ap[:, mc:mc + 1], XN.ap, ALU.add, ALU.add),
                reads=[P, bo, XN], writes=[X2])
            ctx.dma("sync", x2_fm, x2_v[:, mc, tc * 512:(tc + 1) * 512], X2, X2.ap)


def launch_C2():
    W = lambda name: (lambda st: np.ascontiguousarray(st["weights"][name]))
    return Launch(
        "C2",
        inputs=[
            ("qkvT", (3 * D, T), BF16, lambda st: st["qkvT"]),
            ("xn0_fm", (D, T), F32, lambda st: st["xn0_fm"]),
            ("w_o", (D, D), F32, W("L0_ffn_w_o")),
            ("b_o", (D,), F32, W("L0_ffn_b_o")),
        ],
        outputs=[("x2_fm", (D, T), F32)],
        emit=lambda ctx, aps: emit_stage_C2(ctx, aps),
    )


def emit_stage_D1(ctx, aps):
    x2 = aps["x2_fm"]
    x2_v = x2.ap.rearrange("(c p) t -> p c t", p=128)
    h2d = aps["h2T"]
    h2_v = h2d.ap.rearrange("(c p) t -> p c t", p=128)
    ewT = aps["ewT"]
    g2c = load_param_cols(ctx, "D_g2", aps["ln2_g"])
    b2c = load_param_cols(ctx, "D_b2", aps["ln2_b"])
    onesb, onesf = make_ones(ctx)
    gw = ctx.sbuf("D_gw", [128, NC, 8], F32)
    for kc in range(NC):
        ctx.dma("sync", gw, gw.ap[:, kc, :], aps["gate_w"],
                aps["gate_w"].ap[kc * 128:(kc + 1) * 128, :])
    gb_st = ctx.sbuf("D_gbst", [1, 8], F32)
    gb = ctx.sbuf("D_gb", [8, 1], F32)
    ctx.dma("sync", gb_st, gb_st.ap, aps["gate_b"], aps["gate_b"].ap.rearrange("e -> () e"))
    Pg = ctx.bank(7)
    ctx.rop("tensor", lambda e: e.transpose(Pg.ap[0:8, 0:1], gb_st.ap, ctx.ident_f.ap[0:1, 0:1]),
            reads=[gb_st, ctx.ident_f], writes=[Pg])
    ctx.rop("vector", lambda e: e.tensor_copy(gb.ap, Pg.ap[0:8, 0:1]), reads=[Pg], writes=[gb])

    xt = [ctx.sbuf(f"D_xt{i}", [128, NC, 512], BF16) for i in range(2)]
    ht = [ctx.sbuf(f"D_ht{i}", [128, NC, 512], BF16) for i in range(2)]
    hf = ctx.sbuf("D_hf", [128, NC, 512], F32)
    murep = ctx.sbuf("D_mu", [128, 512], F32)
    rstdrep = ctx.sbuf("D_rs", [128, 512], F32)
    row4 = ctx.sbuf("D_row4", [1, 4, 512], F32)
    scr = ctx.sbuf("D_scr", [128, 512], BF16)
    tf = ctx.sbuf("D_tf", [128, 512], F32)
    lg = ctx.sbuf("D_lg", [8, 512], F32)
    ltok = ctx.sbuf("D_ltok", [128, 8], F32)
    cols = ctx.sbuf("D_cols", [128, 8], F32)
    ewtok = ctx.sbuf("D_ewtok", [128, 8], BF16)
    ewev = ctx.sbuf("D_ewev", [8, 128], BF16)
    lmask = ctx.sbuf("D_lmask", [128, 8], F32)
    ewf = ctx.sbuf("D_ewf", [128, 8], F32)

    for tc in range(T // 512):
        X = xt[tc % 2]
        H = ht[tc % 2]
        ctx.dma("gpsimd", X, X.ap, x2, x2_v[:, :, tc * 512:(tc + 1) * 512])
        emit_ln_fm2(ctx, "D", lambda kc: (X, X.ap[:, kc, :]), onesb, onesf,
                    murep, rstdrep, row4, scr)
        for kc in range(NC):
            ctx.rop("vector", lambda e, X=X, kc=kc: e.tensor_sub(tf.ap, X.ap[:, kc, :], murep.ap),
                    reads=[X, murep], writes=[tf])
            ctx.rop("vector", lambda e: e.tensor_mul(tf.ap, tf.ap, rstdrep.ap),
                    reads=[tf, rstdrep], writes=[tf])
            ctx.rop("vector", lambda e, kc=kc: e.tensor_scalar(
                hf.ap[:, kc, :], tf.ap, g2c.ap[:, kc:kc + 1], b2c.ap[:, kc:kc + 1],
                ALU.mult, ALU.add), reads=[tf, g2c, b2c], writes=[hf])
            ctx.rop("vector", lambda e, H=H, kc=kc: e.tensor_copy(
                H.ap[:, kc, :], hf.ap[:, kc, :]), reads=[hf], writes=[H])
        ctx.dma("sync", h2d, h2_v[:, :, tc * 512:(tc + 1) * 512], H, H.ap)
        Pl = ctx.bank(6)
        for kc in range(NC):
            ctx.rop("tensor", lambda e, kc=kc: e.matmul(
                Pl.ap[0:8, :], gw.ap[:, kc, :], hf.ap[:, kc, :],
                start=(kc == 0), stop=(kc == NC - 1)),
                reads=[gw, hf], writes=[Pl])
        ctx.rop("vector", lambda e: e.tensor_scalar_add(lg.ap, Pl.ap[0:8, :], gb.ap),
                reads=[Pl, gb], writes=[lg])
        for j in range(4):
            Pt = ctx.bank(7)
            ctx.rop("tensor", lambda e, j=j: e.transpose(
                Pt.ap[:, 0:8], lg.ap[:, j * 128:(j + 1) * 128], ctx.ident_f.ap[0:8, 0:8]),
                reads=[lg, ctx.ident_f], writes=[Pt])
            ctx.rop("vector", lambda e: e.tensor_copy(ltok.ap, Pt.ap[:, 0:8]),
                    reads=[Pt], writes=[ltok])
            ctx.rop("vector", lambda e: e.reduce_max(cols.ap[:, 0:1], ltok.ap, AX.X),
                    reads=[ltok], writes=[cols])
            ctx.rop("vector", lambda e: e.tensor_scalar(
                lmask.ap, ltok.ap, cols.ap[:, 0:1], None, ALU.is_equal),
                reads=[ltok, cols], writes=[lmask])
            ctx.rop("vector", lambda e: e.scalar_tensor_tensor(
                ewf.ap, lmask.ap, -1e38, ltok.ap, ALU.mult, ALU.add),
                reads=[lmask, ltok], writes=[ewf])
            ctx.rop("vector", lambda e: e.reduce_max(cols.ap[:, 1:2], ewf.ap, AX.X),
                    reads=[ewf], writes=[cols])
            ctx.rop("vector", lambda e: e.tensor_sub(cols.ap[:, 2:3], cols.ap[:, 1:2], cols.ap[:, 0:1]),
                    reads=[cols], writes=[cols])
            ctx.rop("scalar", lambda e: e.activation(cols.ap[:, 3:4], cols.ap[:, 2:3], AF.Exp),
                    reads=[cols], writes=[cols])
            ctx.rop("vector", lambda e: e.tensor_scalar_add(cols.ap[:, 4:5], cols.ap[:, 3:4], 1.0),
                    reads=[cols], writes=[cols])
            ctx.rop("vector", lambda e: e.reciprocal(cols.ap[:, 5:6], cols.ap[:, 4:5]),
                    reads=[cols], writes=[cols])
            ctx.rop("vector", lambda e: e.tensor_mul(cols.ap[:, 6:7], cols.ap[:, 3:4], cols.ap[:, 5:6]),
                    reads=[cols], writes=[cols])
            ctx.rop("vector", lambda e: e.tensor_scalar(
                cols.ap[:, 7:8], ltok.ap[:, 0:1], 0.0, 0.0, ALU.mult, ALU.mult),
                reads=[ltok], writes=[cols])
            ctx.rop("vector", lambda e: e.tensor_scalar(
                ewf.ap, ewf.ap, cols.ap[:, 1:2], None, ALU.is_equal),
                reads=[ewf, cols], writes=[ewf])
            ctx.rop("vector", lambda e: e.tensor_scalar(
                ewf.ap, ewf.ap, cols.ap[:, 6:7], None, ALU.mult),
                reads=[ewf, cols], writes=[ewf])
            ctx.rop("vector", lambda e: e.scalar_tensor_tensor(
                ewf.ap, lmask.ap, cols.ap[:, 5:6], ewf.ap, ALU.mult, ALU.add),
                reads=[lmask, cols, ewf], writes=[ewf])
            ctx.rop("vector", lambda e: e.tensor_copy(ewtok.ap, ewf.ap),
                    reads=[ewf], writes=[ewtok])
            Pe = ctx.bank(5)
            pv = Pe.ap.bitcast(BF16)[0:8, 0:128]
            ctx.rop("tensor", lambda e, pv=pv: e.transpose(pv, ewtok.ap, ctx.ident_b.ap),
                    reads=[ewtok, ctx.ident_b], writes=[Pe])
            ctx.rop("vector", lambda e, pv=pv: e.tensor_copy(ewev.ap, pv),
                    reads=[Pe], writes=[ewev])
            ctx.dma("sync", ewT, ewT.ap[:, tc * 512 + j * 128: tc * 512 + (j + 1) * 128],
                    ewev, ewev.ap)


def launch_D1():
    W = lambda name: (lambda st: np.ascontiguousarray(st["weights"][name]))
    return Launch(
        "D1",
        inputs=[
            ("x2_fm", (D, T), F32, lambda st: st["x2_fm"]),
            ("ln2_g", (D,), F32, W("L0_ffn_ln2_g")),
            ("ln2_b", (D,), F32, W("L0_ffn_ln2_b")),
            ("gate_w", (D, 8), F32, W("L0_ffn_gate_w")),
            ("gate_b", (8,), F32, W("L0_ffn_gate_b")),
        ],
        outputs=[("h2T", (D, T), BF16), ("ewT", (8, T), BF16)],
        emit=lambda ctx, aps: emit_stage_D1(ctx, aps),
    )


def emit_stage_D2(ctx, aps):
    E = 8
    DFF = 4096
    h2_v = aps["h2T"].ap.rearrange("(c p) t -> p c t", p=128)
    ffn_v = aps["ffn0_fm"].ap.rearrange("(c p) t -> p c t", p=128)
    x2_v = aps["x2_fm"].ap.rearrange("(c p) t -> p c t", p=128)
    onesb, onesf = make_ones(ctx)
    onesb_row = ctx.sbuf("E_onesrow", [1, 128], BF16)
    ctx.rop("vector", lambda e: e.memset(onesb_row.ap, 1.0), writes=[onesb_row])

    b1c = ctx.sbuf("E_b1c", [128, E, 32], F32)
    b1st = ctx.sbuf("E_b1st", [32, 128], F32)
    for e_ in range(E):
        ctx.dma("sync", b1st, b1st.ap, aps["b1"],
                aps["b1"].ap[e_, :].rearrange("(a b) -> a b", b=128))
        Pb = ctx.bank(5)
        ctx.rop("tensor", lambda en, e_=e_: en.transpose(
            Pb.ap[:, 0:32], b1st.ap, ctx.ident_f.ap[0:32, 0:32]),
            reads=[b1st, ctx.ident_f], writes=[Pb])
        ctx.rop("vector", lambda en, e_=e_: en.tensor_copy(b1c.ap[:, e_, :], Pb.ap[:, 0:32]),
                reads=[Pb], writes=[b1c])
    b2w = ctx.sbuf("E_b2w", [8, D], BF16)
    ctx.dma("gpsimd", b2w, b2w.ap, aps["b2"], aps["b2"].ap)
    ewTs = ctx.sbuf("E_ewT", [8, 2048], BF16)

    w1b = [ctx.sbuf(f"E_w1b{i}", [128, NC, 16, 128], BF16) for i in range(2)]
    w2b = [ctx.sbuf(f"E_w2b{i}", [128, 16, NC, 128], BF16) for i in range(2)]
    h2t = ctx.sbuf("E_h2t", [128, NC, 512], BF16)
    hid = ctx.sbuf("E_hid", [128, 16, 512], BF16)
    acc = ctx.sbuf("E_acc", [128, NC, 2048], BF16)
    ewrep = ctx.sbuf("E_ewrep", [128, 2048], BF16)
    ewrow = ctx.sbuf("E_ewrow", [1, 2048], BF16)
    tmp = ctx.sbuf("E_tmp", [128, 512], BF16)
    x2t = ctx.sbuf("E_x2t", [128, 512], F32)
    ff = ctx.sbuf("E_ff", [128, 512], F32)

    HALF = 2048
    small = bool(int(os.environ.get("MK_D2_SMALL", "0")))
    for half in range(1 if small else 2):
        t0 = half * HALF
        ctx.dma("sync", ewTs, ewTs.ap, aps["ewT"], aps["ewT"].ap[:, t0:t0 + HALF])
        for dc in range(NC):
            for q in range(4):
                Pq = ctx.bank(4 + (q % 2))
                ctx.rop("tensor", lambda e, dc=dc, q=q, Pq=Pq: e.matmul(
                    Pq.ap, b2w.ap[:, dc * 128:(dc + 1) * 128],
                    ewTs.ap[:, q * 512:(q + 1) * 512],
                    start=True, stop=True),
                    reads=[b2w, ewTs], writes=[Pq])
                ctx.rop("vector", lambda e, dc=dc, q=q, Pq=Pq: e.tensor_copy(
                    acc.ap[:, dc, q * 512:(q + 1) * 512], Pq.ap),
                    reads=[Pq], writes=[acc])
        for e_ in range(2 if small else E):
            ctx.dma("sync", ewrow, ewrow.ap, aps["ewT"],
                    aps["ewT"].ap[e_:e_ + 1, t0:t0 + HALF])
            for q in range(4):
                Pq = ctx.bank(4 + (q % 2))
                ctx.rop("tensor", lambda en, e_=e_, q=q, Pq=Pq: en.matmul(
                    Pq.ap, onesb_row.ap, ewrow.ap[:, q * 512:(q + 1) * 512],
                    start=True, stop=True),
                    reads=[onesb_row, ewrow], writes=[Pq])
                ctx.rop("vector", lambda en, q=q, Pq=Pq: en.tensor_copy(
                    ewrep.ap[:, q * 512:(q + 1) * 512], Pq.ap),
                    reads=[Pq], writes=[ewrep])
            for fh in range(2):
                W1 = w1b[fh % 2]
                W2 = w2b[fh % 2]
                f0 = fh * 2048
                for kc in range(NC):
                    ctx.dma("gpsimd", W1, W1.ap[:, kc, :, :].rearrange("p a b -> p (a b)"),
                            aps["W1"], aps["W1"].ap[e_, kc * 128:(kc + 1) * 128, f0:f0 + 2048])
                for fc in range(16):
                    ctx.dma("gpsimd", W2, W2.ap[:, fc, :, :].rearrange("p a b -> p (a b)"),
                            aps["W2"], aps["W2"].ap[e_, f0 + fc * 128: f0 + (fc + 1) * 128, :])
                for tcq in range(1 if small else 4):
                    q0 = t0 + tcq * 512
                    ctx.dma("gpsimd", h2t, h2t.ap, aps["h2T"], h2_v[:, :, q0:q0 + 512])
                    for fc in range(16):
                        P = ctx.bank(fc % 2)
                        for kc in range(NC):
                            ctx.rop("tensor", lambda e, P=P, fc=fc, kc=kc, W1=W1: e.matmul(
                                P.ap, W1.ap[:, kc, fc, :], h2t.ap[:, kc, :],
                                start=(kc == 0), stop=(kc == NC - 1)),
                                reads=[W1, h2t], writes=[P])
                        gfc = fh * 16 + fc
                        ctx.rop("scalar", lambda e, P=P, fc=fc, e_=e_, gfc=gfc: e.activation(
                            hid.ap[:, fc, :], P.ap, AF.Relu, bias=b1c.ap[:, e_, gfc:gfc + 1]),
                            reads=[P, b1c], writes=[hid])
                    for dc in range(NC):
                        P = ctx.bank(2 + dc % 2)
                        for fc in range(16):
                            ctx.rop("tensor", lambda e, P=P, dc=dc, fc=fc, W2=W2: e.matmul(
                                P.ap, W2.ap[:, fc, dc, :], hid.ap[:, fc, :],
                                start=(fc == 0), stop=(fc == 15)),
                                reads=[W2, hid], writes=[P])
                        ctx.rop("vector", lambda e, P=P, tcq=tcq: e.tensor_mul(
                            tmp.ap, P.ap, ewrep.ap[:, tcq * 512:(tcq + 1) * 512]),
                            reads=[P, ewrep], writes=[tmp])
                        ctx.rop("gpsimd", lambda e, dc=dc, tcq=tcq: e.tensor_add(
                            acc.ap[:, dc, tcq * 512:(tcq + 1) * 512],
                            acc.ap[:, dc, tcq * 512:(tcq + 1) * 512], tmp.ap),
                            reads=[acc, tmp], writes=[acc])
        for dc in range(NC):
            for q in range(4):
                ctx.dma("sync", x2t, x2t.ap, aps["x2_fm"], x2_v[:, dc, t0 + q * 512:t0 + (q + 1) * 512])
                ctx.rop("vector", lambda e, dc=dc, q=q: e.tensor_add(
                    ff.ap, acc.ap[:, dc, q * 512:(q + 1) * 512], x2t.ap),
                    reads=[acc, x2t], writes=[ff])
                ctx.dma("sync", aps["ffn0_fm"], ffn_v[:, dc, t0 + q * 512:t0 + (q + 1) * 512],
                        ff, ff.ap)


def launch_D2():
    W = lambda name: (lambda st: np.ascontiguousarray(st["weights"][name]))
    return Launch(
        "D2",
        inputs=[
            ("h2T", (D, T), BF16, lambda st: st["h2T"]),
            ("ewT", (8, T), BF16, lambda st: st["ewT"]),
            ("x2_fm", (D, T), F32, lambda st: st["x2_fm"]),
            ("W1", (8, D, 4096), F32, W("L0_ffn_W1")),
            ("b1", (8, 4096), F32, W("L0_ffn_b1")),
            ("W2", (8, 4096, D), F32, W("L0_ffn_W2")),
            ("b2", (8, D), F32, W("L0_ffn_b2")),
        ],
        outputs=[("ffn0_fm", (D, T), F32)],
        emit=lambda ctx, aps: emit_stage_D2(ctx, aps),
    )


def emit_stage_E(ctx, aps):
    x_v = aps["x_fm"].ap.rearrange("(c p) t -> p c t", p=128)
    s_v = aps["s0_fm"].ap.rearrange("(c p) t -> p c t", p=128)
    f_v = aps["ffn0_fm"].ap.rearrange("(c p) t -> p c t", p=128)
    x1_v = aps["x1_fm"].ap.rearrange("(c p) t -> p c t", p=128)
    xn1_v = aps["xn1_fm"].ap.rearrange("(c p) t -> p c t", p=128)
    onesb, onesf = make_ones(ctx)
    xt = ctx.sbuf("Ex_xt", [128, 512], F32)
    st_ = ctx.sbuf("Ex_st", [128, 512], F32)
    ft = ctx.sbuf("Ex_ft", [128, 512], F32)
    x1f = ctx.sbuf("Ex_x1f", [128, NC, 512], F32)
    x1b = ctx.sbuf("Ex_x1b", [128, NC, 512], BF16)
    murep = ctx.sbuf("Ex_mu", [128, 512], F32)
    rstdrep = ctx.sbuf("Ex_rs", [128, 512], F32)
    row4 = ctx.sbuf("Ex_row4", [1, 4, 512], F32)
    scr = ctx.sbuf("Ex_scr", [128, 512], BF16)
    xnt = ctx.sbuf("Ex_xnt", [128, 512], F32)
    for tc in range(T // 512):
        for kc in range(NC):
            ctx.dma("sync", xt, xt.ap, aps["x_fm"], x_v[:, kc, tc * 512:(tc + 1) * 512])
            ctx.dma("sync", st_, st_.ap, aps["s0_fm"], s_v[:, kc, tc * 512:(tc + 1) * 512])
            ctx.dma("sync", ft, ft.ap, aps["ffn0_fm"], f_v[:, kc, tc * 512:(tc + 1) * 512])
            ctx.rop("scalar", lambda e: e.activation(ft.ap, ft.ap, AF.Silu),
                    reads=[ft], writes=[ft])
            ctx.rop("vector", lambda e: e.tensor_mul(ft.ap, ft.ap, st_.ap),
                    reads=[ft, st_], writes=[ft])
            ctx.rop("vector", lambda e, kc=kc: e.tensor_add(x1f.ap[:, kc, :], xt.ap, ft.ap),
                    reads=[xt, ft], writes=[x1f])
            ctx.rop("vector", lambda e, kc=kc: e.tensor_copy(x1b.ap[:, kc, :], x1f.ap[:, kc, :]),
                    reads=[x1f], writes=[x1b])
            ctx.dma("sync", aps["x1_fm"], x1_v[:, kc, tc * 512:(tc + 1) * 512],
                    x1f, x1f.ap[:, kc, :])
        emit_ln_fm2(ctx, "E", lambda kc: (x1b, x1b.ap[:, kc, :]), onesb, onesf,
                    murep, rstdrep, row4, scr)
        for kc in range(NC):
            ctx.rop("vector", lambda e, kc=kc: e.tensor_sub(xnt.ap, x1f.ap[:, kc, :], murep.ap),
                    reads=[x1f, murep], writes=[xnt])
            ctx.rop("vector", lambda e: e.tensor_mul(xnt.ap, xnt.ap, rstdrep.ap),
                    reads=[xnt, rstdrep], writes=[xnt])
            ctx.dma("sync", aps["xn1_fm"], xn1_v[:, kc, tc * 512:(tc + 1) * 512],
                    xnt, xnt.ap)


def launch_E():
    return Launch(
        "E",
        inputs=[
            ("x_fm", (D, T), F32, lambda st: st["x_fm"]),
            ("s0_fm", (D, T), F32, lambda st: st["s0_fm"]),
            ("ffn0_fm", (D, T), F32, lambda st: st["ffn0_fm"]),
        ],
        outputs=[("x1_fm", (D, T), F32), ("xn1_fm", (D, T), F32)],
        emit=lambda ctx, aps: emit_stage_E(ctx, aps),
    )


def emit_stage_F(ctx, aps):
    qkvT = aps["qkvT1"]
    qkv_v = qkvT.ap.rearrange("(c p) t -> p c t", p=128)
    xn_v = aps["xn1_fm"].ap.rearrange("(c p) t -> p c t", p=128)
    wqT = load_wT_bf(ctx, "F_wq", aps["qkv_w"], 3 * D, D)
    onesb, onesf = make_ones(ctx)
    qw = ctx.sbuf("F_qw", [128, 4], F32)
    for i, nm in enumerate(("q_norm_w", "q_norm_b", "k_norm_w", "k_norm_b")):
        for hh in range(2):
            ctx.dma("sync", qw, qw.ap[hh * 64:(hh + 1) * 64, i:i + 1],
                    aps[nm], aps[nm].ap.rearrange("d -> d ()"),
                    allow_slow_non_contiguous=True)
    maskb = ctx.sbuf("F_maskb", [128, 2], BF16)
    maskf = ctx.sbuf("F_maskf", [128, 2], F32)
    mask2f = ctx.sbuf("F_m2f", [2, 128], F32)
    ctx.rop("vector", lambda e: e.memset(maskf.ap, 0.0), writes=[maskf])
    ctx.rop("vector", lambda e: e.memset(maskf.ap[0:64, 0:1], 1.0), writes=[maskf])
    ctx.rop("vector", lambda e: e.memset(maskf.ap[64:128, 1:2], 1.0), writes=[maskf])
    ctx.rop("vector", lambda e: e.tensor_copy(maskb.ap, maskf.ap), reads=[maskf], writes=[maskb])
    _Pm = ctx.bank(7)
    ctx.rop("tensor", lambda e: e.transpose(_Pm.ap[0:2, 0:128], maskf.ap, ctx.ident_f.ap),
            reads=[maskf, ctx.ident_f], writes=[_Pm])
    ctx.rop("vector", lambda e: e.tensor_copy(mask2f.ap, _Pm.ap[0:2, 0:128]),
            reads=[_Pm], writes=[mask2f])

    xnt = [ctx.sbuf(f"F_xnt{i}", [128, NC, 512], BF16) for i in range(2)]
    qf = ctx.sbuf("F_qf", [128, 512], F32)
    qb_ = ctx.sbuf("F_qb", [128, 512], BF16)
    sqb = ctx.sbuf("F_sqb", [128, 512], BF16)
    rows = ctx.sbuf("F_rows", [2, 6, 512], F32)
    murep = ctx.sbuf("F_mur", [128, 512], F32)
    rsrep = ctx.sbuf("F_rsr", [128, 512], F32)
    kn_all = ctx.sbuf("F_knall", [128, NC, L], BF16)
    kmean = ctx.sbuf("F_kmean", [128, 2], F32)
    outb = ctx.sbuf("F_outb", [128, 512], BF16)

    def ln_head(P, dst_rows, tc, is_k, kc):
        ctx.rop("vector", lambda e: e.tensor_copy(qf.ap, P.ap), reads=[P], writes=[qf])
        ctx.rop("vector", lambda e: e.tensor_copy(qb_.ap, qf.ap), reads=[qf], writes=[qb_])
        ctx.rop("scalar", lambda e: e.square(sqb.ap, qb_.ap), reads=[qb_], writes=[sqb])
        Pr = ctx.bank(6)
        ctx.rop("tensor", lambda e: e.matmul(Pr.ap[0:2, :], maskb.ap, qb_.ap,
                                             start=True, stop=True),
                reads=[maskb, qb_], writes=[Pr])
        ctx.rop("vector", lambda e: e.tensor_scalar_mul(rows.ap[0:2, 2, :], Pr.ap[0:2, :], 1.0 / 64),
                reads=[Pr], writes=[rows])
        ctx.rop("tensor", lambda e: e.matmul(Pr.ap[0:2, :], maskb.ap, sqb.ap,
                                             start=True, stop=True),
                reads=[maskb, sqb], writes=[Pr])
        ctx.rop("vector", lambda e: e.tensor_scalar_mul(rows.ap[0:2, 1, :], Pr.ap[0:2, :], 1.0 / 64),
                reads=[Pr], writes=[rows])
        ctx.rop("vector", lambda e: e.tensor_mul(rows.ap[0:2, 0, :], rows.ap[0:2, 2, :], rows.ap[0:2, 2, :]),
                reads=[rows], writes=[rows])
        ctx.rop("vector", lambda e: e.tensor_sub(rows.ap[0:2, 0, :], rows.ap[0:2, 1, :], rows.ap[0:2, 0, :]),
                reads=[rows], writes=[rows])
        ctx.rop("scalar", lambda e: e.activation(rows.ap[0:2, 0, :], rows.ap[0:2, 0, :], AF.Sqrt,
                                                 bias=ctx.constcol(1e-5)[0:2, :]),
                reads=[rows, ctx.cconst], writes=[rows])
        ctx.rop("vector", lambda e: e.reciprocal(rows.ap[0:2, 3, :], rows.ap[0:2, 0, :]),
                reads=[rows], writes=[rows])
        Pb = ctx.bank(7)
        ctx.rop("tensor", lambda e: e.matmul(Pb.ap, mask2f.ap, rows.ap[0:2, 2, :],
                                             start=True, stop=True),
                reads=[mask2f, rows], writes=[Pb])
        ctx.rop("vector", lambda e: e.tensor_copy(murep.ap, Pb.ap), reads=[Pb], writes=[murep])
        ctx.rop("tensor", lambda e: e.matmul(Pb.ap, mask2f.ap, rows.ap[0:2, 3, :],
                                             start=True, stop=True),
                reads=[mask2f, rows], writes=[Pb])
        ctx.rop("vector", lambda e: e.tensor_copy(rsrep.ap, Pb.ap), reads=[Pb], writes=[rsrep])
        ctx.rop("vector", lambda e: e.tensor_sub(qf.ap, qf.ap, murep.ap),
                reads=[qf, murep], writes=[qf])
        ctx.rop("vector", lambda e: e.tensor_mul(qf.ap, qf.ap, rsrep.ap),
                reads=[qf, rsrep], writes=[qf])
        wcol = 2 if is_k else 0
        ctx.rop("vector", lambda e, wcol=wcol: e.tensor_scalar(
            outb.ap, qf.ap, qw.ap[:, wcol:wcol + 1], qw.ap[:, wcol + 1:wcol + 2],
            ALU.mult, ALU.add), reads=[qf, qw], writes=[outb])

    for tc in range(T // 512):
        X = xnt[tc % 2]
        ctx.dma("gpsimd", X, X.ap, aps["xn1_fm"], xn_v[:, :, tc * 512:(tc + 1) * 512])
        for mc in range(24):
            P = ctx.bank(mc % 2)
            for kc in range(NC):
                ctx.rop("tensor", lambda e, P=P, X=X, mc=mc, kc=kc: e.matmul(
                    P.ap, wqT.ap[:, kc, mc, :], X.ap[:, kc, :],
                    start=(kc == 0), stop=(kc == NC - 1)),
                    reads=[wqT, X], writes=[P])
            if mc < 8:
                ln_head(P, None, tc, False, mc)
                ctx.dma("sync", qkvT, qkv_v[:, mc, tc * 512:(tc + 1) * 512], outb, outb.ap)
            elif mc < 16:
                ln_head(P, None, tc, True, mc)
                s = tc // 4
                j = tc % 4
                kc_ = mc - 8
                ctx.rop("vector", lambda e, j=j, kc_=kc_: e.tensor_copy(
                    kn_all.ap[:, kc_, j * 512:(j + 1) * 512], outb.ap),
                    reads=[outb], writes=[kn_all])
                if j == 3:
                    ctx.rop("vector", lambda e, kc_=kc_: e.reduce_sum(
                        kmean.ap[:, 0:1], kn_all.ap[:, kc_, :], AX.X),
                        reads=[kn_all], writes=[kmean])
                    ctx.rop("vector", lambda e: e.tensor_scalar_mul(
                        kmean.ap[:, 1:2], kmean.ap[:, 0:1], 1.0 / L),
                        reads=[kmean], writes=[kmean])
                    for jj in range(4):
                        ctx.rop("vector", lambda e, jj=jj, kc_=kc_: e.tensor_scalar(
                            outb.ap, kn_all.ap[:, kc_, jj * 512:(jj + 1) * 512],
                            kmean.ap[:, 1:2], None, ALU.subtract),
                            reads=[kn_all, kmean], writes=[outb])
                        ctx.dma("sync", qkvT,
                                qkv_v[:, mc, s * L + jj * 512: s * L + (jj + 1) * 512],
                                outb, outb.ap)
            else:
                ctx.rop("vector", lambda e, P=P: e.tensor_copy(outb.ap, P.ap),
                        reads=[P], writes=[outb])
                ctx.dma("sync", qkvT, qkv_v[:, mc, tc * 512:(tc + 1) * 512], outb, outb.ap)


def launch_F():
    W = lambda name: (lambda st: np.ascontiguousarray(st["weights"][name]))
    return Launch(
        "F",
        inputs=[
            ("xn1_fm", (D, T), F32, lambda st: st["xn1_fm"]),
            ("qkv_w", (3 * D, D), F32, W("L1_attn_qkv_w")),
            ("q_norm_w", (64,), F32, W("L1_attn_q_norm_w")),
            ("q_norm_b", (64,), F32, W("L1_attn_q_norm_b")),
            ("k_norm_w", (64,), F32, W("L1_attn_k_norm_w")),
            ("k_norm_b", (64,), F32, W("L1_attn_k_norm_b")),
        ],
        outputs=[("qkvT1", (3 * D, T), BF16)],
        emit=lambda ctx, aps: emit_stage_F(ctx, aps),
    )


def emit_stage_F2(ctx, aps):
    oT_dram = ctx.dram("F2_oT", [D, T], BF16)
    emit_attention(ctx, "F2", aps["qkvT1"], oT_dram)
    wpT = load_wT_bf(ctx, "F2_wp", aps["proj_w"], D, D)
    pb = load_param_cols(ctx, "F2_pb", aps["proj_b"])
    oT_v = oT_dram.ap.rearrange("(c p) t -> p c t", p=128)
    s1_v = aps["s1_fm"].ap.rearrange("(c p) t -> p c t", p=128)
    ot = [ctx.sbuf(f"F2_ot{i}", [128, NC, 512], BF16) for i in range(2)]
    s1t = ctx.sbuf("F2_s1t", [128, 512], F32)
    for tc in range(T // 512):
        O = ot[tc % 2]
        ctx.dma("sync", O, O.ap, oT_dram, oT_v[:, :, tc * 512:(tc + 1) * 512])
        for mc in range(NC):
            P = ctx.bank(mc % 2)
            for kc in range(NC):
                ctx.rop("tensor", lambda e, P=P, O=O, mc=mc, kc=kc: e.matmul(
                    P.ap, wpT.ap[:, kc, mc, :], O.ap[:, kc, :],
                    start=(kc == 0), stop=(kc == NC - 1)),
                    reads=[wpT, O], writes=[P])
            ctx.rop("vector", lambda e, P=P, mc=mc: e.tensor_scalar_add(
                s1t.ap, P.ap, pb.ap[:, mc:mc + 1]), reads=[P, pb], writes=[s1t])
            ctx.dma("sync", aps["s1_fm"], s1_v[:, mc, tc * 512:(tc + 1) * 512], s1t, s1t.ap)


def launch_F2():
    W = lambda name: (lambda st: np.ascontiguousarray(st["weights"][name]))
    return Launch(
        "F2",
        inputs=[
            ("qkvT1", (3 * D, T), BF16, lambda st: st["qkvT1"]),
            ("proj_w", (D, D), F32, W("L1_attn_proj_w")),
            ("proj_b", (D,), F32, W("L1_attn_proj_b")),
        ],
        outputs=[("s1_fm", (D, T), F32)],
        emit=lambda ctx, aps: emit_stage_F2(ctx, aps),
    )


def emit_fp4_quant(ctx, xb, x_ap, ob, o_ap, F, bufs):
    nb = F // 16
    bm, sc, iv, vv, cc, gg, sg = (bufs[k] for k in ("bm", "sc", "iv", "vv", "cc", "gg", "sg"))
    x3 = x_ap.rearrange("p (a b) -> p a b", b=16)
    ctx.rop("vector", lambda e: e.reduce_max(bm.ap[:, 0:nb], x3, AX.X,
                                             apply_absolute_value=True),
            reads=[xb], writes=[bm])
    ctx.rop("vector", lambda e: e.tensor_scalar(sc.ap[:, 0:nb], bm.ap[:, 0:nb],
                                                1.0 / 6.0, 1e-12, ALU.mult, ALU.max),
            reads=[bm], writes=[sc])
    ctx.rop("vector", lambda e: e.reciprocal(iv.ap[:, 0:nb], sc.ap[:, 0:nb]),
            reads=[sc], writes=[iv])
    ivrep = iv.ap[:, 0:nb].rearrange("p a -> p a ()").broadcast_to([128, nb, 16])
    screp = sc.ap[:, 0:nb].rearrange("p a -> p a ()").broadcast_to([128, nb, 16])
    vv3 = vv.ap[:, 0:F].rearrange("p (a b) -> p a b", b=16)
    ctx.rop("scalar", lambda e: e.activation(vv.ap[:, 0:F], x_ap, AF.Abs),
            reads=[xb], writes=[vv])
    ctx.rop("vector", lambda e: e.tensor_tensor(vv3, vv3, ivrep, ALU.mult),
            reads=[vv, iv], writes=[vv])
    ctx.rop("vector", lambda e: e.tensor_scalar(cc.ap[:, 0:F], vv.ap[:, 0:F], 0.25, None, ALU.is_gt),
            reads=[vv], writes=[cc])
    for thr in (0.75, 1.25, 1.75):
        ctx.rop("vector", lambda e, thr=thr: e.scalar_tensor_tensor(
            cc.ap[:, 0:F], vv.ap[:, 0:F], thr, cc.ap[:, 0:F], ALU.is_gt, ALU.add),
            reads=[vv, cc], writes=[cc])
    ctx.rop("vector", lambda e: e.tensor_scalar(gg.ap[:, 0:F], vv.ap[:, 0:F], 2.5, None, ALU.is_gt),
            reads=[vv], writes=[gg])
    ctx.rop("vector", lambda e: e.scalar_tensor_tensor(
        gg.ap[:, 0:F], vv.ap[:, 0:F], 3.5, gg.ap[:, 0:F], ALU.is_gt, ALU.add),
        reads=[vv, gg], writes=[gg])
    ctx.rop("vector", lambda e: e.tensor_scalar(sg.ap[:, 0:F], vv.ap[:, 0:F], 5.0, 2.0, ALU.is_gt, ALU.mult),
            reads=[vv], writes=[sg])
    ctx.rop("vector", lambda e: e.tensor_add(gg.ap[:, 0:F], gg.ap[:, 0:F], sg.ap[:, 0:F]),
            reads=[gg, sg], writes=[gg])
    ctx.rop("vector", lambda e: e.scalar_tensor_tensor(
        gg.ap[:, 0:F], cc.ap[:, 0:F], 0.5, gg.ap[:, 0:F], ALU.mult, ALU.add),
        reads=[cc, gg], writes=[gg])
    ctx.rop("scalar", lambda e: e.sign(sg.ap[:, 0:F], x_ap), reads=[xb], writes=[sg])
    ctx.rop("vector", lambda e: e.tensor_mul(gg.ap[:, 0:F], gg.ap[:, 0:F], sg.ap[:, 0:F]),
            reads=[gg, sg], writes=[gg])
    gg3 = gg.ap[:, 0:F].rearrange("p (a b) -> p a b", b=16)
    o3 = o_ap.rearrange("p (a b) -> p a b", b=16)
    ctx.rop("vector", lambda e: e.tensor_tensor(o3, gg3, screp, ALU.mult),
            reads=[gg, sc], writes=[ob])


def _quant_bufs(ctx, tag, F):
    return {
        "bm": ctx.sbuf(f"{tag}_bm", [128, F // 16], F32),
        "sc": ctx.sbuf(f"{tag}_sc", [128, F // 16], F32),
        "iv": ctx.sbuf(f"{tag}_iv", [128, F // 16], F32),
        "vv": ctx.sbuf(f"{tag}_vv", [128, F], F32),
        "cc": ctx.sbuf(f"{tag}_cc", [128, F], F32),
        "gg": ctx.sbuf(f"{tag}_gg", [128, F], F32),
        "sg": ctx.sbuf(f"{tag}_sg", [128, F], F32),
    }


def emit_stage_G0(ctx, aps):
    IC = 2816
    ICT = IC // 128
    bufs = _quant_bufs(ctx, "G0", IC)
    wrow = ctx.sbuf("G0_wrow", [128, IC], F32)
    wq = ctx.sbuf("G0_wq", [128, IC], BF16)
    ev = ctx.sbuf("G0_ev", [128, 128], BF16)
    for (nm, dst, nrt, ncol) in (("w_gate", "wgT", ICT, D), ("w_up", "wuT", ICT, D),
                                 ("w_down", "wdT", NC, IC)):
        for rt in range(nrt):
            ctx.dma("sync", wrow, wrow.ap[:, 0:ncol], aps[nm],
                    aps[nm].ap[rt * 128:(rt + 1) * 128, :])
            emit_fp4_quant(ctx, wrow, wrow.ap[:, 0:ncol], wq, wq.ap[:, 0:ncol], ncol, bufs)
            for kc in range(ncol // 128):
                P = ctx.bank(4 + kc % 2)
                pv = P.ap.bitcast(BF16)[:, 0:128]
                ctx.rop("tensor", lambda e, kc=kc, pv=pv: e.transpose(
                    pv, wq.ap[:, kc * 128:(kc + 1) * 128], ctx.ident_b.ap),
                    reads=[wq, ctx.ident_b], writes=[P])
                ctx.rop("vector", lambda e, pv=pv: e.tensor_copy(ev.ap, pv),
                        reads=[P], writes=[ev])
                ctx.dma("sync", aps[dst],
                        aps[dst].ap[kc * 128:(kc + 1) * 128, rt * 128:(rt + 1) * 128],
                        ev, ev.ap)


def launch_G0():
    W = lambda name: (lambda st: np.ascontiguousarray(st["weights"][name]))
    IC = 2816
    return Launch(
        "G0",
        inputs=[
            ("w_gate", (IC, D), F32, W("L1_ffn_w_gate")),
            ("w_up", (IC, D), F32, W("L1_ffn_w_up")),
            ("w_down", (D, IC), F32, W("L1_ffn_w_down")),
        ],
        outputs=[("wgT", (D, IC), BF16), ("wuT", (D, IC), BF16), ("wdT", (IC, D), BF16)],
        emit=lambda ctx, aps: emit_stage_G0(ctx, aps),
    )


def emit_stage_G1(ctx, aps):
    IC = 2816
    ICT = IC // 128
    xn_v = aps["xn1_fm"].ap.rearrange("(c p) t -> p c t", p=128)
    hT_v = aps["hT"].ap.rearrange("(c p) t -> p c t", p=128)
    bufs = _quant_bufs(ctx, "G1", D)
    xnt = ctx.sbuf("G1_xnt", [128, NC, 512], F32)
    xtok = ctx.sbuf("G1_xtok", [128, D], F32)
    xqt = ctx.sbuf("G1_xqt", [128, D], BF16)
    xq = ctx.sbuf("G1_xq", [128, NC, T], BF16)
    for tc in range(T // 512):
        ctx.dma("sync", xnt, xnt.ap, aps["xn1_fm"], xn_v[:, :, tc * 512:(tc + 1) * 512])
        for sub in range(4):
            for kc in range(NC):
                P = ctx.bank(kc % 2)
                ctx.rop("tensor", lambda e, P=P, kc=kc, sub=sub: e.transpose(
                    P.ap[:, 0:128], xnt.ap[:, kc, sub * 128:(sub + 1) * 128], ctx.ident_f.ap),
                    reads=[xnt, ctx.ident_f], writes=[P])
                ctx.rop("vector", lambda e, P=P, kc=kc: e.tensor_copy(
                    xtok.ap[:, kc * 128:(kc + 1) * 128], P.ap[:, 0:128]),
                    reads=[P], writes=[xtok])
            emit_fp4_quant(ctx, xtok, xtok.ap, xqt, xqt.ap, D, bufs)
            for kc in range(NC):
                P = ctx.bank(2 + kc % 2)
                pv = P.ap.bitcast(BF16)[:, 0:128]
                ctx.rop("tensor", lambda e, kc=kc, pv=pv: e.transpose(
                    pv, xqt.ap[:, kc * 128:(kc + 1) * 128], ctx.ident_b.ap),
                    reads=[xqt, ctx.ident_b], writes=[P])
                ctx.rop("vector", lambda e, pv=pv, kc=kc, tc=tc, sub=sub: e.tensor_copy(
                    xq.ap[:, kc, tc * 512 + sub * 128: tc * 512 + (sub + 1) * 128], pv),
                    reads=[P], writes=[xq])
    wg = ctx.sbuf("G1_wg", [128, NC, 128], BF16)
    wu = ctx.sbuf("G1_wu", [128, NC, 128], BF16)
    gs = ctx.sbuf("G1_gs", [128, 512], F32)
    hb = ctx.sbuf("G1_hb", [128, 512], F32)
    for mc in range(ICT):
        ctx.dma("gpsimd", wg, wg.ap, aps["wgT"],
                aps["wgT"].ap.rearrange("(c p) m -> p c m", p=128)[:, :, mc * 128:(mc + 1) * 128])
        ctx.dma("gpsimd", wu, wu.ap, aps["wuT"],
                aps["wuT"].ap.rearrange("(c p) m -> p c m", p=128)[:, :, mc * 128:(mc + 1) * 128])
        for tc in range(T // 512):
            Pg = ctx.bank(4 + tc % 2)
            Pu = ctx.bank(6 + tc % 2)
            for kc in range(NC):
                ctx.rop("tensor", lambda e, Pg=Pg, mc=mc, kc=kc, tc=tc: e.matmul(
                    Pg.ap, wg.ap[:, kc, :], xq.ap[:, kc, tc * 512:(tc + 1) * 512],
                    start=(kc == 0), stop=(kc == NC - 1)),
                    reads=[wg, xq], writes=[Pg])
            for kc in range(NC):
                ctx.rop("tensor", lambda e, Pu=Pu, mc=mc, kc=kc, tc=tc: e.matmul(
                    Pu.ap, wu.ap[:, kc, :], xq.ap[:, kc, tc * 512:(tc + 1) * 512],
                    start=(kc == 0), stop=(kc == NC - 1)),
                    reads=[wu, xq], writes=[Pu])
            ctx.rop("scalar", lambda e, Pg=Pg: e.activation(gs.ap, Pg.ap, AF.Silu),
                    reads=[Pg], writes=[gs])
            ctx.rop("vector", lambda e, Pu=Pu: e.tensor_mul(hb.ap, gs.ap, Pu.ap),
                    reads=[gs, Pu], writes=[hb])
            ctx.dma("sync", aps["hT"], hT_v[:, mc, tc * 512:(tc + 1) * 512], hb, hb.ap)


def launch_G1():
    IC = 2816
    return Launch(
        "G1",
        inputs=[
            ("xn1_fm", (D, T), F32, lambda st: st["xn1_fm"]),
            ("wgT", (D, IC), BF16, lambda st: st["wgT"]),
            ("wuT", (D, IC), BF16, lambda st: st["wuT"]),
        ],
        outputs=[("hT", (IC, T), F32)],
        emit=lambda ctx, aps: emit_stage_G1(ctx, aps),
    )


def emit_stage_G3(ctx, aps):
    IC = 2816
    ICT = IC // 128
    hT_v = aps["hT"].ap.rearrange("(c p) t -> p c t", p=128)
    g1_v = aps["g1_fm"].ap.rearrange("(c p) t -> p c t", p=128)
    bufs = _quant_bufs(ctx, "G3", IC)
    wd = ctx.sbuf("G3_wd", [128, ICT, NC, 128], BF16)
    for kc in range(ICT):
        ctx.dma("gpsimd", wd, wd.ap[:, kc, :, :].rearrange("p a b -> p (a b)"),
                aps["wdT"], aps["wdT"].ap[kc * 128:(kc + 1) * 128, :])
    ht = ctx.sbuf("G3_ht", [128, ICT, 128], F32)
    htok = ctx.sbuf("G3_htok", [128, IC], F32)
    hqt = ctx.sbuf("G3_hqt", [128, IC], BF16)
    hq = ctx.sbuf("G3_hq", [128, ICT, 512], BF16)
    g1t = ctx.sbuf("G3_g1t", [128, 512], F32)
    for tc in range(T // 512):
        for sub in range(4):
            s0_ = tc * 512 + sub * 128
            ctx.dma("sync", ht, ht.ap, aps["hT"], hT_v[:, :, s0_:s0_ + 128])
            for kc in range(ICT):
                P = ctx.bank(kc % 2)
                ctx.rop("tensor", lambda e, kc=kc, P=P: e.transpose(
                    P.ap[:, 0:128], ht.ap[:, kc, :], ctx.ident_f.ap),
                    reads=[ht, ctx.ident_f], writes=[P])
                ctx.rop("vector", lambda e, P=P, kc=kc: e.tensor_copy(
                    htok.ap[:, kc * 128:(kc + 1) * 128], P.ap[:, 0:128]),
                    reads=[P], writes=[htok])
            emit_fp4_quant(ctx, htok, htok.ap, hqt, hqt.ap, IC, bufs)
            for kc in range(ICT):
                P = ctx.bank(2 + kc % 2)
                pv = P.ap.bitcast(BF16)[:, 0:128]
                ctx.rop("tensor", lambda e, kc=kc, pv=pv: e.transpose(
                    pv, hqt.ap[:, kc * 128:(kc + 1) * 128], ctx.ident_b.ap),
                    reads=[hqt, ctx.ident_b], writes=[P])
                ctx.rop("vector", lambda e, pv=pv, kc=kc, sub=sub: e.tensor_copy(
                    hq.ap[:, kc, sub * 128:(sub + 1) * 128], pv),
                    reads=[P], writes=[hq])
        for mc in range(NC):
            P = ctx.bank(4 + mc % 2)
            for kc in range(ICT):
                ctx.rop("tensor", lambda e, P=P, mc=mc, kc=kc: e.matmul(
                    P.ap, wd.ap[:, kc, mc, :], hq.ap[:, kc, :],
                    start=(kc == 0), stop=(kc == ICT - 1)),
                    reads=[wd, hq], writes=[P])
            ctx.rop("vector", lambda e, P=P: e.tensor_copy(g1t.ap, P.ap),
                    reads=[P], writes=[g1t])
            ctx.dma("sync", aps["g1_fm"], g1_v[:, mc, tc * 512:(tc + 1) * 512], g1t, g1t.ap)


def launch_G3():
    IC = 2816
    return Launch(
        "G3",
        inputs=[
            ("hT", (IC, T), F32, lambda st: st["hT"]),
            ("wdT", (IC, D), BF16, lambda st: st["wdT"]),
        ],
        outputs=[("g1_fm", (D, T), F32)],
        emit=lambda ctx, aps: emit_stage_G3(ctx, aps),
    )


def emit_stage_H(ctx, aps):
    x1_v = aps["x1_fm"].ap.rearrange("(c p) t -> p c t", p=128)
    s1_v = aps["s1_fm"].ap.rearrange("(c p) t -> p c t", p=128)
    g1_v = aps["g1_fm"].ap.rearrange("(c p) t -> p c t", p=128)
    xt = ctx.sbuf("H_xt", [128, 512], F32)
    st_ = ctx.sbuf("H_st", [128, 512], F32)
    gt = ctx.sbuf("H_gt", [128, 512], F32)
    ofm = ctx.sbuf("H_ofm", [128, NC, 512], F32)
    otok = [ctx.sbuf(f"H_otok{i}", [128, D], F32) for i in range(2)]
    for tc in range(T // 512):
        for kc in range(NC):
            ctx.dma("sync", xt, xt.ap, aps["x1_fm"], x1_v[:, kc, tc * 512:(tc + 1) * 512])
            ctx.dma("sync", st_, st_.ap, aps["s1_fm"], s1_v[:, kc, tc * 512:(tc + 1) * 512])
            ctx.dma("sync", gt, gt.ap, aps["g1_fm"], g1_v[:, kc, tc * 512:(tc + 1) * 512])
            ctx.rop("scalar", lambda e: e.activation(gt.ap, gt.ap, AF.Silu),
                    reads=[gt], writes=[gt])
            ctx.rop("vector", lambda e: e.tensor_mul(gt.ap, gt.ap, st_.ap),
                    reads=[gt, st_], writes=[gt])
            ctx.rop("vector", lambda e, kc=kc: e.tensor_add(ofm.ap[:, kc, :], xt.ap, gt.ap),
                    reads=[xt, gt], writes=[ofm])
        for sub in range(4):
            O = otok[sub % 2]
            for kc in range(NC):
                P = ctx.bank(kc % 2)
                ctx.rop("tensor", lambda e, P=P, kc=kc, sub=sub: e.transpose(
                    P.ap[:, 0:128], ofm.ap[:, kc, sub * 128:(sub + 1) * 128], ctx.ident_f.ap),
                    reads=[ofm, ctx.ident_f], writes=[P])
                ctx.rop("vector", lambda e, P=P, kc=kc, O=O: e.tensor_copy(
                    O.ap[:, kc * 128:(kc + 1) * 128], P.ap[:, 0:128]),
                    reads=[P], writes=[O])
            r0 = tc * 512 + sub * 128
            ctx.dma("sync", aps["out"], aps["out"].ap[r0:r0 + 128, :], O, O.ap)


def launch_H():
    return Launch(
        "H",
        inputs=[
            ("x1_fm", (D, T), F32, lambda st: st["x1_fm"]),
            ("s1_fm", (D, T), F32, lambda st: st["s1_fm"]),
            ("g1_fm", (D, T), F32, lambda st: st["g1_fm"]),
        ],
        outputs=[("out", (T, D), F32)],
        emit=lambda ctx, aps: emit_stage_H(ctx, aps),
    )


def kernel(**inputs):
    states = shard_inputs(inputs)
    for mk in (launch_A, launch_B, launch_C, launch_C2, launch_D1, launch_D2,
               launch_E, launch_F, launch_F2, launch_G0, launch_G1, launch_G3,
               launch_H):
        mk().run(states)
    out = np.stack([st["out"].reshape(NSEQ, L, D) for st in states])
    return np.ascontiguousarray(out.reshape(16, L, D))


# version 02
# speedup vs baseline: 1.0060x; 1.0060x over previous
import os
import numpy as np
import ml_dtypes
from contextlib import ExitStack, contextmanager

import concourse.bass as bass
import concourse.mybir as mybir
from concourse.bass_utils import run_bass_kernel_spmd

F32 = mybir.dt.float32
BF16 = mybir.dt.bfloat16
AF = mybir.ActivationFunctionType
ALU = mybir.AluOpType
AX = mybir.AxisListType
BF = ml_dtypes.bfloat16

ENGINES = ("sync", "tensor", "vector", "scalar", "gpsimd")

D = 1024
L = 2048
NSEQ = 2
T = NSEQ * L
NC = D // 128
NCORE = 8

IDENTF = np.eye(128, dtype=np.float32)
IDENTB = np.eye(128, dtype=BF)


class Dep:
    __slots__ = ("sem", "val", "eng")

    def __init__(self, sem, val, eng):
        self.sem = sem
        self.val = val
        self.eng = eng


class Buf:
    __slots__ = ("name", "ap", "last_write", "reads")

    def __init__(self, name, ap):
        self.name = name
        self.ap = ap
        self.last_write = None
        self.reads = []

    def __getitem__(self, idx):
        return self.ap.__getitem__(idx)


class Ctx:

    def __init__(self, name="prog"):
        self.nc = bass.Bass("TRN2", target_bir_lowering=False)
        self.stack = ExitStack()
        self.sems = {}
        self.semval = {}
        self.thunks = {e: [] for e in ENGINES}
        self.dma_sems = {}
        self._buf_names = set()
        self.constvals = []
        self._constmap = {}
        self.cconst = None
        self.ident_f = None
        self.ident_b = None
        self.waited = {e: {} for e in ENGINES}
        self._banks = {}
        self._dmaring = {}
        self._dmaslot = {}
        self.DMA_RING = 12

    def sem(self, name):
        if name not in self.sems:
            self.sems[name] = self.stack.enter_context(self.nc.semaphore(name))
            self.semval[name] = 0
        return self.sems[name]

    def sbuf(self, name, shape, dtype):
        assert name not in self._buf_names, name
        self._buf_names.add(name)
        cur = getattr(self, "_cur", None) or self.stack
        h = cur.enter_context(self.nc.sbuf_tensor(name, shape, dtype))
        return Buf(name, h[:])

    @contextmanager
    def scope(self):
        old = getattr(self, "_cur", None)
        self._cur = ExitStack()
        try:
            yield
        finally:
            self._cur.close()
            self._cur = old

    def psum(self, name, shape, dtype=F32):
        assert name not in self._buf_names, name
        self._buf_names.add(name)
        h = self.stack.enter_context(self.nc.psum_tensor(name, shape, dtype))
        return Buf(name, h[:])

    def vfence(self, buf):
        if not hasattr(self, "_junk"):
            cur = getattr(self, "_cur", None)
            self._cur = None
            self._junk = self.sbuf("junk_fence", [1, 8], F32)
            self._cur = cur
        return self.rop("vector", lambda e: e.memset(self._junk.ap, 0.0),
                        reads=[], writes=[buf, self._junk])

    def _psum_all(self):
        if not hasattr(self, "_psall"):
            self._psall = self.stack.enter_context(
                self.nc.psum_tensor("psum_all", [128, 8, 512], F32))
        return self._psall

    def bank(self, i):
        if i not in self._banks:
            assert 0 <= i < 8
            self._banks[i] = Buf(f"pbank{i}", self._psum_all()[:][:, i, :])
        return self._banks[i]

    def pspan(self, i, k):
        return self._psum_all()[:][:, i:i + k, :].rearrange("p a b -> p (a b)")

    def dram(self, name, shape, dtype, kind="Internal"):
        ap = self.nc.dram_tensor(name, shape, dtype, kind=kind).ap()
        return Buf(name, ap)

    def _wait(self, engine, sem_name, val):
        if val <= 0:
            return
        w = self.waited[engine]
        if w.get(sem_name, 0) >= val:
            return
        w[sem_name] = val
        self.sem(sem_name)
        sems = self.sems
        self.thunks[engine].append(lambda eng: eng.wait_ge(sems[sem_name], val))

    def _emit(self, engine, build, inc):
        name, n = inc
        self.sem(name)
        self.semval[name] += n
        tgt = self.semval[name]
        sems = self.sems

        def thunk(eng, build=build, name=name, n=n):
            build(eng).then_inc(sems[name], n)
        self.thunks[engine].append(thunk)
        return tgt

    def rop(self, engine, build, reads=(), writes=(), dma=False, sem=None):
        if sem is None:
            if dma:
                ring = self._dmaring.setdefault(engine, [0] * self.DMA_RING)
                slot = self._dmaslot.get(engine, 0)
                self._dmaslot[engine] = (slot + 1) % self.DMA_RING
                sem = f"d_{engine}_{slot}"
                if ring[slot] > 0:
                    self._wait(engine, sem, 16 * ring[slot])
                ring[slot] += 1
            else:
                sem = "c_" + engine
        if dma:
            self.dma_sems[sem] = engine
        n = 16 if dma else 1
        deps = []
        for b in reads:
            if b.last_write is not None:
                deps.append(b.last_write)
        for b in writes:
            if b.last_write is not None:
                deps.append(b.last_write)
            deps.extend(b.reads)
        for d in deps:
            if (d.eng == engine == "tensor" and not dma
                    and not d.sem.startswith("d_")):
                continue
            self._wait(engine, d.sem, d.val)
        tgt = self._emit(engine, build, (sem, n))
        dep = Dep(sem, tgt, engine)
        for b in reads:
            b.reads.append(dep)
        for b in writes:
            b.last_write = dep
            b.reads = []
        return dep

    def dma(self, engine, dst, dst_ap, src, src_ap, **kw):
        reads = [src] if src is not None else []
        writes = [dst] if dst is not None else []
        return self.rop(engine,
                        lambda e: e.dma_start(out=dst_ap, in_=src_ap, **kw),
                        reads=reads, writes=writes, dma=True)

    def constcol(self, v):
        v = float(v)
        if v not in self._constmap:
            self._constmap[v] = len(self.constvals)
            self.constvals.append(v)
            assert len(self.constvals) <= 16
        i = self._constmap[v]
        return self.cconst[:, i:i + 1]

    def barrier(self):
        for sem, eng in self.dma_sems.items():
            self._wait(eng, sem, self.semval[sem])
        self.sem("bar")
        for e in ENGINES:
            self.semval["bar"] += 1
            sems = self.sems
            self.thunks[e].append(lambda eng, s=sems: eng.sem_inc(s["bar"], 1))
        tgt = self.semval["bar"]
        for e in ENGINES:
            self._wait(e, "bar", tgt)

    def build(self):
        with self.stack:
            with self.nc.Block() as block:
                for e in ENGINES:
                    ths = self.thunks[e]
                    getattr(block, e)(lambda eng, ths=ths: [t(eng) for t in ths])
        return self.nc


class Launch:
    def __init__(self, name, inputs, outputs, emit):
        self.name = name
        self.inputs = inputs
        self.outputs = outputs
        self.emit = emit
        self._nc = None

    def build(self):
        if self._nc is not None:
            return self._nc
        ctx = Ctx(self.name)
        self._ctx = ctx
        aps = {}
        std = [("cconst", (128, 16), F32, None),
               ("identf", (128, 128), F32, None),
               ("identb", (128, 128), BF16, None)]
        for (n, shape, dt, _src) in std + list(self.inputs):
            aps[n] = ctx.dram(n, list(shape), dt, kind="ExternalInput")
        for (n, shape, dt) in self.outputs:
            aps[n] = ctx.dram(n, list(shape), dt, kind="ExternalOutput")
        ctx.cconst = ctx.sbuf("cconst_sb", [128, 16], F32)
        idf = ctx.sbuf("identf_sb", [128, 128], F32)
        idb = ctx.sbuf("identb_sb", [128, 128], BF16)
        ctx.ident_f = idf
        ctx.ident_b = idb
        ctx.dma("sync", ctx.cconst, ctx.cconst.ap, aps["cconst"], aps["cconst"].ap)
        ctx.dma("sync", idf, idf.ap, aps["identf"], aps["identf"].ap)
        ctx.dma("sync", idb, idb.ap, aps["identb"], aps["identb"].ap)
        self.emit(ctx, aps)
        ctx.barrier()
        self._nc = ctx.build()
        return self._nc

    def run(self, states, **kw):
        nc = self.build()
        cvals = np.zeros((128, 16), np.float32)
        for i, v in enumerate(self._ctx.constvals):
            cvals[:, i] = v
        in_maps = []
        for st in states:
            m = {"cconst": cvals, "identf": IDENTF, "identb": IDENTB}
            for (n, shape, dt, src) in self.inputs:
                a = src(st)
                assert tuple(a.shape) == tuple(shape), (self.name, n, a.shape, shape)
                m[n] = a
            in_maps.append(m)
        res = run_bass_kernel_spmd(nc, in_maps, core_ids=list(range(len(states))), **kw)
        for st, r in zip(states, res.results):
            for (n, shape, dt) in self.outputs:
                st[n] = r[n]
        return res


def emit_stage_A(ctx, xin, x_fm, xn0_fm):
    NT = T // 128
    xt = [ctx.sbuf(f"A_xt{i}", [128, D], F32) for i in range(2)]
    sq = [ctx.sbuf(f"A_sq{i}", [128, D], F32) for i in range(2)]
    xnt = [ctx.sbuf(f"A_xn{i}", [128, D], F32) for i in range(2)]
    st = [ctx.sbuf(f"A_st{i}", [128, 8], F32) for i in range(2)]
    xev = [ctx.sbuf(f"A_xe{i}", [128, D], F32) for i in range(2)]
    xnev = [ctx.sbuf(f"A_xne{i}", [128, D], F32) for i in range(2)]
    pss = [ctx.bank(0), ctx.bank(1)]

    x_fm_v = x_fm.ap.rearrange("(c p) t -> p c t", p=128)
    xn_fm_v = xn0_fm.ap.rearrange("(c p) t -> p c t", p=128)
    epsc = ctx.constcol(1e-5)

    for ti in range(NT):
        b = ti % 2
        X, SQ, XN, S = xt[b], sq[b], xnt[b], st[b]
        ctx.dma("sync", X, X.ap, xin, xin.ap[ti * 128:(ti + 1) * 128, :])
        ctx.rop("vector", lambda e, X=X, SQ=SQ: e.tensor_mul(SQ.ap, X.ap, X.ap),
                reads=[X], writes=[SQ])
        ctx.rop("vector", lambda e, X=X, S=S: e.reduce_sum(S[:, 0:1], X.ap, AX.X),
                reads=[X], writes=[S])
        ctx.rop("vector", lambda e, SQ=SQ, S=S: e.reduce_sum(S[:, 1:2], SQ.ap, AX.X),
                reads=[SQ], writes=[S])
        ctx.rop("vector", lambda e, S=S: e.tensor_scalar_mul(S[:, 2:3], S[:, 0:1], 1.0 / D),
                reads=[S], writes=[S])
        ctx.rop("vector", lambda e, S=S: e.tensor_scalar_mul(S[:, 3:4], S[:, 1:2], 1.0 / D),
                reads=[S], writes=[S])
        ctx.rop("vector", lambda e, S=S: e.tensor_mul(S[:, 4:5], S[:, 2:3], S[:, 2:3]),
                reads=[S], writes=[S])
        ctx.rop("vector", lambda e, S=S: e.tensor_sub(S[:, 5:6], S[:, 3:4], S[:, 4:5]),
                reads=[S], writes=[S])
        ctx.rop("scalar", lambda e, S=S, epsc=epsc: e.activation(S[:, 6:7], S[:, 5:6], AF.Sqrt, bias=epsc),
                reads=[S, ctx.cconst], writes=[S])
        ctx.rop("vector", lambda e, S=S: e.reciprocal(S[:, 7:8], S[:, 6:7]),
                reads=[S], writes=[S])
        ctx.rop("vector", lambda e, X=X, XN=XN, S=S: e.tensor_scalar(
            XN.ap, X.ap, S[:, 2:3], S[:, 7:8], ALU.subtract, ALU.mult),
            reads=[X, S], writes=[XN])
        for g in range(4):
            pb = (ti * 4 + g) % 2
            srcbuf = X if g < 2 else XN
            P = pss[pb]
            for k in range(4):
                dc = (g % 2) * 4 + k
                ctx.rop("tensor", lambda e, sb=srcbuf, dc=dc, P=P, k=k: e.transpose(
                    P[:, k * 128:(k + 1) * 128], sb.ap[:, dc * 128:(dc + 1) * 128],
                    ctx.ident_f.ap),
                    reads=[srcbuf, ctx.ident_f], writes=[P])
            dstbuf = xev[b] if g < 2 else xnev[b]
            off = (g % 2) * 512
            ctx.rop("vector", lambda e, db=dstbuf, off=off, P=P: e.tensor_copy(
                db.ap[:, off:off + 512], P.ap), reads=[P], writes=[dstbuf])
        ctx.dma("sync", x_fm, x_fm_v[:, :, ti * 128:(ti + 1) * 128],
                xev[b], xev[b].ap.rearrange("p (c t) -> p c t", t=128))
        ctx.dma("sync", xn0_fm, xn_fm_v[:, :, ti * 128:(ti + 1) * 128],
                xnev[b], xnev[b].ap.rearrange("p (c t) -> p c t", t=128))


def launch_A():
    return Launch(
        "A",
        inputs=[("xin", (T, D), F32, lambda st: st["xin"])],
        outputs=[("x_fm", (D, T), F32), ("xn0_fm", (D, T), F32)],
        emit=lambda ctx, aps: emit_stage_A(ctx, aps["xin"], aps["x_fm"], aps["xn0_fm"]),
    )


def shard_inputs(inputs):
    x = np.ascontiguousarray(inputs["L0_attn_x"])
    states = []
    for c in range(NCORE):
        xs = x[c * NSEQ:(c + 1) * NSEQ].reshape(T, D)
        states.append({"xin": np.ascontiguousarray(xs), "weights": inputs})
    return states





def load_param_cols(ctx, tag, src_buf, n=D):
    nc_ = n // 128
    stage = ctx.sbuf(f"{tag}_pst", [nc_, 128], F32)
    ps = ctx.bank(0)
    out = ctx.sbuf(f"{tag}_pc", [128, nc_], F32)
    ctx.dma("sync", stage, stage.ap,
            src_buf, src_buf.ap.rearrange("(c p) -> c p", p=128))
    ctx.rop("tensor", lambda e: e.transpose(ps.ap[:, 0:nc_], stage.ap, ctx.ident_f.ap[0:nc_, 0:nc_]),
            reads=[stage, ctx.ident_f], writes=[ps])
    ctx.rop("vector", lambda e: e.tensor_copy(out.ap, ps.ap[:, 0:nc_]), reads=[ps], writes=[out])
    return out


def load_wT_bf(ctx, tag, w_buf, M, K):
    mt, kt = M // 128, K // 128
    wT = ctx.sbuf(f"{tag}_wT", [128, kt, mt, 128], BF16)
    stg = [ctx.sbuf(f"{tag}_ws{i}", [128, 128], BF16) for i in range(2)]
    pss = [ctx.bank(0), ctx.bank(1)]
    i = 0
    for mc in range(mt):
        for kc in range(kt):
            b = i % 2
            i += 1
            S, P = stg[b], pss[b]
            pv = P.ap.bitcast(BF16)[:, 0:128]
            ctx.dma("gpsimd", S, S.ap,
                    w_buf, w_buf.ap[mc * 128:(mc + 1) * 128, kc * 128:(kc + 1) * 128])
            ctx.rop("tensor", lambda e, S=S, pv=pv: e.transpose(pv, S.ap, ctx.ident_b.ap),
                    reads=[S, ctx.ident_b], writes=[P])
            ctx.rop("vector", lambda e, pv=pv, kc=kc, mc=mc: e.tensor_copy(
                wT.ap[:, kc, mc, :], pv), reads=[P], writes=[wT])
    return wT


def make_ones(ctx):
    if getattr(ctx, "_onesbf", None) is not None:
        return ctx._onesbf
    cur = getattr(ctx, "_cur", None)
    ctx._cur = None
    onesb = ctx.sbuf("onesb", [128, 1], BF16)
    onesf = ctx.sbuf("onesf_row", [1, 128], F32)
    ctx._cur = cur
    ctx.rop("vector", lambda e: e.memset(onesb.ap, 1.0), writes=[onesb])
    ctx.rop("vector", lambda e: e.memset(onesf.ap, 1.0), writes=[onesf])
    ctx._onesbf = (onesb, onesf)
    return onesb, onesf


def emit_stage_B(ctx, aps):
    xn_fm = aps["xn0_fm"]
    s0_fm = aps["s0_fm"]
    wrT = load_wT_bf(ctx, "wr", aps["W_r"], D, D)
    wiT = load_wT_bf(ctx, "wi", aps["W_i"], D, D)
    woT = load_wT_bf(ctx, "wo", aps["W_out"], D, D)
    cw = ctx.sbuf("B_cw", [128, NC, 4], F32)
    for c in range(NC):
        ctx.dma("sync", cw, cw.ap[:, c, :], aps["conv_w"],
                aps["conv_w"].ap[c * 128:(c + 1) * 128, :])
    cb = load_param_cols(ctx, "B_cb", aps["conv_b"])
    brc = load_param_cols(ctx, "B_br", aps["b_r"])
    bic = load_param_cols(ctx, "B_bi", aps["b_i"])
    lac = load_param_cols(ctx, "B_la", aps["log_a"])
    gac = load_param_cols(ctx, "B_ga", aps["gamma"])
    onesb, onesf = make_ones(ctx)

    la8 = ctx.sbuf("B_la8", [128, NC], F32)
    ctx.rop("scalar", lambda e: e.activation(la8.ap, lac.ap, AF.Sigmoid),
            reads=[lac], writes=[la8])
    ctx.rop("vector", lambda e: e.tensor_scalar_max(la8.ap, la8.ap, 1e-38),
            reads=[la8], writes=[la8])
    ctx.rop("scalar", lambda e: e.activation(la8.ap, la8.ap, AF.Ln),
            reads=[la8], writes=[la8])
    ctx.rop("vector", lambda e: e.tensor_scalar_mul(la8.ap, la8.ap, 8.0),
            reads=[la8], writes=[la8])

    xcv = ctx.sbuf("B_xcv", [128, NC, T], BF16)
    xnp = [ctx.sbuf(f"B_xnp{i}", [128, L + 3], BF16) for i in range(2)]
    xn_fm_v = xn_fm.ap.rearrange("(c p) t -> p c t", p=128)
    for c in range(NC):
        for s in range(NSEQ):
            b = (c * NSEQ + s) % 2
            XP = xnp[b]
            ctx.rop("vector", lambda e, XP=XP: e.memset(XP.ap[:, 0:3], 0.0), writes=[XP])
            ctx.dma("gpsimd", XP, XP.ap[:, 3:L + 3],
                    xn_fm, xn_fm_v[:, c, s * L:(s + 1) * L])
            tacc = ctx.sbuf(f"B_cvt{c}_{s}", [128, L], F32) if c == 0 and s == 0 else tacc
            ctx.rop("vector", lambda e, XP=XP, c=c: e.tensor_scalar(
                tacc.ap, XP.ap[:, 0:L], cw.ap[:, c, 0:1], None, ALU.mult),
                reads=[XP, cw], writes=[tacc])
            for j in range(1, 4):
                ctx.rop("vector", lambda e, XP=XP, c=c, j=j: e.scalar_tensor_tensor(
                    tacc.ap, XP.ap[:, j:j + L], cw.ap[:, c, j:j + 1], tacc.ap,
                    ALU.mult, ALU.add), reads=[XP, cw, tacc], writes=[tacc])
            ctx.rop("vector", lambda e, c=c, s=s: e.tensor_scalar_add(
                xcv.ap[:, c, s * L:(s + 1) * L], tacc.ap, cb.ap[:, c:c + 1]),
                reads=[tacc, cb], writes=[xcv])
            if "dbg_xcv" in aps:
                ctx.dma("sync", aps["dbg_xcv"],
                        aps["dbg_xcv"].ap.rearrange("(c p) t -> p c t", p=128)[:, c, s * L:(s + 1) * L],
                        xcv, xcv.ap[:, c, s * L:(s + 1) * L])

    h_dram = aps.get("dbg_h") or ctx.dram("B_h_dram", [D, T], BF16)
    h_dram_v = h_dram.ap.rearrange("(c p) t -> p c t", p=128)
    hs = [ctx.sbuf("B_hs0", [128, L], BF16)]
    zr_banks = [ctx.bank(i) for i in range(4)]
    zi_banks = [ctx.bank(4 + i) for i in range(4)]
    rt = ctx.sbuf("B_rt", [128, L], F32)
    it = ctx.sbuf("B_it", [128, L], F32)
    at = ctx.sbuf("B_at", [128, L], F32)
    t1 = ctx.sbuf("B_t1", [128, L], F32)
    t2 = rt
    for oc in range(NC):
        for s in range(NSEQ):
            for (banks, wT) in ((zr_banks, wrT), (zi_banks, wiT)):
                for nb in range(4):
                    ps = banks[nb]
                    for kc in range(NC):
                        ctx.rop("tensor", lambda e, ps=ps, wT=wT, nb=nb, kc=kc, oc=oc, s=s: e.matmul(
                            ps.ap,
                            wT.ap[:, kc, oc, :],
                            xcv.ap[:, kc, s * L + nb * 512: s * L + (nb + 1) * 512],
                            start=(kc == 0), stop=(kc == NC - 1)),
                            reads=[wT, xcv], writes=[ps])
            for nb in range(4):
                ctx.rop("scalar", lambda e, oc=oc, nb=nb: e.activation(
                    rt.ap[:, nb * 512:(nb + 1) * 512], zr_banks[nb].ap, AF.Sigmoid,
                    bias=brc.ap[:, oc:oc + 1]),
                    reads=[zr_banks[nb], brc], writes=[rt])
                ctx.rop("scalar", lambda e, oc=oc, nb=nb: e.activation(
                    it.ap[:, nb * 512:(nb + 1) * 512], zi_banks[nb].ap, AF.Sigmoid,
                    bias=bic.ap[:, oc:oc + 1]),
                    reads=[zi_banks[nb], bic], writes=[it])
            ctx.rop("scalar", lambda e, oc=oc: e.activation(at.ap, rt.ap, AF.Exp,
                                                            scale=la8.ap[:, oc:oc + 1]),
                    reads=[rt, la8], writes=[at])
            ctx.rop("vector", lambda e: e.tensor_mul(t1.ap, at.ap, at.ap),
                    reads=[at], writes=[t1])
            ctx.rop("vector", lambda e: e.tensor_scalar(t1.ap, t1.ap, -1.0, 1.0, ALU.mult, ALU.add),
                    reads=[t1], writes=[t1])
            ctx.rop("vector", lambda e: e.tensor_scalar_max(t1.ap, t1.ap, 0.0),
                    reads=[t1], writes=[t1])
            ctx.rop("scalar", lambda e: e.activation(t1.ap, t1.ap, AF.Sqrt),
                    reads=[t1], writes=[t1])
            ctx.rop("vector", lambda e, oc=oc, s=s: e.tensor_copy(
                t2.ap, xcv.ap[:, oc, s * L:(s + 1) * L]), reads=[xcv], writes=[t2])
            ctx.rop("vector", lambda e: e.tensor_mul(t1.ap, t1.ap, it.ap),
                    reads=[t1, it], writes=[t1])
            ctx.rop("vector", lambda e: e.tensor_mul(t2.ap, t1.ap, t2.ap),
                    reads=[t1, t2], writes=[t2])
            hb = hs[0]
            ctx.rop("vector", lambda e, hb=hb: e.tensor_tensor_scan(
                hb.ap, at.ap, t2.ap, 0.0, ALU.mult, ALU.add),
                reads=[at, t2], writes=[hb])
            ctx.dma("sync", h_dram, h_dram_v[:, oc, s * L:(s + 1) * L], hb, hb.ap)

    if "dbg_pc" in aps:
        ctx.dma("sync", aps["dbg_pc"], aps["dbg_pc"].ap[:, 0:NC], gac, gac.ap)
        ctx.dma("sync", aps["dbg_pc"], aps["dbg_pc"].ap[:, 8:8 + NC], la8, la8.ap)
        ctx.dma("sync", aps["dbg_pc"], aps["dbg_pc"].ap[:, 16:16 + NC], brc, brc.ap)
        ctx.dma("sync", aps["dbg_pc"], aps["dbg_pc"].ap[:, 24:24 + NC], cb, cb.ap)
    if "dbg_wT" in aps:
        ctx.dma("sync", aps["dbg_wT"], aps["dbg_wT"].ap.rearrange("p a b q -> p (a b q)"),
                woT, woT.ap.rearrange("p a b q -> p (a b q)"))
    ht = [ctx.sbuf("B_ht0", [128, NC, 512], BF16)]
    o_f = ctx.sbuf("B_of", [128, NC, 512], F32)
    osq = ctx.sbuf("B_osq", [128, NC, 512], BF16)
    ps_o = [ctx.bank(0), ctx.bank(1)]
    ps_r = ctx.bank(2)
    ps_b = ctx.bank(3)
    rowA = ctx.sbuf("B_rowA", [1, 512], F32)
    rrep = ctx.sbuf("B_rrep", [128, 512], F32)
    sout = ctx.sbuf("B_sout", [128, 512], F32)
    s0_fm_v = s0_fm.ap.rearrange("(c p) t -> p c t", p=128)
    for tc in range(T // 512):
        H = ht[0]
        ctx.dma("sync", H, H.ap, h_dram, h_dram_v[:, :, tc * 512:(tc + 1) * 512])
        for mc in range(NC):
            P = ps_o[mc % 2]
            for kc in range(NC):
                ctx.rop("tensor", lambda e, P=P, H=H, mc=mc, kc=kc: e.matmul(
                    P.ap, woT.ap[:, kc, mc, :], H.ap[:, kc, :],
                    start=(kc == 0), stop=(kc == NC - 1)),
                    reads=[woT, H], writes=[P])
            ctx.rop("vector", lambda e, P=P, mc=mc: e.tensor_copy(o_f.ap[:, mc, :], P.ap),
                    reads=[P], writes=[o_f])
            ctx.vfence(P)
            if "dbg_o" in aps:
                ctx.dma("sync", aps["dbg_o"],
                        aps["dbg_o"].ap.rearrange("(c p) t -> p c t", p=128)[:, mc, tc * 512:(tc + 1) * 512],
                        o_f, o_f.ap[:, mc, :])
            ctx.rop("scalar", lambda e, mc=mc: e.square(osq.ap[:, mc, :], o_f.ap[:, mc, :]),
                    reads=[o_f], writes=[osq])
        for mc in range(NC):
            ctx.rop("tensor", lambda e, mc=mc: e.matmul(
                ps_r.ap[0:1, :], onesb.ap, osq.ap[:, mc, :],
                start=(mc == 0), stop=(mc == NC - 1)),
                reads=[onesb, osq], writes=[ps_r])
        ctx.rop("vector", lambda e: e.tensor_scalar_mul(rowA.ap, ps_r.ap[0:1, :], 1.0 / D),
                reads=[ps_r], writes=[rowA])
        ctx.rop("scalar", lambda e: e.activation(rowA.ap, rowA.ap, AF.Sqrt,
                                                 bias=ctx.constcol(1e-6)[0:1, :]),
                reads=[rowA, ctx.cconst], writes=[rowA])
        ctx.rop("vector", lambda e: e.reciprocal(rowA.ap, rowA.ap),
                reads=[rowA], writes=[rowA])
        ctx.rop("tensor", lambda e: e.matmul(ps_b.ap, onesf.ap, rowA.ap,
                                             start=True, stop=True),
                reads=[onesf, rowA], writes=[ps_b])
        ctx.rop("vector", lambda e: e.tensor_copy(rrep.ap, ps_b.ap),
                reads=[ps_b], writes=[rrep])
        for mc in range(NC):
            ctx.rop("vector", lambda e, mc=mc: e.scalar_tensor_tensor(
                sout.ap, o_f.ap[:, mc, :], gac.ap[:, mc:mc + 1], rrep.ap,
                ALU.mult, ALU.mult), reads=[o_f, gac, rrep], writes=[sout])
            ctx.dma("sync", s0_fm, s0_fm_v[:, mc, tc * 512:(tc + 1) * 512],
                    sout, sout.ap)


def launch_B(dbg=False):
    W = lambda name: (lambda st: np.ascontiguousarray(st["weights"][name]))
    extra = [("dbg_xcv", (D, T), BF16), ("dbg_h", (D, T), BF16), ("dbg_o", (D, T), F32),
             ("dbg_wT", (128, 8, 8, 128), BF16), ("dbg_pc", (128, 32), F32)] if dbg else []
    return Launch(
        "Bdbg" if dbg else "B",
        inputs=[
            ("xn0_fm", (D, T), F32, lambda st: st["xn0_fm"]),
            ("conv_w", (D, 4), F32, lambda st: np.ascontiguousarray(
                st["weights"]["L0_attn_conv_w"].reshape(D, 4))),
            ("conv_b", (D,), F32, W("L0_attn_conv_b")),
            ("W_r", (D, D), F32, W("L0_attn_W_r")),
            ("b_r", (D,), F32, W("L0_attn_b_r")),
            ("W_i", (D, D), F32, W("L0_attn_W_i")),
            ("b_i", (D,), F32, W("L0_attn_b_i")),
            ("log_a", (D,), F32, W("L0_attn_log_a")),
            ("W_out", (D, D), F32, W("L0_attn_W_out")),
            ("gamma", (D,), F32, W("L0_attn_gamma")),
        ],
        outputs=[("s0_fm", (D, T), F32)] + extra,
        emit=lambda ctx, aps: emit_stage_B(ctx, aps),
    )


def emit_ln_fm2(ctx, tag, get_tile, onesb, onesf, murep, rstdrep, row4, scr,
                banks=(6, 7), eps=1e-5):
    prow = ctx.bank(banks[0])
    pbc = ctx.bank(banks[1])
    for kc in range(NC):
        b, ap = get_tile(kc)
        ctx.rop("tensor", lambda e, ap=ap, kc=kc: e.matmul(
            prow.ap[0:1, :], onesb.ap, ap, start=(kc == 0), stop=(kc == NC - 1)),
            reads=[onesb, b], writes=[prow])
    ctx.rop("vector", lambda e: e.tensor_scalar_mul(row4.ap[0:1, 2, :], prow.ap[0:1, :], 1.0 / D),
            reads=[prow], writes=[row4])
    for kc in range(NC):
        b, ap = get_tile(kc)
        ctx.rop("scalar", lambda e, ap=ap: e.square(scr.ap, ap), reads=[b], writes=[scr])
        ctx.rop("tensor", lambda e, kc=kc: e.matmul(
            prow.ap[0:1, :], onesb.ap, scr.ap, start=(kc == 0), stop=(kc == NC - 1)),
            reads=[onesb, scr], writes=[prow])
    ctx.rop("vector", lambda e: e.tensor_scalar_mul(row4.ap[0:1, 1, :], prow.ap[0:1, :], 1.0 / D),
            reads=[prow], writes=[row4])
    ctx.rop("vector", lambda e: e.tensor_mul(row4.ap[0:1, 0, :], row4.ap[0:1, 2, :], row4.ap[0:1, 2, :]),
            reads=[row4], writes=[row4])
    ctx.rop("vector", lambda e: e.tensor_sub(row4.ap[0:1, 0, :], row4.ap[0:1, 1, :], row4.ap[0:1, 0, :]),
            reads=[row4], writes=[row4])
    ctx.rop("scalar", lambda e: e.activation(row4.ap[0:1, 0, :], row4.ap[0:1, 0, :], AF.Sqrt,
                                             bias=ctx.constcol(eps)[0:1, :]),
            reads=[row4, ctx.cconst], writes=[row4])
    ctx.rop("vector", lambda e: e.reciprocal(row4.ap[0:1, 3, :], row4.ap[0:1, 0, :]),
            reads=[row4], writes=[row4])
    ctx.rop("tensor", lambda e: e.matmul(pbc.ap, onesf.ap, row4.ap[0:1, 2, :],
                                         start=True, stop=True),
            reads=[onesf, row4], writes=[pbc])
    ctx.rop("vector", lambda e: e.tensor_copy(murep.ap, pbc.ap), reads=[pbc], writes=[murep])
    ctx.rop("tensor", lambda e: e.matmul(pbc.ap, onesf.ap, row4.ap[0:1, 3, :],
                                         start=True, stop=True),
            reads=[onesf, row4], writes=[pbc])
    ctx.rop("vector", lambda e: e.tensor_copy(rstdrep.ap, pbc.ap), reads=[pbc], writes=[rstdrep])


def emit_stage_C(ctx, aps, qkvT_dram):
    xn_fm = aps["xn0_fm"]
    xn_fm_v = xn_fm.ap.rearrange("(c p) t -> p c t", p=128)
    qkv_v = qkvT_dram.ap.rearrange("(c p) t -> p c t", p=128)
    wq = ctx.sbuf("C_wq", [128, NC, 24, 128], BF16)
    for kc in range(NC):
        ctx.dma("gpsimd", wq, wq.ap[:, kc, :, :].rearrange("p a b -> p (a b)"),
                aps["w_qkv"], aps["w_qkv"].ap[kc * 128:(kc + 1) * 128, :])
    bq = load_param_cols(ctx, "C_bq", aps["b_qkv"], n=3 * D)
    g1 = load_param_cols(ctx, "C_g1", aps["ln1_g"])
    b1 = load_param_cols(ctx, "C_b1", aps["ln1_b"])
    onesb, onesf = make_ones(ctx)

    xnt = [ctx.sbuf(f"C_xnt{i}", [128, NC, 512], BF16) for i in range(2)]
    h1t = [ctx.sbuf(f"C_h1t{i}", [128, NC, 512], BF16) for i in range(2)]
    murep = ctx.sbuf("C_mu", [128, 512], F32)
    rstdrep = ctx.sbuf("C_rs", [128, 512], F32)
    row4 = ctx.sbuf("C_row4", [1, 4, 512], F32)
    scr = ctx.sbuf("C_scr", [128, 512], BF16)
    tf = ctx.sbuf("C_tf", [128, 512], F32)
    qev = [ctx.sbuf(f"C_qev{i}", [128, 512], BF16) for i in range(2)]

    for tc in range(T // 512):
        X = xnt[tc % 2]
        H = h1t[tc % 2]
        ctx.dma("gpsimd", X, X.ap, xn_fm, xn_fm_v[:, :, tc * 512:(tc + 1) * 512])
        emit_ln_fm2(ctx, "C", lambda kc: (X, X.ap[:, kc, :]), onesb, onesf,
                    murep, rstdrep, row4, scr)
        for kc in range(NC):
            ctx.rop("vector", lambda e, X=X, kc=kc: e.tensor_sub(tf.ap, X.ap[:, kc, :], murep.ap),
                    reads=[X, murep], writes=[tf])
            ctx.rop("vector", lambda e: e.tensor_mul(tf.ap, tf.ap, rstdrep.ap),
                    reads=[tf, rstdrep], writes=[tf])
            ctx.rop("vector", lambda e, H=H, kc=kc: e.tensor_scalar(
                H.ap[:, kc, :], tf.ap, g1.ap[:, kc:kc + 1], b1.ap[:, kc:kc + 1],
                ALU.mult, ALU.add), reads=[tf, g1, b1], writes=[H])
        for mc in range(24):
            P = ctx.bank(mc % 2)
            for kc in range(NC):
                ctx.rop("tensor", lambda e, P=P, H=H, mc=mc, kc=kc: e.matmul(
                    P.ap, wq.ap[:, kc, mc, :], H.ap[:, kc, :],
                    start=(kc == 0), stop=(kc == NC - 1)),
                    reads=[wq, H], writes=[P])
            Q = qev[mc % 2]
            ctx.rop("vector", lambda e, P=P, Q=Q, mc=mc: e.tensor_scalar_add(
                Q.ap, P.ap, bq.ap[:, mc:mc + 1]),
                reads=[P, bq], writes=[Q])
            ctx.dma("sync", qkvT_dram, qkv_v[:, mc, tc * 512:(tc + 1) * 512], Q, Q.ap)


def launch_C():
    W = lambda name: (lambda st: np.ascontiguousarray(st["weights"][name]))
    return Launch(
        "C",
        inputs=[
            ("xn0_fm", (D, T), F32, lambda st: st["xn0_fm"]),
            ("ln1_g", (D,), F32, W("L0_ffn_ln1_g")),
            ("ln1_b", (D,), F32, W("L0_ffn_ln1_b")),
            ("w_qkv", (D, 3 * D), F32, W("L0_ffn_w_qkv")),
            ("b_qkv", (3 * D,), F32, W("L0_ffn_b_qkv")),
        ],
        outputs=[("qkvT", (3 * D, T), BF16)],
        emit=lambda ctx, aps: emit_stage_C(ctx, aps, aps["qkvT"]),
    )


def emit_attention(ctx, tag, qkvT, oT_dram, nheads=16, dk=64, scale=0.125):
    qv = qkvT.ap
    onesb, onesf = make_ones(ctx)
    qaug = ctx.sbuf(f"{tag}_qaug", [65, L], BF16)
    kaug = ctx.sbuf(f"{tag}_kaug", [65, L], BF16)
    vln = ctx.sbuf(f"{tag}_vln", [64, L], BF16)
    vtok = ctx.sbuf(f"{tag}_vtok", [128, 16, 64], BF16)
    expT = ctx.sbuf(f"{tag}_expT", [128, 16, 512], BF16)
    maxc = ctx.sbuf(f"{tag}_maxc", [128, 16], F32)
    mrow = ctx.sbuf(f"{tag}_mrow", [16, 128], BF16)
    maxb = ctx.sbuf(f"{tag}_maxb", [128, 16], BF16)
    sumr = ctx.sbuf(f"{tag}_sumr", [1, 512], F32)
    mrow_dram = ctx.dram(f"{tag}_mrowd", [16, 128], BF16)
    rrep = ctx.sbuf(f"{tag}_rrep", [64, 512], F32)
    oev = ctx.sbuf(f"{tag}_oev", [64, 512], BF16)
    oT_v = oT_dram.ap.rearrange("(c p) t -> p c t", p=128)

    for s in range(NSEQ):
        for h in range(nheads):
            t0 = s * L
            ctx.dma("sync", qaug, qaug.ap[0:64, :], qkvT, qv[h * dk:(h + 1) * dk, t0:t0 + L])
            ctx.dma("sync", kaug, kaug.ap[0:64, :], qkvT, qv[D + h * dk:D + (h + 1) * dk, t0:t0 + L])
            ctx.dma("sync", vln, vln.ap, qkvT, qv[2 * D + h * dk:2 * D + (h + 1) * dk, t0:t0 + L])
            ctx.rop("vector", lambda e: e.memset(kaug.ap[64:65, :], 1.0), writes=[kaug])
            for kb in range(16):
                P = ctx.bank(4 + kb % 2)
                pv = P.ap.bitcast(BF16)[:, 0:64]
                ctx.rop("tensor", lambda e, kb=kb, pv=pv: e.transpose(
                    pv, vln.ap[:, kb * 128:(kb + 1) * 128], ctx.ident_b.ap[0:64, 0:64]),
                    reads=[vln, ctx.ident_b], writes=[P])
                ctx.rop("vector", lambda e, kb=kb, pv=pv: e.tensor_copy(
                    vtok.ap[:, kb, :], pv), reads=[P], writes=[vtok])
            for qt in range(16):
                pb = [ctx.bank(i) for i in range(4)]
                for nb in range(4):
                    ctx.rop("tensor", lambda e, qt=qt, nb=nb: e.matmul(
                        pb[nb].ap, qaug.ap[0:64, qt * 128:(qt + 1) * 128],
                        kaug.ap[0:64, nb * 512:(nb + 1) * 512],
                        start=True, stop=True),
                        reads=[qaug, kaug], writes=[pb[nb]])
                ctx.rop("vector", lambda e, qt=qt: e.reduce_max(
                    maxc.ap[:, qt:qt + 1], ctx.pspan(0, 4), AX.X),
                    reads=[ctx.bank(0), ctx.bank(1), ctx.bank(2), ctx.bank(3)],
                    writes=[maxc])
            ctx.rop("vector", lambda e: e.tensor_scalar_mul(maxb.ap, maxc.ap, -scale),
                    reads=[maxc], writes=[maxb])
            P = ctx.bank(4)
            pv = P.ap.bitcast(BF16)[0:16, 0:128]
            ctx.rop("tensor", lambda e, pv=pv: e.transpose(pv, maxb.ap, ctx.ident_b.ap),
                    reads=[maxb, ctx.ident_b], writes=[P])
            ctx.rop("vector", lambda e, pv=pv: e.tensor_copy(mrow.ap, pv),
                    reads=[P], writes=[mrow])
            ctx.dma("gpsimd", mrow_dram, mrow_dram.ap, mrow, mrow.ap)
            ctx.dma("gpsimd", qaug, qaug.ap[64:65, :],
                    mrow_dram, mrow_dram.ap.rearrange("j t -> () (j t)"))
            for nb in range(4):
                q0 = nb * 512
                for kb in range(16):
                    P = ctx.bank(kb % 2)
                    ctx.rop("tensor", lambda e, P=P, kb=kb, q0=q0: e.matmul(
                        P.ap, kaug.ap[:, kb * 128:(kb + 1) * 128],
                        qaug.ap[:, t0 * 0 + q0:q0 + 512],
                        start=True, stop=True),
                        reads=[kaug, qaug], writes=[P])
                    ctx.rop("scalar", lambda e, P=P, kb=kb: e.activation(
                        expT.ap[:, kb, :], P.ap, AF.Exp, scale=scale),
                        reads=[P], writes=[expT])
                    ctx.rop("tensor", lambda e, kb=kb: e.matmul(
                        ctx.bank(3).ap[0:1, :], onesb.ap, expT.ap[:, kb, :],
                        start=(kb == 0), stop=(kb == 15)),
                        reads=[onesb, expT], writes=[ctx.bank(3)])
                    ctx.rop("tensor", lambda e, kb=kb: e.matmul(
                        ctx.bank(2).ap[0:64, :], vtok.ap[:, kb, :], expT.ap[:, kb, :],
                        start=(kb == 0), stop=(kb == 15)),
                        reads=[vtok, expT], writes=[ctx.bank(2)])
                ctx.rop("vector", lambda e: e.reciprocal(sumr.ap, ctx.bank(3).ap[0:1, :]),
                        reads=[ctx.bank(3)], writes=[sumr])
                ctx.rop("tensor", lambda e: e.matmul(
                    ctx.bank(5).ap[0:64, :], onesf.ap[:, 0:64], sumr.ap,
                    start=True, stop=True),
                    reads=[onesf, sumr], writes=[ctx.bank(5)])
                ctx.rop("vector", lambda e: e.tensor_copy(rrep.ap, ctx.bank(5).ap[0:64, :]),
                        reads=[ctx.bank(5)], writes=[rrep])
                ctx.rop("vector", lambda e: e.tensor_mul(oev.ap, ctx.bank(2).ap[0:64, :], rrep.ap),
                        reads=[ctx.bank(2), rrep], writes=[oev])
                ctx.dma("sync", oT_dram, oT_dram.ap[h * dk:(h + 1) * dk, t0 + q0:t0 + q0 + 512],
                        oev, oev.ap)


def emit_stage_C2(ctx, aps):
    qkvT = aps["qkvT"]
    x2_fm = aps["x2_fm"]
    oT_dram = ctx.dram("C2_oT", [D, T], BF16)
    emit_attention(ctx, "C2", qkvT, oT_dram)
    wo = ctx.sbuf("C2_wo", [128, NC, NC, 128], BF16)
    for kc in range(NC):
        ctx.dma("gpsimd", wo, wo.ap[:, kc, :, :].rearrange("p a b -> p (a b)"),
                aps["w_o"], aps["w_o"].ap[kc * 128:(kc + 1) * 128, :])
    bo = load_param_cols(ctx, "C2_bo", aps["b_o"])
    oT_v = oT_dram.ap.rearrange("(c p) t -> p c t", p=128)
    xn_v = aps["xn0_fm"].ap.rearrange("(c p) t -> p c t", p=128)
    x2_v = x2_fm.ap.rearrange("(c p) t -> p c t", p=128)
    ot = [ctx.sbuf(f"C2_ot{i}", [128, NC, 512], BF16) for i in range(2)]
    xnt = [ctx.sbuf(f"C2_xnt{i}", [128, 512], F32) for i in range(2)]
    x2t = [ctx.sbuf(f"C2_x2t{i}", [128, 512], F32) for i in range(2)]
    for tc in range(T // 512):
        O = ot[tc % 2]
        ctx.dma("sync", O, O.ap, oT_dram, oT_v[:, :, tc * 512:(tc + 1) * 512])
        for mc in range(NC):
            P = ctx.bank(6 + mc % 2)
            for kc in range(NC):
                ctx.rop("tensor", lambda e, P=P, O=O, mc=mc, kc=kc: e.matmul(
                    P.ap, wo.ap[:, kc, mc, :], O.ap[:, kc, :],
                    start=(kc == 0), stop=(kc == NC - 1)),
                    reads=[wo, O], writes=[P])
            XN = xnt[mc % 2]
            X2 = x2t[mc % 2]
            ctx.dma("gpsimd", XN, XN.ap, aps["xn0_fm"], xn_v[:, mc, tc * 512:(tc + 1) * 512])
            ctx.rop("vector", lambda e, P=P, XN=XN, X2=X2, mc=mc: e.scalar_tensor_tensor(
                X2.ap, P.ap, bo.ap[:, mc:mc + 1], XN.ap, ALU.add, ALU.add),
                reads=[P, bo, XN], writes=[X2])
            ctx.dma("sync", x2_fm, x2_v[:, mc, tc * 512:(tc + 1) * 512], X2, X2.ap)


def launch_C2():
    W = lambda name: (lambda st: np.ascontiguousarray(st["weights"][name]))
    return Launch(
        "C2",
        inputs=[
            ("qkvT", (3 * D, T), BF16, lambda st: st["qkvT"]),
            ("xn0_fm", (D, T), F32, lambda st: st["xn0_fm"]),
            ("w_o", (D, D), F32, W("L0_ffn_w_o")),
            ("b_o", (D,), F32, W("L0_ffn_b_o")),
        ],
        outputs=[("x2_fm", (D, T), F32)],
        emit=lambda ctx, aps: emit_stage_C2(ctx, aps),
    )


def emit_stage_D1(ctx, aps):
    x2 = aps["x2_fm"]
    x2_v = x2.ap.rearrange("(c p) t -> p c t", p=128)
    h2d = aps["h2T"]
    h2_v = h2d.ap.rearrange("(c p) t -> p c t", p=128)
    ewT = aps["ewT"]
    g2c = load_param_cols(ctx, "D_g2", aps["ln2_g"])
    b2c = load_param_cols(ctx, "D_b2", aps["ln2_b"])
    onesb, onesf = make_ones(ctx)
    gw = ctx.sbuf("D_gw", [128, NC, 8], F32)
    for kc in range(NC):
        ctx.dma("sync", gw, gw.ap[:, kc, :], aps["gate_w"],
                aps["gate_w"].ap[kc * 128:(kc + 1) * 128, :])
    gb_st = ctx.sbuf("D_gbst", [1, 8], F32)
    gb = ctx.sbuf("D_gb", [8, 1], F32)
    ctx.dma("sync", gb_st, gb_st.ap, aps["gate_b"], aps["gate_b"].ap.rearrange("e -> () e"))
    Pg = ctx.bank(7)
    ctx.rop("tensor", lambda e: e.transpose(Pg.ap[0:8, 0:1], gb_st.ap, ctx.ident_f.ap[0:1, 0:1]),
            reads=[gb_st, ctx.ident_f], writes=[Pg])
    ctx.rop("vector", lambda e: e.tensor_copy(gb.ap, Pg.ap[0:8, 0:1]), reads=[Pg], writes=[gb])

    xt = [ctx.sbuf(f"D_xt{i}", [128, NC, 512], BF16) for i in range(2)]
    ht = [ctx.sbuf(f"D_ht{i}", [128, NC, 512], BF16) for i in range(2)]
    hf = ctx.sbuf("D_hf", [128, NC, 512], F32)
    murep = ctx.sbuf("D_mu", [128, 512], F32)
    rstdrep = ctx.sbuf("D_rs", [128, 512], F32)
    row4 = ctx.sbuf("D_row4", [1, 4, 512], F32)
    scr = ctx.sbuf("D_scr", [128, 512], BF16)
    tf = ctx.sbuf("D_tf", [128, 512], F32)
    lg = ctx.sbuf("D_lg", [8, 512], F32)
    ltok = ctx.sbuf("D_ltok", [128, 8], F32)
    cols = ctx.sbuf("D_cols", [128, 8], F32)
    ewtok = ctx.sbuf("D_ewtok", [128, 8], BF16)
    ewev = ctx.sbuf("D_ewev", [8, 128], BF16)
    lmask = ctx.sbuf("D_lmask", [128, 8], F32)
    ewf = ctx.sbuf("D_ewf", [128, 8], F32)

    for tc in range(T // 512):
        X = xt[tc % 2]
        H = ht[tc % 2]
        ctx.dma("gpsimd", X, X.ap, x2, x2_v[:, :, tc * 512:(tc + 1) * 512])
        emit_ln_fm2(ctx, "D", lambda kc: (X, X.ap[:, kc, :]), onesb, onesf,
                    murep, rstdrep, row4, scr)
        for kc in range(NC):
            ctx.rop("vector", lambda e, X=X, kc=kc: e.tensor_sub(tf.ap, X.ap[:, kc, :], murep.ap),
                    reads=[X, murep], writes=[tf])
            ctx.rop("vector", lambda e: e.tensor_mul(tf.ap, tf.ap, rstdrep.ap),
                    reads=[tf, rstdrep], writes=[tf])
            ctx.rop("vector", lambda e, kc=kc: e.tensor_scalar(
                hf.ap[:, kc, :], tf.ap, g2c.ap[:, kc:kc + 1], b2c.ap[:, kc:kc + 1],
                ALU.mult, ALU.add), reads=[tf, g2c, b2c], writes=[hf])
            ctx.rop("vector", lambda e, H=H, kc=kc: e.tensor_copy(
                H.ap[:, kc, :], hf.ap[:, kc, :]), reads=[hf], writes=[H])
        ctx.dma("sync", h2d, h2_v[:, :, tc * 512:(tc + 1) * 512], H, H.ap)
        Pl = ctx.bank(6)
        for kc in range(NC):
            ctx.rop("tensor", lambda e, kc=kc: e.matmul(
                Pl.ap[0:8, :], gw.ap[:, kc, :], hf.ap[:, kc, :],
                start=(kc == 0), stop=(kc == NC - 1)),
                reads=[gw, hf], writes=[Pl])
        ctx.rop("vector", lambda e: e.tensor_scalar_add(lg.ap, Pl.ap[0:8, :], gb.ap),
                reads=[Pl, gb], writes=[lg])
        for j in range(4):
            Pt = ctx.bank(7)
            ctx.rop("tensor", lambda e, j=j: e.transpose(
                Pt.ap[:, 0:8], lg.ap[:, j * 128:(j + 1) * 128], ctx.ident_f.ap[0:8, 0:8]),
                reads=[lg, ctx.ident_f], writes=[Pt])
            ctx.rop("vector", lambda e: e.tensor_copy(ltok.ap, Pt.ap[:, 0:8]),
                    reads=[Pt], writes=[ltok])
            ctx.rop("vector", lambda e: e.reduce_max(cols.ap[:, 0:1], ltok.ap, AX.X),
                    reads=[ltok], writes=[cols])
            ctx.rop("vector", lambda e: e.tensor_scalar(
                lmask.ap, ltok.ap, cols.ap[:, 0:1], None, ALU.is_equal),
                reads=[ltok, cols], writes=[lmask])
            ctx.rop("vector", lambda e: e.scalar_tensor_tensor(
                ewf.ap, lmask.ap, -1e38, ltok.ap, ALU.mult, ALU.add),
                reads=[lmask, ltok], writes=[ewf])
            ctx.rop("vector", lambda e: e.reduce_max(cols.ap[:, 1:2], ewf.ap, AX.X),
                    reads=[ewf], writes=[cols])
            ctx.rop("vector", lambda e: e.tensor_sub(cols.ap[:, 2:3], cols.ap[:, 1:2], cols.ap[:, 0:1]),
                    reads=[cols], writes=[cols])
            ctx.rop("scalar", lambda e: e.activation(cols.ap[:, 3:4], cols.ap[:, 2:3], AF.Exp),
                    reads=[cols], writes=[cols])
            ctx.rop("vector", lambda e: e.tensor_scalar_add(cols.ap[:, 4:5], cols.ap[:, 3:4], 1.0),
                    reads=[cols], writes=[cols])
            ctx.rop("vector", lambda e: e.reciprocal(cols.ap[:, 5:6], cols.ap[:, 4:5]),
                    reads=[cols], writes=[cols])
            ctx.rop("vector", lambda e: e.tensor_mul(cols.ap[:, 6:7], cols.ap[:, 3:4], cols.ap[:, 5:6]),
                    reads=[cols], writes=[cols])
            ctx.rop("vector", lambda e: e.tensor_scalar(
                cols.ap[:, 7:8], ltok.ap[:, 0:1], 0.0, 0.0, ALU.mult, ALU.mult),
                reads=[ltok], writes=[cols])
            ctx.rop("vector", lambda e: e.tensor_scalar(
                ewf.ap, ewf.ap, cols.ap[:, 1:2], None, ALU.is_equal),
                reads=[ewf, cols], writes=[ewf])
            ctx.rop("vector", lambda e: e.tensor_scalar(
                ewf.ap, ewf.ap, cols.ap[:, 6:7], None, ALU.mult),
                reads=[ewf, cols], writes=[ewf])
            ctx.rop("vector", lambda e: e.scalar_tensor_tensor(
                ewf.ap, lmask.ap, cols.ap[:, 5:6], ewf.ap, ALU.mult, ALU.add),
                reads=[lmask, cols, ewf], writes=[ewf])
            ctx.rop("vector", lambda e: e.tensor_copy(ewtok.ap, ewf.ap),
                    reads=[ewf], writes=[ewtok])
            Pe = ctx.bank(5)
            pv = Pe.ap.bitcast(BF16)[0:8, 0:128]
            ctx.rop("tensor", lambda e, pv=pv: e.transpose(pv, ewtok.ap, ctx.ident_b.ap),
                    reads=[ewtok, ctx.ident_b], writes=[Pe])
            ctx.rop("vector", lambda e, pv=pv: e.tensor_copy(ewev.ap, pv),
                    reads=[Pe], writes=[ewev])
            ctx.dma("sync", ewT, ewT.ap[:, tc * 512 + j * 128: tc * 512 + (j + 1) * 128],
                    ewev, ewev.ap)


def launch_D1():
    W = lambda name: (lambda st: np.ascontiguousarray(st["weights"][name]))
    return Launch(
        "D1",
        inputs=[
            ("x2_fm", (D, T), F32, lambda st: st["x2_fm"]),
            ("ln2_g", (D,), F32, W("L0_ffn_ln2_g")),
            ("ln2_b", (D,), F32, W("L0_ffn_ln2_b")),
            ("gate_w", (D, 8), F32, W("L0_ffn_gate_w")),
            ("gate_b", (8,), F32, W("L0_ffn_gate_b")),
        ],
        outputs=[("h2T", (D, T), BF16), ("ewT", (8, T), BF16)],
        emit=lambda ctx, aps: emit_stage_D1(ctx, aps),
    )


def emit_stage_D2(ctx, aps):
    E = 8
    DFF = 4096
    h2_v = aps["h2T"].ap.rearrange("(c p) t -> p c t", p=128)
    ffn_v = aps["ffn0_fm"].ap.rearrange("(c p) t -> p c t", p=128)
    x2_v = aps["x2_fm"].ap.rearrange("(c p) t -> p c t", p=128)
    onesb, onesf = make_ones(ctx)
    onesb_row = ctx.sbuf("E_onesrow", [1, 128], BF16)
    ctx.rop("vector", lambda e: e.memset(onesb_row.ap, 1.0), writes=[onesb_row])

    b1c = ctx.sbuf("E_b1c", [128, E, 32], F32)
    b1st = ctx.sbuf("E_b1st", [32, 128], F32)
    for e_ in range(E):
        ctx.dma("sync", b1st, b1st.ap, aps["b1"],
                aps["b1"].ap[e_, :].rearrange("(a b) -> a b", b=128))
        Pb = ctx.bank(5)
        ctx.rop("tensor", lambda en, e_=e_: en.transpose(
            Pb.ap[:, 0:32], b1st.ap, ctx.ident_f.ap[0:32, 0:32]),
            reads=[b1st, ctx.ident_f], writes=[Pb])
        ctx.rop("vector", lambda en, e_=e_: en.tensor_copy(b1c.ap[:, e_, :], Pb.ap[:, 0:32]),
                reads=[Pb], writes=[b1c])
    b2w = ctx.sbuf("E_b2w", [8, D], BF16)
    ctx.dma("gpsimd", b2w, b2w.ap, aps["b2"], aps["b2"].ap)
    ewTs = ctx.sbuf("E_ewT", [8, 2048], BF16)

    w1b = [ctx.sbuf(f"E_w1b{i}", [128, NC, 16, 128], BF16) for i in range(2)]
    w2b = [ctx.sbuf(f"E_w2b{i}", [128, 16, NC, 128], BF16) for i in range(2)]
    h2t = ctx.sbuf("E_h2t", [128, NC, 512], BF16)
    hid = ctx.sbuf("E_hid", [128, 16, 512], BF16)
    acc = ctx.sbuf("E_acc", [128, NC, 2048], BF16)
    ewrep = ctx.sbuf("E_ewrep", [128, 2048], BF16)
    ewrow = ctx.sbuf("E_ewrow", [1, 2048], BF16)
    tmp = ctx.sbuf("E_tmp", [128, 512], BF16)
    x2t = ctx.sbuf("E_x2t", [128, 512], F32)
    ff = ctx.sbuf("E_ff", [128, 512], F32)

    HALF = 2048
    small = bool(int(os.environ.get("MK_D2_SMALL", "0")))
    for half in range(1 if small else 2):
        t0 = half * HALF
        ctx.dma("sync", ewTs, ewTs.ap, aps["ewT"], aps["ewT"].ap[:, t0:t0 + HALF])
        for dc in range(NC):
            for q in range(4):
                Pq = ctx.bank(4 + (q % 2))
                ctx.rop("tensor", lambda e, dc=dc, q=q, Pq=Pq: e.matmul(
                    Pq.ap, b2w.ap[:, dc * 128:(dc + 1) * 128],
                    ewTs.ap[:, q * 512:(q + 1) * 512],
                    start=True, stop=True),
                    reads=[b2w, ewTs], writes=[Pq])
                ctx.rop("vector", lambda e, dc=dc, q=q, Pq=Pq: e.tensor_copy(
                    acc.ap[:, dc, q * 512:(q + 1) * 512], Pq.ap),
                    reads=[Pq], writes=[acc])
        for e_ in range(2 if small else E):
            ctx.dma("sync", ewrow, ewrow.ap, aps["ewT"],
                    aps["ewT"].ap[e_:e_ + 1, t0:t0 + HALF])
            for q in range(4):
                Pq = ctx.bank(4 + (q % 2))
                ctx.rop("tensor", lambda en, e_=e_, q=q, Pq=Pq: en.matmul(
                    Pq.ap, onesb_row.ap, ewrow.ap[:, q * 512:(q + 1) * 512],
                    start=True, stop=True),
                    reads=[onesb_row, ewrow], writes=[Pq])
                ctx.rop("vector", lambda en, q=q, Pq=Pq: en.tensor_copy(
                    ewrep.ap[:, q * 512:(q + 1) * 512], Pq.ap),
                    reads=[Pq], writes=[ewrep])
            for fh in range(2):
                W1 = w1b[fh % 2]
                W2 = w2b[fh % 2]
                f0 = fh * 2048
                for kc in range(NC):
                    ctx.dma("gpsimd", W1, W1.ap[:, kc, :, :].rearrange("p a b -> p (a b)"),
                            aps["W1"], aps["W1"].ap[e_, kc * 128:(kc + 1) * 128, f0:f0 + 2048])
                for fc in range(16):
                    ctx.dma("gpsimd", W2, W2.ap[:, fc, :, :].rearrange("p a b -> p (a b)"),
                            aps["W2"], aps["W2"].ap[e_, f0 + fc * 128: f0 + (fc + 1) * 128, :])
                for tcq in range(1 if small else 4):
                    q0 = t0 + tcq * 512
                    ctx.dma("gpsimd", h2t, h2t.ap, aps["h2T"], h2_v[:, :, q0:q0 + 512])
                    for fc in range(16):
                        P = ctx.bank(fc % 2)
                        for kc in range(NC):
                            ctx.rop("tensor", lambda e, P=P, fc=fc, kc=kc, W1=W1: e.matmul(
                                P.ap, W1.ap[:, kc, fc, :], h2t.ap[:, kc, :],
                                start=(kc == 0), stop=(kc == NC - 1)),
                                reads=[W1, h2t], writes=[P])
                        gfc = fh * 16 + fc
                        ctx.rop("scalar", lambda e, P=P, fc=fc, e_=e_, gfc=gfc: e.activation(
                            hid.ap[:, fc, :], P.ap, AF.Relu, bias=b1c.ap[:, e_, gfc:gfc + 1]),
                            reads=[P, b1c], writes=[hid])
                    for dc in range(NC):
                        P = ctx.bank(2 + dc % 2)
                        for fc in range(16):
                            ctx.rop("tensor", lambda e, P=P, dc=dc, fc=fc, W2=W2: e.matmul(
                                P.ap, W2.ap[:, fc, dc, :], hid.ap[:, fc, :],
                                start=(fc == 0), stop=(fc == 15)),
                                reads=[W2, hid], writes=[P])
                        ctx.rop("vector", lambda e, P=P, tcq=tcq: e.tensor_mul(
                            tmp.ap, P.ap, ewrep.ap[:, tcq * 512:(tcq + 1) * 512]),
                            reads=[P, ewrep], writes=[tmp])
                        ctx.rop("gpsimd", lambda e, dc=dc, tcq=tcq: e.tensor_add(
                            acc.ap[:, dc, tcq * 512:(tcq + 1) * 512],
                            acc.ap[:, dc, tcq * 512:(tcq + 1) * 512], tmp.ap),
                            reads=[acc, tmp], writes=[acc])
        for dc in range(NC):
            for q in range(4):
                ctx.dma("sync", x2t, x2t.ap, aps["x2_fm"], x2_v[:, dc, t0 + q * 512:t0 + (q + 1) * 512])
                ctx.rop("vector", lambda e, dc=dc, q=q: e.tensor_add(
                    ff.ap, acc.ap[:, dc, q * 512:(q + 1) * 512], x2t.ap),
                    reads=[acc, x2t], writes=[ff])
                ctx.dma("sync", aps["ffn0_fm"], ffn_v[:, dc, t0 + q * 512:t0 + (q + 1) * 512],
                        ff, ff.ap)


def launch_D2():
    W = lambda name: (lambda st: np.ascontiguousarray(st["weights"][name]))
    return Launch(
        "D2",
        inputs=[
            ("h2T", (D, T), BF16, lambda st: st["h2T"]),
            ("ewT", (8, T), BF16, lambda st: st["ewT"]),
            ("x2_fm", (D, T), F32, lambda st: st["x2_fm"]),
            ("W1", (8, D, 4096), F32, W("L0_ffn_W1")),
            ("b1", (8, 4096), F32, W("L0_ffn_b1")),
            ("W2", (8, 4096, D), F32, W("L0_ffn_W2")),
            ("b2", (8, D), F32, W("L0_ffn_b2")),
        ],
        outputs=[("ffn0_fm", (D, T), F32)],
        emit=lambda ctx, aps: emit_stage_D2(ctx, aps),
    )


def emit_stage_E(ctx, aps):
    x_v = aps["x_fm"].ap.rearrange("(c p) t -> p c t", p=128)
    s_v = aps["s0_fm"].ap.rearrange("(c p) t -> p c t", p=128)
    f_v = aps["ffn0_fm"].ap.rearrange("(c p) t -> p c t", p=128)
    x1_v = aps["x1_fm"].ap.rearrange("(c p) t -> p c t", p=128)
    xn1_v = aps["xn1_fm"].ap.rearrange("(c p) t -> p c t", p=128)
    onesb, onesf = make_ones(ctx)
    xt = ctx.sbuf("Ex_xt", [128, 512], F32)
    st_ = ctx.sbuf("Ex_st", [128, 512], F32)
    ft = ctx.sbuf("Ex_ft", [128, 512], F32)
    x1f = ctx.sbuf("Ex_x1f", [128, NC, 512], F32)
    x1b = ctx.sbuf("Ex_x1b", [128, NC, 512], BF16)
    murep = ctx.sbuf("Ex_mu", [128, 512], F32)
    rstdrep = ctx.sbuf("Ex_rs", [128, 512], F32)
    row4 = ctx.sbuf("Ex_row4", [1, 4, 512], F32)
    scr = ctx.sbuf("Ex_scr", [128, 512], BF16)
    xnt = ctx.sbuf("Ex_xnt", [128, 512], F32)
    for tc in range(T // 512):
        for kc in range(NC):
            ctx.dma("sync", xt, xt.ap, aps["x_fm"], x_v[:, kc, tc * 512:(tc + 1) * 512])
            ctx.dma("sync", st_, st_.ap, aps["s0_fm"], s_v[:, kc, tc * 512:(tc + 1) * 512])
            ctx.dma("sync", ft, ft.ap, aps["ffn0_fm"], f_v[:, kc, tc * 512:(tc + 1) * 512])
            ctx.rop("scalar", lambda e: e.activation(ft.ap, ft.ap, AF.Silu),
                    reads=[ft], writes=[ft])
            ctx.rop("vector", lambda e: e.tensor_mul(ft.ap, ft.ap, st_.ap),
                    reads=[ft, st_], writes=[ft])
            ctx.rop("vector", lambda e, kc=kc: e.tensor_add(x1f.ap[:, kc, :], xt.ap, ft.ap),
                    reads=[xt, ft], writes=[x1f])
            ctx.rop("vector", lambda e, kc=kc: e.tensor_copy(x1b.ap[:, kc, :], x1f.ap[:, kc, :]),
                    reads=[x1f], writes=[x1b])
            ctx.dma("sync", aps["x1_fm"], x1_v[:, kc, tc * 512:(tc + 1) * 512],
                    x1f, x1f.ap[:, kc, :])
        emit_ln_fm2(ctx, "E", lambda kc: (x1b, x1b.ap[:, kc, :]), onesb, onesf,
                    murep, rstdrep, row4, scr)
        for kc in range(NC):
            ctx.rop("vector", lambda e, kc=kc: e.tensor_sub(xnt.ap, x1f.ap[:, kc, :], murep.ap),
                    reads=[x1f, murep], writes=[xnt])
            ctx.rop("vector", lambda e: e.tensor_mul(xnt.ap, xnt.ap, rstdrep.ap),
                    reads=[xnt, rstdrep], writes=[xnt])
            ctx.dma("sync", aps["xn1_fm"], xn1_v[:, kc, tc * 512:(tc + 1) * 512],
                    xnt, xnt.ap)


def launch_E():
    return Launch(
        "E",
        inputs=[
            ("x_fm", (D, T), F32, lambda st: st["x_fm"]),
            ("s0_fm", (D, T), F32, lambda st: st["s0_fm"]),
            ("ffn0_fm", (D, T), F32, lambda st: st["ffn0_fm"]),
        ],
        outputs=[("x1_fm", (D, T), F32), ("xn1_fm", (D, T), F32)],
        emit=lambda ctx, aps: emit_stage_E(ctx, aps),
    )


def emit_stage_F(ctx, aps):
    qkvT = aps["qkvT1"]
    qkv_v = qkvT.ap.rearrange("(c p) t -> p c t", p=128)
    xn_v = aps["xn1_fm"].ap.rearrange("(c p) t -> p c t", p=128)
    wqT = load_wT_bf(ctx, "F_wq", aps["qkv_w"], 3 * D, D)
    onesb, onesf = make_ones(ctx)
    qw = ctx.sbuf("F_qw", [128, 4], F32)
    for i, nm in enumerate(("q_norm_w", "q_norm_b", "k_norm_w", "k_norm_b")):
        for hh in range(2):
            ctx.dma("sync", qw, qw.ap[hh * 64:(hh + 1) * 64, i:i + 1],
                    aps[nm], aps[nm].ap.rearrange("d -> d ()"),
                    allow_slow_non_contiguous=True)
    maskb = ctx.sbuf("F_maskb", [128, 2], BF16)
    maskf = ctx.sbuf("F_maskf", [128, 2], F32)
    mask2f = ctx.sbuf("F_m2f", [2, 128], F32)
    ctx.rop("vector", lambda e: e.memset(maskf.ap, 0.0), writes=[maskf])
    ctx.rop("vector", lambda e: e.memset(maskf.ap[0:64, 0:1], 1.0), writes=[maskf])
    ctx.rop("vector", lambda e: e.memset(maskf.ap[64:128, 1:2], 1.0), writes=[maskf])
    ctx.rop("vector", lambda e: e.tensor_copy(maskb.ap, maskf.ap), reads=[maskf], writes=[maskb])
    _Pm = ctx.bank(7)
    ctx.rop("tensor", lambda e: e.transpose(_Pm.ap[0:2, 0:128], maskf.ap, ctx.ident_f.ap),
            reads=[maskf, ctx.ident_f], writes=[_Pm])
    ctx.rop("vector", lambda e: e.tensor_copy(mask2f.ap, _Pm.ap[0:2, 0:128]),
            reads=[_Pm], writes=[mask2f])

    xnt = [ctx.sbuf(f"F_xnt{i}", [128, NC, 512], BF16) for i in range(2)]
    qf = ctx.sbuf("F_qf", [128, 512], F32)
    qb_ = ctx.sbuf("F_qb", [128, 512], BF16)
    sqb = ctx.sbuf("F_sqb", [128, 512], BF16)
    rows = ctx.sbuf("F_rows", [2, 6, 512], F32)
    murep = ctx.sbuf("F_mur", [128, 512], F32)
    rsrep = ctx.sbuf("F_rsr", [128, 512], F32)
    kn_all = ctx.sbuf("F_knall", [128, NC, L], BF16)
    kmean = ctx.sbuf("F_kmean", [128, 2], F32)
    outb = ctx.sbuf("F_outb", [128, 512], BF16)

    def ln_head(P, dst_rows, tc, is_k, kc):
        ctx.rop("vector", lambda e: e.tensor_copy(qf.ap, P.ap), reads=[P], writes=[qf])
        ctx.rop("vector", lambda e: e.tensor_copy(qb_.ap, qf.ap), reads=[qf], writes=[qb_])
        ctx.rop("scalar", lambda e: e.square(sqb.ap, qb_.ap), reads=[qb_], writes=[sqb])
        Pr = ctx.bank(6)
        ctx.rop("tensor", lambda e: e.matmul(Pr.ap[0:2, :], maskb.ap, qb_.ap,
                                             start=True, stop=True),
                reads=[maskb, qb_], writes=[Pr])
        ctx.rop("vector", lambda e: e.tensor_scalar_mul(rows.ap[0:2, 2, :], Pr.ap[0:2, :], 1.0 / 64),
                reads=[Pr], writes=[rows])
        ctx.rop("tensor", lambda e: e.matmul(Pr.ap[0:2, :], maskb.ap, sqb.ap,
                                             start=True, stop=True),
                reads=[maskb, sqb], writes=[Pr])
        ctx.rop("vector", lambda e: e.tensor_scalar_mul(rows.ap[0:2, 1, :], Pr.ap[0:2, :], 1.0 / 64),
                reads=[Pr], writes=[rows])
        ctx.rop("vector", lambda e: e.tensor_mul(rows.ap[0:2, 0, :], rows.ap[0:2, 2, :], rows.ap[0:2, 2, :]),
                reads=[rows], writes=[rows])
        ctx.rop("vector", lambda e: e.tensor_sub(rows.ap[0:2, 0, :], rows.ap[0:2, 1, :], rows.ap[0:2, 0, :]),
                reads=[rows], writes=[rows])
        ctx.rop("scalar", lambda e: e.activation(rows.ap[0:2, 0, :], rows.ap[0:2, 0, :], AF.Sqrt,
                                                 bias=ctx.constcol(1e-5)[0:2, :]),
                reads=[rows, ctx.cconst], writes=[rows])
        ctx.rop("vector", lambda e: e.reciprocal(rows.ap[0:2, 3, :], rows.ap[0:2, 0, :]),
                reads=[rows], writes=[rows])
        Pb = ctx.bank(7)
        ctx.rop("tensor", lambda e: e.matmul(Pb.ap, mask2f.ap, rows.ap[0:2, 2, :],
                                             start=True, stop=True),
                reads=[mask2f, rows], writes=[Pb])
        ctx.rop("vector", lambda e: e.tensor_copy(murep.ap, Pb.ap), reads=[Pb], writes=[murep])
        ctx.rop("tensor", lambda e: e.matmul(Pb.ap, mask2f.ap, rows.ap[0:2, 3, :],
                                             start=True, stop=True),
                reads=[mask2f, rows], writes=[Pb])
        ctx.rop("vector", lambda e: e.tensor_copy(rsrep.ap, Pb.ap), reads=[Pb], writes=[rsrep])
        ctx.rop("vector", lambda e: e.tensor_sub(qf.ap, qf.ap, murep.ap),
                reads=[qf, murep], writes=[qf])
        ctx.rop("vector", lambda e: e.tensor_mul(qf.ap, qf.ap, rsrep.ap),
                reads=[qf, rsrep], writes=[qf])
        wcol = 2 if is_k else 0
        ctx.rop("vector", lambda e, wcol=wcol: e.tensor_scalar(
            outb.ap, qf.ap, qw.ap[:, wcol:wcol + 1], qw.ap[:, wcol + 1:wcol + 2],
            ALU.mult, ALU.add), reads=[qf, qw], writes=[outb])

    for tc in range(T // 512):
        X = xnt[tc % 2]
        ctx.dma("gpsimd", X, X.ap, aps["xn1_fm"], xn_v[:, :, tc * 512:(tc + 1) * 512])
        for mc in range(24):
            P = ctx.bank(mc % 2)
            for kc in range(NC):
                ctx.rop("tensor", lambda e, P=P, X=X, mc=mc, kc=kc: e.matmul(
                    P.ap, wqT.ap[:, kc, mc, :], X.ap[:, kc, :],
                    start=(kc == 0), stop=(kc == NC - 1)),
                    reads=[wqT, X], writes=[P])
            if mc < 8:
                ln_head(P, None, tc, False, mc)
                ctx.dma("sync", qkvT, qkv_v[:, mc, tc * 512:(tc + 1) * 512], outb, outb.ap)
            elif mc < 16:
                ln_head(P, None, tc, True, mc)
                s = tc // 4
                j = tc % 4
                kc_ = mc - 8
                ctx.rop("vector", lambda e, j=j, kc_=kc_: e.tensor_copy(
                    kn_all.ap[:, kc_, j * 512:(j + 1) * 512], outb.ap),
                    reads=[outb], writes=[kn_all])
                if j == 3:
                    ctx.rop("vector", lambda e, kc_=kc_: e.reduce_sum(
                        kmean.ap[:, 0:1], kn_all.ap[:, kc_, :], AX.X),
                        reads=[kn_all], writes=[kmean])
                    ctx.rop("vector", lambda e: e.tensor_scalar_mul(
                        kmean.ap[:, 1:2], kmean.ap[:, 0:1], 1.0 / L),
                        reads=[kmean], writes=[kmean])
                    for jj in range(4):
                        ctx.rop("vector", lambda e, jj=jj, kc_=kc_: e.tensor_scalar(
                            outb.ap, kn_all.ap[:, kc_, jj * 512:(jj + 1) * 512],
                            kmean.ap[:, 1:2], None, ALU.subtract),
                            reads=[kn_all, kmean], writes=[outb])
                        ctx.dma("sync", qkvT,
                                qkv_v[:, mc, s * L + jj * 512: s * L + (jj + 1) * 512],
                                outb, outb.ap)
            else:
                ctx.rop("vector", lambda e, P=P: e.tensor_copy(outb.ap, P.ap),
                        reads=[P], writes=[outb])
                ctx.dma("sync", qkvT, qkv_v[:, mc, tc * 512:(tc + 1) * 512], outb, outb.ap)


def launch_F():
    W = lambda name: (lambda st: np.ascontiguousarray(st["weights"][name]))
    return Launch(
        "F",
        inputs=[
            ("xn1_fm", (D, T), F32, lambda st: st["xn1_fm"]),
            ("qkv_w", (3 * D, D), F32, W("L1_attn_qkv_w")),
            ("q_norm_w", (64,), F32, W("L1_attn_q_norm_w")),
            ("q_norm_b", (64,), F32, W("L1_attn_q_norm_b")),
            ("k_norm_w", (64,), F32, W("L1_attn_k_norm_w")),
            ("k_norm_b", (64,), F32, W("L1_attn_k_norm_b")),
        ],
        outputs=[("qkvT1", (3 * D, T), BF16)],
        emit=lambda ctx, aps: emit_stage_F(ctx, aps),
    )


def emit_stage_F2(ctx, aps):
    oT_dram = ctx.dram("F2_oT", [D, T], BF16)
    emit_attention(ctx, "F2", aps["qkvT1"], oT_dram)
    wpT = load_wT_bf(ctx, "F2_wp", aps["proj_w"], D, D)
    pb = load_param_cols(ctx, "F2_pb", aps["proj_b"])
    oT_v = oT_dram.ap.rearrange("(c p) t -> p c t", p=128)
    s1_v = aps["s1_fm"].ap.rearrange("(c p) t -> p c t", p=128)
    ot = [ctx.sbuf(f"F2_ot{i}", [128, NC, 512], BF16) for i in range(2)]
    s1t = ctx.sbuf("F2_s1t", [128, 512], F32)
    for tc in range(T // 512):
        O = ot[tc % 2]
        ctx.dma("sync", O, O.ap, oT_dram, oT_v[:, :, tc * 512:(tc + 1) * 512])
        for mc in range(NC):
            P = ctx.bank(mc % 2)
            for kc in range(NC):
                ctx.rop("tensor", lambda e, P=P, O=O, mc=mc, kc=kc: e.matmul(
                    P.ap, wpT.ap[:, kc, mc, :], O.ap[:, kc, :],
                    start=(kc == 0), stop=(kc == NC - 1)),
                    reads=[wpT, O], writes=[P])
            ctx.rop("vector", lambda e, P=P, mc=mc: e.tensor_scalar_add(
                s1t.ap, P.ap, pb.ap[:, mc:mc + 1]), reads=[P, pb], writes=[s1t])
            ctx.dma("sync", aps["s1_fm"], s1_v[:, mc, tc * 512:(tc + 1) * 512], s1t, s1t.ap)


def launch_F2():
    W = lambda name: (lambda st: np.ascontiguousarray(st["weights"][name]))
    return Launch(
        "F2",
        inputs=[
            ("qkvT1", (3 * D, T), BF16, lambda st: st["qkvT1"]),
            ("proj_w", (D, D), F32, W("L1_attn_proj_w")),
            ("proj_b", (D,), F32, W("L1_attn_proj_b")),
        ],
        outputs=[("s1_fm", (D, T), F32)],
        emit=lambda ctx, aps: emit_stage_F2(ctx, aps),
    )


def emit_fp4_quant(ctx, xb, x_ap, ob, o_ap, F, bufs):
    nb = F // 16
    bm, sc, iv, vv, cc, gg, sg = (bufs[k] for k in ("bm", "sc", "iv", "vv", "cc", "gg", "sg"))
    x3 = x_ap.rearrange("p (a b) -> p a b", b=16)
    ctx.rop("vector", lambda e: e.reduce_max(bm.ap[:, 0:nb], x3, AX.X,
                                             apply_absolute_value=True),
            reads=[xb], writes=[bm])
    ctx.rop("vector", lambda e: e.tensor_scalar(sc.ap[:, 0:nb], bm.ap[:, 0:nb],
                                                1.0 / 6.0, 1e-12, ALU.mult, ALU.max),
            reads=[bm], writes=[sc])
    ctx.rop("vector", lambda e: e.reciprocal(iv.ap[:, 0:nb], sc.ap[:, 0:nb]),
            reads=[sc], writes=[iv])
    ivrep = iv.ap[:, 0:nb].rearrange("p a -> p a ()").broadcast_to([128, nb, 16])
    screp = sc.ap[:, 0:nb].rearrange("p a -> p a ()").broadcast_to([128, nb, 16])
    vv3 = vv.ap[:, 0:F].rearrange("p (a b) -> p a b", b=16)
    ctx.rop("scalar", lambda e: e.activation(vv.ap[:, 0:F], x_ap, AF.Abs),
            reads=[xb], writes=[vv])
    ctx.rop("vector", lambda e: e.tensor_tensor(vv3, vv3, ivrep, ALU.mult),
            reads=[vv, iv], writes=[vv])
    ctx.rop("vector", lambda e: e.tensor_scalar(cc.ap[:, 0:F], vv.ap[:, 0:F], 0.25, None, ALU.is_gt),
            reads=[vv], writes=[cc])
    for thr in (0.75, 1.25, 1.75):
        ctx.rop("vector", lambda e, thr=thr: e.scalar_tensor_tensor(
            cc.ap[:, 0:F], vv.ap[:, 0:F], thr, cc.ap[:, 0:F], ALU.is_gt, ALU.add),
            reads=[vv, cc], writes=[cc])
    ctx.rop("vector", lambda e: e.tensor_scalar(gg.ap[:, 0:F], vv.ap[:, 0:F], 2.5, None, ALU.is_gt),
            reads=[vv], writes=[gg])
    ctx.rop("vector", lambda e: e.scalar_tensor_tensor(
        gg.ap[:, 0:F], vv.ap[:, 0:F], 3.5, gg.ap[:, 0:F], ALU.is_gt, ALU.add),
        reads=[vv, gg], writes=[gg])
    ctx.rop("vector", lambda e: e.tensor_scalar(sg.ap[:, 0:F], vv.ap[:, 0:F], 5.0, 2.0, ALU.is_gt, ALU.mult),
            reads=[vv], writes=[sg])
    ctx.rop("vector", lambda e: e.tensor_add(gg.ap[:, 0:F], gg.ap[:, 0:F], sg.ap[:, 0:F]),
            reads=[gg, sg], writes=[gg])
    ctx.rop("vector", lambda e: e.scalar_tensor_tensor(
        gg.ap[:, 0:F], cc.ap[:, 0:F], 0.5, gg.ap[:, 0:F], ALU.mult, ALU.add),
        reads=[cc, gg], writes=[gg])
    ctx.rop("scalar", lambda e: e.sign(sg.ap[:, 0:F], x_ap), reads=[xb], writes=[sg])
    ctx.rop("vector", lambda e: e.tensor_mul(gg.ap[:, 0:F], gg.ap[:, 0:F], sg.ap[:, 0:F]),
            reads=[gg, sg], writes=[gg])
    gg3 = gg.ap[:, 0:F].rearrange("p (a b) -> p a b", b=16)
    o3 = o_ap.rearrange("p (a b) -> p a b", b=16)
    ctx.rop("vector", lambda e: e.tensor_tensor(o3, gg3, screp, ALU.mult),
            reads=[gg, sc], writes=[ob])


def _quant_bufs(ctx, tag, F):
    return {
        "bm": ctx.sbuf(f"{tag}_bm", [128, F // 16], F32),
        "sc": ctx.sbuf(f"{tag}_sc", [128, F // 16], F32),
        "iv": ctx.sbuf(f"{tag}_iv", [128, F // 16], F32),
        "vv": ctx.sbuf(f"{tag}_vv", [128, F], F32),
        "cc": ctx.sbuf(f"{tag}_cc", [128, F], F32),
        "gg": ctx.sbuf(f"{tag}_gg", [128, F], F32),
        "sg": ctx.sbuf(f"{tag}_sg", [128, F], F32),
    }


def emit_stage_G0(ctx, aps):
    IC = 2816
    ICT = IC // 128
    bufs = _quant_bufs(ctx, "G0", IC)
    wrow = ctx.sbuf("G0_wrow", [128, IC], F32)
    wq = ctx.sbuf("G0_wq", [128, IC], BF16)
    ev = ctx.sbuf("G0_ev", [128, 128], BF16)
    for (nm, dst, nrt, ncol) in (("w_gate", "wgT", ICT, D), ("w_up", "wuT", ICT, D),
                                 ("w_down", "wdT", NC, IC)):
        for rt in range(nrt):
            ctx.dma("sync", wrow, wrow.ap[:, 0:ncol], aps[nm],
                    aps[nm].ap[rt * 128:(rt + 1) * 128, :])
            emit_fp4_quant(ctx, wrow, wrow.ap[:, 0:ncol], wq, wq.ap[:, 0:ncol], ncol, bufs)
            for kc in range(ncol // 128):
                P = ctx.bank(4 + kc % 2)
                pv = P.ap.bitcast(BF16)[:, 0:128]
                ctx.rop("tensor", lambda e, kc=kc, pv=pv: e.transpose(
                    pv, wq.ap[:, kc * 128:(kc + 1) * 128], ctx.ident_b.ap),
                    reads=[wq, ctx.ident_b], writes=[P])
                ctx.rop("vector", lambda e, pv=pv: e.tensor_copy(ev.ap, pv),
                        reads=[P], writes=[ev])
                ctx.dma("sync", aps[dst],
                        aps[dst].ap[kc * 128:(kc + 1) * 128, rt * 128:(rt + 1) * 128],
                        ev, ev.ap)


def launch_G0():
    W = lambda name: (lambda st: np.ascontiguousarray(st["weights"][name]))
    IC = 2816
    return Launch(
        "G0",
        inputs=[
            ("w_gate", (IC, D), F32, W("L1_ffn_w_gate")),
            ("w_up", (IC, D), F32, W("L1_ffn_w_up")),
            ("w_down", (D, IC), F32, W("L1_ffn_w_down")),
        ],
        outputs=[("wgT", (D, IC), BF16), ("wuT", (D, IC), BF16), ("wdT", (IC, D), BF16)],
        emit=lambda ctx, aps: emit_stage_G0(ctx, aps),
    )


def emit_stage_G1(ctx, aps):
    IC = 2816
    ICT = IC // 128
    xn_v = aps["xn1_fm"].ap.rearrange("(c p) t -> p c t", p=128)
    hT_v = aps["hT"].ap.rearrange("(c p) t -> p c t", p=128)
    bufs = _quant_bufs(ctx, "G1", D)
    xnt = ctx.sbuf("G1_xnt", [128, NC, 512], F32)
    xtok = ctx.sbuf("G1_xtok", [128, D], F32)
    xqt = ctx.sbuf("G1_xqt", [128, D], BF16)
    xq = ctx.sbuf("G1_xq", [128, NC, T], BF16)
    for tc in range(T // 512):
        ctx.dma("sync", xnt, xnt.ap, aps["xn1_fm"], xn_v[:, :, tc * 512:(tc + 1) * 512])
        for sub in range(4):
            for kc in range(NC):
                P = ctx.bank(kc % 2)
                ctx.rop("tensor", lambda e, P=P, kc=kc, sub=sub: e.transpose(
                    P.ap[:, 0:128], xnt.ap[:, kc, sub * 128:(sub + 1) * 128], ctx.ident_f.ap),
                    reads=[xnt, ctx.ident_f], writes=[P])
                ctx.rop("vector", lambda e, P=P, kc=kc: e.tensor_copy(
                    xtok.ap[:, kc * 128:(kc + 1) * 128], P.ap[:, 0:128]),
                    reads=[P], writes=[xtok])
            emit_fp4_quant(ctx, xtok, xtok.ap, xqt, xqt.ap, D, bufs)
            for kc in range(NC):
                P = ctx.bank(2 + kc % 2)
                pv = P.ap.bitcast(BF16)[:, 0:128]
                ctx.rop("tensor", lambda e, kc=kc, pv=pv: e.transpose(
                    pv, xqt.ap[:, kc * 128:(kc + 1) * 128], ctx.ident_b.ap),
                    reads=[xqt, ctx.ident_b], writes=[P])
                ctx.rop("vector", lambda e, pv=pv, kc=kc, tc=tc, sub=sub: e.tensor_copy(
                    xq.ap[:, kc, tc * 512 + sub * 128: tc * 512 + (sub + 1) * 128], pv),
                    reads=[P], writes=[xq])
    wg = ctx.sbuf("G1_wg", [128, NC, 128], BF16)
    wu = ctx.sbuf("G1_wu", [128, NC, 128], BF16)
    gs = ctx.sbuf("G1_gs", [128, 512], F32)
    hb = ctx.sbuf("G1_hb", [128, 512], F32)
    for mc in range(ICT):
        ctx.dma("gpsimd", wg, wg.ap, aps["wgT"],
                aps["wgT"].ap.rearrange("(c p) m -> p c m", p=128)[:, :, mc * 128:(mc + 1) * 128])
        ctx.dma("gpsimd", wu, wu.ap, aps["wuT"],
                aps["wuT"].ap.rearrange("(c p) m -> p c m", p=128)[:, :, mc * 128:(mc + 1) * 128])
        for tc in range(T // 512):
            Pg = ctx.bank(4 + tc % 2)
            Pu = ctx.bank(6 + tc % 2)
            for kc in range(NC):
                ctx.rop("tensor", lambda e, Pg=Pg, mc=mc, kc=kc, tc=tc: e.matmul(
                    Pg.ap, wg.ap[:, kc, :], xq.ap[:, kc, tc * 512:(tc + 1) * 512],
                    start=(kc == 0), stop=(kc == NC - 1)),
                    reads=[wg, xq], writes=[Pg])
            for kc in range(NC):
                ctx.rop("tensor", lambda e, Pu=Pu, mc=mc, kc=kc, tc=tc: e.matmul(
                    Pu.ap, wu.ap[:, kc, :], xq.ap[:, kc, tc * 512:(tc + 1) * 512],
                    start=(kc == 0), stop=(kc == NC - 1)),
                    reads=[wu, xq], writes=[Pu])
            ctx.rop("scalar", lambda e, Pg=Pg: e.activation(gs.ap, Pg.ap, AF.Silu),
                    reads=[Pg], writes=[gs])
            ctx.rop("vector", lambda e, Pu=Pu: e.tensor_mul(hb.ap, gs.ap, Pu.ap),
                    reads=[gs, Pu], writes=[hb])
            ctx.dma("sync", aps["hT"], hT_v[:, mc, tc * 512:(tc + 1) * 512], hb, hb.ap)


def launch_G1():
    IC = 2816
    return Launch(
        "G1",
        inputs=[
            ("xn1_fm", (D, T), F32, lambda st: st["xn1_fm"]),
            ("wgT", (D, IC), BF16, lambda st: st["wgT"]),
            ("wuT", (D, IC), BF16, lambda st: st["wuT"]),
        ],
        outputs=[("hT", (IC, T), F32)],
        emit=lambda ctx, aps: emit_stage_G1(ctx, aps),
    )


def emit_stage_G3(ctx, aps):
    IC = 2816
    ICT = IC // 128
    hT_v = aps["hT"].ap.rearrange("(c p) t -> p c t", p=128)
    g1_v = aps["g1_fm"].ap.rearrange("(c p) t -> p c t", p=128)
    bufs = _quant_bufs(ctx, "G3", IC)
    wd = ctx.sbuf("G3_wd", [128, ICT, NC, 128], BF16)
    for kc in range(ICT):
        ctx.dma("gpsimd", wd, wd.ap[:, kc, :, :].rearrange("p a b -> p (a b)"),
                aps["wdT"], aps["wdT"].ap[kc * 128:(kc + 1) * 128, :])
    ht = ctx.sbuf("G3_ht", [128, ICT, 128], F32)
    htok = ctx.sbuf("G3_htok", [128, IC], F32)
    hqt = ctx.sbuf("G3_hqt", [128, IC], BF16)
    hq = ctx.sbuf("G3_hq", [128, ICT, 512], BF16)
    g1t = ctx.sbuf("G3_g1t", [128, 512], F32)
    for tc in range(T // 512):
        for sub in range(4):
            s0_ = tc * 512 + sub * 128
            ctx.dma("sync", ht, ht.ap, aps["hT"], hT_v[:, :, s0_:s0_ + 128])
            for kc in range(ICT):
                P = ctx.bank(kc % 2)
                ctx.rop("tensor", lambda e, kc=kc, P=P: e.transpose(
                    P.ap[:, 0:128], ht.ap[:, kc, :], ctx.ident_f.ap),
                    reads=[ht, ctx.ident_f], writes=[P])
                ctx.rop("vector", lambda e, P=P, kc=kc: e.tensor_copy(
                    htok.ap[:, kc * 128:(kc + 1) * 128], P.ap[:, 0:128]),
                    reads=[P], writes=[htok])
            emit_fp4_quant(ctx, htok, htok.ap, hqt, hqt.ap, IC, bufs)
            for kc in range(ICT):
                P = ctx.bank(2 + kc % 2)
                pv = P.ap.bitcast(BF16)[:, 0:128]
                ctx.rop("tensor", lambda e, kc=kc, pv=pv: e.transpose(
                    pv, hqt.ap[:, kc * 128:(kc + 1) * 128], ctx.ident_b.ap),
                    reads=[hqt, ctx.ident_b], writes=[P])
                ctx.rop("vector", lambda e, pv=pv, kc=kc, sub=sub: e.tensor_copy(
                    hq.ap[:, kc, sub * 128:(sub + 1) * 128], pv),
                    reads=[P], writes=[hq])
        for mc in range(NC):
            P = ctx.bank(4 + mc % 2)
            for kc in range(ICT):
                ctx.rop("tensor", lambda e, P=P, mc=mc, kc=kc: e.matmul(
                    P.ap, wd.ap[:, kc, mc, :], hq.ap[:, kc, :],
                    start=(kc == 0), stop=(kc == ICT - 1)),
                    reads=[wd, hq], writes=[P])
            ctx.rop("vector", lambda e, P=P: e.tensor_copy(g1t.ap, P.ap),
                    reads=[P], writes=[g1t])
            ctx.dma("sync", aps["g1_fm"], g1_v[:, mc, tc * 512:(tc + 1) * 512], g1t, g1t.ap)


def launch_G3():
    IC = 2816
    return Launch(
        "G3",
        inputs=[
            ("hT", (IC, T), F32, lambda st: st["hT"]),
            ("wdT", (IC, D), BF16, lambda st: st["wdT"]),
        ],
        outputs=[("g1_fm", (D, T), F32)],
        emit=lambda ctx, aps: emit_stage_G3(ctx, aps),
    )


def emit_stage_H(ctx, aps):
    x1_v = aps["x1_fm"].ap.rearrange("(c p) t -> p c t", p=128)
    s1_v = aps["s1_fm"].ap.rearrange("(c p) t -> p c t", p=128)
    g1_v = aps["g1_fm"].ap.rearrange("(c p) t -> p c t", p=128)
    xt = ctx.sbuf("H_xt", [128, 512], F32)
    st_ = ctx.sbuf("H_st", [128, 512], F32)
    gt = ctx.sbuf("H_gt", [128, 512], F32)
    ofm = ctx.sbuf("H_ofm", [128, NC, 512], F32)
    otok = [ctx.sbuf(f"H_otok{i}", [128, D], F32) for i in range(2)]
    for tc in range(T // 512):
        for kc in range(NC):
            ctx.dma("sync", xt, xt.ap, aps["x1_fm"], x1_v[:, kc, tc * 512:(tc + 1) * 512])
            ctx.dma("sync", st_, st_.ap, aps["s1_fm"], s1_v[:, kc, tc * 512:(tc + 1) * 512])
            ctx.dma("sync", gt, gt.ap, aps["g1_fm"], g1_v[:, kc, tc * 512:(tc + 1) * 512])
            ctx.rop("scalar", lambda e: e.activation(gt.ap, gt.ap, AF.Silu),
                    reads=[gt], writes=[gt])
            ctx.rop("vector", lambda e: e.tensor_mul(gt.ap, gt.ap, st_.ap),
                    reads=[gt, st_], writes=[gt])
            ctx.rop("vector", lambda e, kc=kc: e.tensor_add(ofm.ap[:, kc, :], xt.ap, gt.ap),
                    reads=[xt, gt], writes=[ofm])
        for sub in range(4):
            O = otok[sub % 2]
            for kc in range(NC):
                P = ctx.bank(kc % 2)
                ctx.rop("tensor", lambda e, P=P, kc=kc, sub=sub: e.transpose(
                    P.ap[:, 0:128], ofm.ap[:, kc, sub * 128:(sub + 1) * 128], ctx.ident_f.ap),
                    reads=[ofm, ctx.ident_f], writes=[P])
                ctx.rop("vector", lambda e, P=P, kc=kc, O=O: e.tensor_copy(
                    O.ap[:, kc * 128:(kc + 1) * 128], P.ap[:, 0:128]),
                    reads=[P], writes=[O])
            r0 = tc * 512 + sub * 128
            ctx.dma("sync", aps["out"], aps["out"].ap[r0:r0 + 128, :], O, O.ap)


def launch_H():
    return Launch(
        "H",
        inputs=[
            ("x1_fm", (D, T), F32, lambda st: st["x1_fm"]),
            ("s1_fm", (D, T), F32, lambda st: st["s1_fm"]),
            ("g1_fm", (D, T), F32, lambda st: st["g1_fm"]),
        ],
        outputs=[("out", (T, D), F32)],
        emit=lambda ctx, aps: emit_stage_H(ctx, aps),
    )


def kernel_chain(**inputs):
    states = shard_inputs(inputs)
    for mk in (launch_A, launch_B, launch_C, launch_C2, launch_D1, launch_D2,
               launch_E, launch_F, launch_F2, launch_G0, launch_G1, launch_G3,
               launch_H):
        mk().run(states)
    out = np.stack([st["out"].reshape(NSEQ, L, D) for st in states])
    return np.ascontiguousarray(out.reshape(16, L, D))


def emit_fused(ctx, aps):
    IC = 2816
    inter = {}
    for nm, shape, dt in (
        ("x_fm", (D, T), F32), ("xn0_fm", (D, T), F32), ("s0_fm", (D, T), F32),
        ("qkvT", (3 * D, T), BF16), ("x2_fm", (D, T), F32),
        ("h2T", (D, T), BF16), ("ewT", (8, T), BF16), ("ffn0_fm", (D, T), F32),
        ("x1_fm", (D, T), F32), ("xn1_fm", (D, T), F32),
        ("qkvT1", (3 * D, T), BF16), ("s1_fm", (D, T), F32),
        ("wgT", (D, IC), BF16), ("wuT", (D, IC), BF16), ("wdT", (IC, D), BF16),
        ("hT", (IC, T), F32), ("g1_fm", (D, T), F32),
    ):
        inter[nm] = ctx.dram("i_" + nm, list(shape), dt, kind="Internal")
    env = dict(aps)
    env.update(inter)
    make_ones(ctx)
    ctx.bank(0)
    ctx.vfence(ctx.bank(0))

    stages = [
        lambda: emit_stage_A(ctx, env["xin"], env["x_fm"], env["xn0_fm"]),
        lambda: emit_stage_B(ctx, env),
        lambda: emit_stage_C(ctx, env, env["qkvT"]),
        lambda: emit_stage_C2(ctx, env),
        lambda: emit_stage_D1(ctx, env),
        lambda: emit_stage_D2(ctx, env),
        lambda: emit_stage_E(ctx, env),
        lambda: emit_stage_F(ctx, env),
        lambda: emit_stage_F2(ctx, env),
        lambda: emit_stage_G0(ctx, env),
        lambda: emit_stage_G1(ctx, env),
        lambda: emit_stage_G3(ctx, env),
        lambda: emit_stage_H(ctx, env),
    ]
    for st in stages:
        with ctx.scope():
            st()
        ctx.barrier()


def launch_FUSED():
    W = lambda name: (lambda st: np.ascontiguousarray(st["weights"][name]))
    IC = 2816
    return Launch(
        "FUSED",
        inputs=[
            ("xin", (T, D), F32, lambda st: st["xin"]),
            ("conv_w", (D, 4), F32, lambda st: np.ascontiguousarray(
                st["weights"]["L0_attn_conv_w"].reshape(D, 4))),
            ("conv_b", (D,), F32, W("L0_attn_conv_b")),
            ("W_r", (D, D), F32, W("L0_attn_W_r")),
            ("b_r", (D,), F32, W("L0_attn_b_r")),
            ("W_i", (D, D), F32, W("L0_attn_W_i")),
            ("b_i", (D,), F32, W("L0_attn_b_i")),
            ("log_a", (D,), F32, W("L0_attn_log_a")),
            ("W_out", (D, D), F32, W("L0_attn_W_out")),
            ("gamma", (D,), F32, W("L0_attn_gamma")),
            ("ln1_g", (D,), F32, W("L0_ffn_ln1_g")),
            ("ln1_b", (D,), F32, W("L0_ffn_ln1_b")),
            ("w_qkv", (D, 3 * D), F32, W("L0_ffn_w_qkv")),
            ("b_qkv", (3 * D,), F32, W("L0_ffn_b_qkv")),
            ("w_o", (D, D), F32, W("L0_ffn_w_o")),
            ("b_o", (D,), F32, W("L0_ffn_b_o")),
            ("ln2_g", (D,), F32, W("L0_ffn_ln2_g")),
            ("ln2_b", (D,), F32, W("L0_ffn_ln2_b")),
            ("gate_w", (D, 8), F32, W("L0_ffn_gate_w")),
            ("gate_b", (8,), F32, W("L0_ffn_gate_b")),
            ("W1", (8, D, 4096), F32, W("L0_ffn_W1")),
            ("b1", (8, 4096), F32, W("L0_ffn_b1")),
            ("W2", (8, 4096, D), F32, W("L0_ffn_W2")),
            ("b2", (8, D), F32, W("L0_ffn_b2")),
            ("qkv_w", (3 * D, D), F32, W("L1_attn_qkv_w")),
            ("q_norm_w", (64,), F32, W("L1_attn_q_norm_w")),
            ("q_norm_b", (64,), F32, W("L1_attn_q_norm_b")),
            ("k_norm_w", (64,), F32, W("L1_attn_k_norm_w")),
            ("k_norm_b", (64,), F32, W("L1_attn_k_norm_b")),
            ("proj_w", (D, D), F32, W("L1_attn_proj_w")),
            ("proj_b", (D,), F32, W("L1_attn_proj_b")),
            ("w_gate", (IC, D), F32, W("L1_ffn_w_gate")),
            ("w_up", (IC, D), F32, W("L1_ffn_w_up")),
            ("w_down", (D, IC), F32, W("L1_ffn_w_down")),
        ],
        outputs=[("out", (T, D), F32)],
        emit=lambda ctx, aps: emit_fused(ctx, aps),
    )


def kernel(**inputs):
    states = shard_inputs(inputs)
    launch_FUSED().run(states)
    out = np.stack([st["out"].reshape(NSEQ, L, D) for st in states])
    return np.ascontiguousarray(out.reshape(16, L, D))


# version 03
# speedup vs baseline: 1.0639x; 1.0575x over previous
import os
import numpy as np
import ml_dtypes
from contextlib import ExitStack, contextmanager

import concourse.bass as bass
import concourse.mybir as mybir
from concourse.bass_utils import run_bass_kernel_spmd

F32 = mybir.dt.float32
BF16 = mybir.dt.bfloat16
AF = mybir.ActivationFunctionType
ALU = mybir.AluOpType
AX = mybir.AxisListType
BF = ml_dtypes.bfloat16

ENGINES = ("sync", "tensor", "vector", "scalar", "gpsimd")

D = 1024
L = 2048
NSEQ = 2
T = NSEQ * L
NC = D // 128
NCORE = 8

IDENTF = np.eye(128, dtype=np.float32)
IDENTB = np.eye(128, dtype=BF)


class Dep:
    __slots__ = ("sem", "val", "eng")

    def __init__(self, sem, val, eng):
        self.sem = sem
        self.val = val
        self.eng = eng


class Buf:
    __slots__ = ("name", "ap", "last_write", "reads")

    def __init__(self, name, ap):
        self.name = name
        self.ap = ap
        self.last_write = None
        self.reads = []

    def __getitem__(self, idx):
        return self.ap.__getitem__(idx)


class Ctx:

    def __init__(self, name="prog"):
        self.nc = bass.Bass("TRN2", target_bir_lowering=False)
        self.stack = ExitStack()
        self.sems = {}
        self.semval = {}
        self.thunks = {e: [] for e in ENGINES}
        self.dma_sems = {}
        self._buf_names = set()
        self.constvals = []
        self._constmap = {}
        self.cconst = None
        self.ident_f = None
        self.ident_b = None
        self.waited = {e: {} for e in ENGINES}
        self._banks = {}
        self._dmaring = {}
        self._dmaslot = {}
        self.DMA_RING = 12

    def sem(self, name):
        if name not in self.sems:
            self.sems[name] = self.stack.enter_context(self.nc.semaphore(name))
            self.semval[name] = 0
        return self.sems[name]

    def sbuf(self, name, shape, dtype):
        assert name not in self._buf_names, name
        self._buf_names.add(name)
        cur = getattr(self, "_cur", None) or self.stack
        h = cur.enter_context(self.nc.sbuf_tensor(name, shape, dtype))
        return Buf(name, h[:])

    @contextmanager
    def scope(self):
        old = getattr(self, "_cur", None)
        self._cur = ExitStack()
        try:
            yield
        finally:
            self._cur.close()
            self._cur = old

    def psum(self, name, shape, dtype=F32):
        assert name not in self._buf_names, name
        self._buf_names.add(name)
        h = self.stack.enter_context(self.nc.psum_tensor(name, shape, dtype))
        return Buf(name, h[:])

    def vfence(self, buf):
        if not hasattr(self, "_junk"):
            cur = getattr(self, "_cur", None)
            self._cur = None
            self._junk = self.sbuf("junk_fence", [1, 8], F32)
            self._cur = cur
        return self.rop("vector", lambda e: e.memset(self._junk.ap, 0.0),
                        reads=[], writes=[buf, self._junk])

    def _psum_all(self):
        if not hasattr(self, "_psall"):
            self._psall = self.stack.enter_context(
                self.nc.psum_tensor("psum_all", [128, 8, 512], F32))
        return self._psall

    def bank(self, i):
        if i not in self._banks:
            assert 0 <= i < 8
            self._banks[i] = Buf(f"pbank{i}", self._psum_all()[:][:, i, :])
        return self._banks[i]

    def pspan(self, i, k):
        return self._psum_all()[:][:, i:i + k, :].rearrange("p a b -> p (a b)")

    def dram(self, name, shape, dtype, kind="Internal"):
        ap = self.nc.dram_tensor(name, shape, dtype, kind=kind).ap()
        return Buf(name, ap)

    def _wait(self, engine, sem_name, val):
        if val <= 0:
            return
        w = self.waited[engine]
        if w.get(sem_name, 0) >= val:
            return
        w[sem_name] = val
        self.sem(sem_name)
        sems = self.sems
        self.thunks[engine].append(lambda eng: eng.wait_ge(sems[sem_name], val))

    def _emit(self, engine, build, inc):
        name, n = inc
        self.sem(name)
        self.semval[name] += n
        tgt = self.semval[name]
        sems = self.sems

        def thunk(eng, build=build, name=name, n=n):
            build(eng).then_inc(sems[name], n)
        self.thunks[engine].append(thunk)
        return tgt

    def rop(self, engine, build, reads=(), writes=(), dma=False, sem=None):
        if sem is None:
            if dma:
                ring = self._dmaring.setdefault(engine, [0] * self.DMA_RING)
                slot = self._dmaslot.get(engine, 0)
                self._dmaslot[engine] = (slot + 1) % self.DMA_RING
                sem = f"d_{engine}_{slot}"
                if ring[slot] > 0:
                    self._wait(engine, sem, 16 * ring[slot])
                ring[slot] += 1
            else:
                sem = "c_" + engine
        if dma:
            self.dma_sems[sem] = engine
        n = 16 if dma else 1
        deps = []
        for b in reads:
            if b.last_write is not None:
                deps.append(b.last_write)
        for b in writes:
            if b.last_write is not None:
                deps.append(b.last_write)
            deps.extend(b.reads)
        for d in deps:
            if (d.eng == engine == "tensor" and not dma
                    and not d.sem.startswith("d_")):
                continue
            self._wait(engine, d.sem, d.val)
        tgt = self._emit(engine, build, (sem, n))
        dep = Dep(sem, tgt, engine)
        for b in reads:
            b.reads.append(dep)
        for b in writes:
            b.last_write = dep
            b.reads = []
        return dep

    def dma(self, engine, dst, dst_ap, src, src_ap, **kw):
        reads = [src] if src is not None else []
        writes = [dst] if dst is not None else []
        return self.rop(engine,
                        lambda e: e.dma_start(out=dst_ap, in_=src_ap, **kw),
                        reads=reads, writes=writes, dma=True)

    def constcol(self, v):
        v = float(v)
        if v not in self._constmap:
            self._constmap[v] = len(self.constvals)
            self.constvals.append(v)
            assert len(self.constvals) <= 16
        i = self._constmap[v]
        return self.cconst[:, i:i + 1]

    def barrier(self):
        for sem, eng in self.dma_sems.items():
            self._wait(eng, sem, self.semval[sem])
        self.sem("bar")
        for e in ENGINES:
            self.semval["bar"] += 1
            sems = self.sems
            self.thunks[e].append(lambda eng, s=sems: eng.sem_inc(s["bar"], 1))
        tgt = self.semval["bar"]
        for e in ENGINES:
            self._wait(e, "bar", tgt)

    def build(self):
        with self.stack:
            with self.nc.Block() as block:
                for e in ENGINES:
                    ths = self.thunks[e]
                    getattr(block, e)(lambda eng, ths=ths: [t(eng) for t in ths])
        return self.nc


class Launch:
    def __init__(self, name, inputs, outputs, emit):
        self.name = name
        self.inputs = inputs
        self.outputs = outputs
        self.emit = emit
        self._nc = None

    def build(self):
        if self._nc is not None:
            return self._nc
        ctx = Ctx(self.name)
        self._ctx = ctx
        aps = {}
        std = [("cconst", (128, 16), F32, None),
               ("identf", (128, 128), F32, None),
               ("identb", (128, 128), BF16, None)]
        for (n, shape, dt, _src) in std + list(self.inputs):
            aps[n] = ctx.dram(n, list(shape), dt, kind="ExternalInput")
        for (n, shape, dt) in self.outputs:
            aps[n] = ctx.dram(n, list(shape), dt, kind="ExternalOutput")
        ctx.cconst = ctx.sbuf("cconst_sb", [128, 16], F32)
        idf = ctx.sbuf("identf_sb", [128, 128], F32)
        idb = ctx.sbuf("identb_sb", [128, 128], BF16)
        ctx.ident_f = idf
        ctx.ident_b = idb
        ctx.dma("sync", ctx.cconst, ctx.cconst.ap, aps["cconst"], aps["cconst"].ap)
        ctx.dma("sync", idf, idf.ap, aps["identf"], aps["identf"].ap)
        ctx.dma("sync", idb, idb.ap, aps["identb"], aps["identb"].ap)
        self.emit(ctx, aps)
        ctx.barrier()
        self._nc = ctx.build()
        return self._nc

    def run(self, states, **kw):
        nc = self.build()
        cvals = np.zeros((128, 16), np.float32)
        for i, v in enumerate(self._ctx.constvals):
            cvals[:, i] = v
        in_maps = []
        for st in states:
            m = {"cconst": cvals, "identf": IDENTF, "identb": IDENTB}
            for (n, shape, dt, src) in self.inputs:
                a = src(st)
                assert tuple(a.shape) == tuple(shape), (self.name, n, a.shape, shape)
                m[n] = a
            in_maps.append(m)
        res = run_bass_kernel_spmd(nc, in_maps, core_ids=list(range(len(states))), **kw)
        for st, r in zip(states, res.results):
            for (n, shape, dt) in self.outputs:
                st[n] = r[n]
        return res


def emit_stage_A(ctx, xin, x_fm, xn0_fm):
    NT = T // 128
    xt = [ctx.sbuf(f"A_xt{i}", [128, D], F32) for i in range(2)]
    sq = [ctx.sbuf(f"A_sq{i}", [128, D], F32) for i in range(2)]
    xnt = [ctx.sbuf(f"A_xn{i}", [128, D], F32) for i in range(2)]
    st = [ctx.sbuf(f"A_st{i}", [128, 8], F32) for i in range(2)]
    xev = [ctx.sbuf(f"A_xe{i}", [128, D], F32) for i in range(2)]
    xnev = [ctx.sbuf(f"A_xne{i}", [128, D], F32) for i in range(2)]
    pss = [ctx.bank(0), ctx.bank(1)]

    x_fm_v = x_fm.ap.rearrange("(c p) t -> p c t", p=128)
    xn_fm_v = xn0_fm.ap.rearrange("(c p) t -> p c t", p=128)
    epsc = ctx.constcol(1e-5)

    for ti in range(NT):
        b = ti % 2
        X, SQ, XN, S = xt[b], sq[b], xnt[b], st[b]
        ctx.dma("sync", X, X.ap, xin, xin.ap[ti * 128:(ti + 1) * 128, :])
        ctx.rop("vector", lambda e, X=X, SQ=SQ: e.tensor_mul(SQ.ap, X.ap, X.ap),
                reads=[X], writes=[SQ])
        ctx.rop("vector", lambda e, X=X, S=S: e.reduce_sum(S[:, 0:1], X.ap, AX.X),
                reads=[X], writes=[S])
        ctx.rop("vector", lambda e, SQ=SQ, S=S: e.reduce_sum(S[:, 1:2], SQ.ap, AX.X),
                reads=[SQ], writes=[S])
        ctx.rop("vector", lambda e, S=S: e.tensor_scalar_mul(S[:, 2:3], S[:, 0:1], 1.0 / D),
                reads=[S], writes=[S])
        ctx.rop("vector", lambda e, S=S: e.tensor_scalar_mul(S[:, 3:4], S[:, 1:2], 1.0 / D),
                reads=[S], writes=[S])
        ctx.rop("vector", lambda e, S=S: e.tensor_mul(S[:, 4:5], S[:, 2:3], S[:, 2:3]),
                reads=[S], writes=[S])
        ctx.rop("vector", lambda e, S=S: e.tensor_sub(S[:, 5:6], S[:, 3:4], S[:, 4:5]),
                reads=[S], writes=[S])
        ctx.rop("scalar", lambda e, S=S, epsc=epsc: e.activation(S[:, 6:7], S[:, 5:6], AF.Sqrt, bias=epsc),
                reads=[S, ctx.cconst], writes=[S])
        ctx.rop("vector", lambda e, S=S: e.reciprocal(S[:, 7:8], S[:, 6:7]),
                reads=[S], writes=[S])
        ctx.rop("vector", lambda e, X=X, XN=XN, S=S: e.tensor_scalar(
            XN.ap, X.ap, S[:, 2:3], S[:, 7:8], ALU.subtract, ALU.mult),
            reads=[X, S], writes=[XN])
        for g in range(4):
            pb = (ti * 4 + g) % 2
            srcbuf = X if g < 2 else XN
            P = pss[pb]
            for k in range(4):
                dc = (g % 2) * 4 + k
                ctx.rop("tensor", lambda e, sb=srcbuf, dc=dc, P=P, k=k: e.transpose(
                    P[:, k * 128:(k + 1) * 128], sb.ap[:, dc * 128:(dc + 1) * 128],
                    ctx.ident_f.ap),
                    reads=[srcbuf, ctx.ident_f], writes=[P])
            dstbuf = xev[b] if g < 2 else xnev[b]
            off = (g % 2) * 512
            ctx.rop("vector", lambda e, db=dstbuf, off=off, P=P: e.tensor_copy(
                db.ap[:, off:off + 512], P.ap), reads=[P], writes=[dstbuf])
        ctx.dma("sync", x_fm, x_fm_v[:, :, ti * 128:(ti + 1) * 128],
                xev[b], xev[b].ap.rearrange("p (c t) -> p c t", t=128))
        ctx.dma("sync", xn0_fm, xn_fm_v[:, :, ti * 128:(ti + 1) * 128],
                xnev[b], xnev[b].ap.rearrange("p (c t) -> p c t", t=128))


def launch_A():
    return Launch(
        "A",
        inputs=[("xin", (T, D), F32, lambda st: st["xin"])],
        outputs=[("x_fm", (D, T), F32), ("xn0_fm", (D, T), F32)],
        emit=lambda ctx, aps: emit_stage_A(ctx, aps["xin"], aps["x_fm"], aps["xn0_fm"]),
    )


def shard_inputs(inputs):
    x = np.ascontiguousarray(inputs["L0_attn_x"])
    states = []
    for c in range(NCORE):
        xs = x[c * NSEQ:(c + 1) * NSEQ].reshape(T, D)
        states.append({"xin": np.ascontiguousarray(xs), "weights": inputs})
    return states





def load_param_cols(ctx, tag, src_buf, n=D):
    nc_ = n // 128
    stage = ctx.sbuf(f"{tag}_pst", [nc_, 128], F32)
    ps = ctx.bank(0)
    out = ctx.sbuf(f"{tag}_pc", [128, nc_], F32)
    ctx.dma("sync", stage, stage.ap,
            src_buf, src_buf.ap.rearrange("(c p) -> c p", p=128))
    ctx.rop("tensor", lambda e: e.transpose(ps.ap[:, 0:nc_], stage.ap, ctx.ident_f.ap[0:nc_, 0:nc_]),
            reads=[stage, ctx.ident_f], writes=[ps])
    ctx.rop("vector", lambda e: e.tensor_copy(out.ap, ps.ap[:, 0:nc_]), reads=[ps], writes=[out])
    return out


def load_wT_bf(ctx, tag, w_buf, M, K):
    mt, kt = M // 128, K // 128
    wT = ctx.sbuf(f"{tag}_wT", [128, kt, mt, 128], BF16)
    stg = [ctx.sbuf(f"{tag}_ws{i}", [128, 128], BF16) for i in range(2)]
    pss = [ctx.bank(0), ctx.bank(1)]
    i = 0
    for mc in range(mt):
        for kc in range(kt):
            b = i % 2
            i += 1
            S, P = stg[b], pss[b]
            pv = P.ap.bitcast(BF16)[:, 0:128]
            ctx.dma("gpsimd", S, S.ap,
                    w_buf, w_buf.ap[mc * 128:(mc + 1) * 128, kc * 128:(kc + 1) * 128])
            ctx.rop("tensor", lambda e, S=S, pv=pv: e.transpose(pv, S.ap, ctx.ident_b.ap),
                    reads=[S, ctx.ident_b], writes=[P])
            ctx.rop("vector", lambda e, pv=pv, kc=kc, mc=mc: e.tensor_copy(
                wT.ap[:, kc, mc, :], pv), reads=[P], writes=[wT])
    return wT


def make_ones(ctx):
    if getattr(ctx, "_onesbf", None) is not None:
        return ctx._onesbf
    cur = getattr(ctx, "_cur", None)
    ctx._cur = None
    onesb = ctx.sbuf("onesb", [128, 1], BF16)
    onesf = ctx.sbuf("onesf_row", [1, 128], F32)
    ctx._cur = cur
    ctx.rop("vector", lambda e: e.memset(onesb.ap, 1.0), writes=[onesb])
    ctx.rop("vector", lambda e: e.memset(onesf.ap, 1.0), writes=[onesf])
    ctx._onesbf = (onesb, onesf)
    return onesb, onesf


def emit_stage_B(ctx, aps):
    xn_fm = aps["xn0_fm"]
    s0_fm = aps["s0_fm"]
    wrT = load_wT_bf(ctx, "wr", aps["W_r"], D, D)
    wiT = load_wT_bf(ctx, "wi", aps["W_i"], D, D)
    woT = load_wT_bf(ctx, "wo", aps["W_out"], D, D)
    cw = ctx.sbuf("B_cw", [128, NC, 4], F32)
    for c in range(NC):
        ctx.dma("sync", cw, cw.ap[:, c, :], aps["conv_w"],
                aps["conv_w"].ap[c * 128:(c + 1) * 128, :])
    cb = load_param_cols(ctx, "B_cb", aps["conv_b"])
    brc = load_param_cols(ctx, "B_br", aps["b_r"])
    bic = load_param_cols(ctx, "B_bi", aps["b_i"])
    lac = load_param_cols(ctx, "B_la", aps["log_a"])
    gac = load_param_cols(ctx, "B_ga", aps["gamma"])
    onesb, onesf = make_ones(ctx)

    la8 = ctx.sbuf("B_la8", [128, NC], F32)
    ctx.rop("scalar", lambda e: e.activation(la8.ap, lac.ap, AF.Sigmoid),
            reads=[lac], writes=[la8])
    ctx.rop("vector", lambda e: e.tensor_scalar_max(la8.ap, la8.ap, 1e-38),
            reads=[la8], writes=[la8])
    ctx.rop("scalar", lambda e: e.activation(la8.ap, la8.ap, AF.Ln),
            reads=[la8], writes=[la8])
    ctx.rop("vector", lambda e: e.tensor_scalar_mul(la8.ap, la8.ap, 8.0),
            reads=[la8], writes=[la8])

    xcv = ctx.sbuf("B_xcv", [128, NC, T], BF16)
    xnp = [ctx.sbuf(f"B_xnp{i}", [128, L + 3], BF16) for i in range(2)]
    xn_fm_v = xn_fm.ap.rearrange("(c p) t -> p c t", p=128)
    for c in range(NC):
        for s in range(NSEQ):
            b = (c * NSEQ + s) % 2
            XP = xnp[b]
            ctx.rop("vector", lambda e, XP=XP: e.memset(XP.ap[:, 0:3], 0.0), writes=[XP])
            ctx.dma("gpsimd", XP, XP.ap[:, 3:L + 3],
                    xn_fm, xn_fm_v[:, c, s * L:(s + 1) * L])
            tacc = ctx.sbuf(f"B_cvt{c}_{s}", [128, L], F32) if c == 0 and s == 0 else tacc
            ctx.rop("vector", lambda e, XP=XP, c=c: e.tensor_scalar(
                tacc.ap, XP.ap[:, 0:L], cw.ap[:, c, 0:1], None, ALU.mult),
                reads=[XP, cw], writes=[tacc])
            for j in range(1, 4):
                ctx.rop("vector", lambda e, XP=XP, c=c, j=j: e.scalar_tensor_tensor(
                    tacc.ap, XP.ap[:, j:j + L], cw.ap[:, c, j:j + 1], tacc.ap,
                    ALU.mult, ALU.add), reads=[XP, cw, tacc], writes=[tacc])
            ctx.rop("vector", lambda e, c=c, s=s: e.tensor_scalar_add(
                xcv.ap[:, c, s * L:(s + 1) * L], tacc.ap, cb.ap[:, c:c + 1]),
                reads=[tacc, cb], writes=[xcv])
            if "dbg_xcv" in aps:
                ctx.dma("sync", aps["dbg_xcv"],
                        aps["dbg_xcv"].ap.rearrange("(c p) t -> p c t", p=128)[:, c, s * L:(s + 1) * L],
                        xcv, xcv.ap[:, c, s * L:(s + 1) * L])

    h_dram = aps.get("dbg_h") or ctx.dram("B_h_dram", [D, T], BF16)
    h_dram_v = h_dram.ap.rearrange("(c p) t -> p c t", p=128)
    hs = [ctx.sbuf("B_hs0", [128, L], BF16)]
    zr_banks = [ctx.bank(i) for i in range(4)]
    zi_banks = [ctx.bank(4 + i) for i in range(4)]
    rt = ctx.sbuf("B_rt", [128, L], F32)
    it = ctx.sbuf("B_it", [128, L], F32)
    at = ctx.sbuf("B_at", [128, L], F32)
    t1 = ctx.sbuf("B_t1", [128, L], F32)
    t2 = rt
    for oc in range(NC):
        for s in range(NSEQ):
            for (banks, wT) in ((zr_banks, wrT), (zi_banks, wiT)):
                for nb in range(4):
                    ps = banks[nb]
                    for kc in range(NC):
                        ctx.rop("tensor", lambda e, ps=ps, wT=wT, nb=nb, kc=kc, oc=oc, s=s: e.matmul(
                            ps.ap,
                            wT.ap[:, kc, oc, :],
                            xcv.ap[:, kc, s * L + nb * 512: s * L + (nb + 1) * 512],
                            start=(kc == 0), stop=(kc == NC - 1)),
                            reads=[wT, xcv], writes=[ps])
            for nb in range(4):
                ctx.rop("scalar", lambda e, oc=oc, nb=nb: e.activation(
                    rt.ap[:, nb * 512:(nb + 1) * 512], zr_banks[nb].ap, AF.Sigmoid,
                    bias=brc.ap[:, oc:oc + 1]),
                    reads=[zr_banks[nb], brc], writes=[rt])
                ctx.rop("scalar", lambda e, oc=oc, nb=nb: e.activation(
                    it.ap[:, nb * 512:(nb + 1) * 512], zi_banks[nb].ap, AF.Sigmoid,
                    bias=bic.ap[:, oc:oc + 1]),
                    reads=[zi_banks[nb], bic], writes=[it])
            ctx.rop("scalar", lambda e, oc=oc: e.activation(at.ap, rt.ap, AF.Exp,
                                                            scale=la8.ap[:, oc:oc + 1]),
                    reads=[rt, la8], writes=[at])
            ctx.rop("vector", lambda e: e.tensor_mul(t1.ap, at.ap, at.ap),
                    reads=[at], writes=[t1])
            ctx.rop("vector", lambda e: e.tensor_scalar(t1.ap, t1.ap, -1.0, 1.0, ALU.mult, ALU.add),
                    reads=[t1], writes=[t1])
            ctx.rop("vector", lambda e: e.tensor_scalar_max(t1.ap, t1.ap, 0.0),
                    reads=[t1], writes=[t1])
            ctx.rop("scalar", lambda e: e.activation(t1.ap, t1.ap, AF.Sqrt),
                    reads=[t1], writes=[t1])
            ctx.rop("vector", lambda e, oc=oc, s=s: e.tensor_copy(
                t2.ap, xcv.ap[:, oc, s * L:(s + 1) * L]), reads=[xcv], writes=[t2])
            ctx.rop("vector", lambda e: e.tensor_mul(t1.ap, t1.ap, it.ap),
                    reads=[t1, it], writes=[t1])
            ctx.rop("vector", lambda e: e.tensor_mul(t2.ap, t1.ap, t2.ap),
                    reads=[t1, t2], writes=[t2])
            hb = hs[0]
            ctx.rop("vector", lambda e, hb=hb: e.tensor_tensor_scan(
                hb.ap, at.ap, t2.ap, 0.0, ALU.mult, ALU.add),
                reads=[at, t2], writes=[hb])
            ctx.dma("sync", h_dram, h_dram_v[:, oc, s * L:(s + 1) * L], hb, hb.ap)

    if "dbg_pc" in aps:
        ctx.dma("sync", aps["dbg_pc"], aps["dbg_pc"].ap[:, 0:NC], gac, gac.ap)
        ctx.dma("sync", aps["dbg_pc"], aps["dbg_pc"].ap[:, 8:8 + NC], la8, la8.ap)
        ctx.dma("sync", aps["dbg_pc"], aps["dbg_pc"].ap[:, 16:16 + NC], brc, brc.ap)
        ctx.dma("sync", aps["dbg_pc"], aps["dbg_pc"].ap[:, 24:24 + NC], cb, cb.ap)
    if "dbg_wT" in aps:
        ctx.dma("sync", aps["dbg_wT"], aps["dbg_wT"].ap.rearrange("p a b q -> p (a b q)"),
                woT, woT.ap.rearrange("p a b q -> p (a b q)"))
    ht = [ctx.sbuf("B_ht0", [128, NC, 512], BF16)]
    o_f = ctx.sbuf("B_of", [128, NC, 512], F32)
    osq = ctx.sbuf("B_osq", [128, NC, 512], BF16)
    ps_o = [ctx.bank(0), ctx.bank(1)]
    ps_r = ctx.bank(2)
    ps_b = ctx.bank(3)
    rowA = ctx.sbuf("B_rowA", [1, 512], F32)
    rrep = ctx.sbuf("B_rrep", [128, 512], F32)
    sout = ctx.sbuf("B_sout", [128, 512], F32)
    s0_fm_v = s0_fm.ap.rearrange("(c p) t -> p c t", p=128)
    for tc in range(T // 512):
        H = ht[0]
        ctx.dma("sync", H, H.ap, h_dram, h_dram_v[:, :, tc * 512:(tc + 1) * 512])
        for mc in range(NC):
            P = ps_o[mc % 2]
            for kc in range(NC):
                ctx.rop("tensor", lambda e, P=P, H=H, mc=mc, kc=kc: e.matmul(
                    P.ap, woT.ap[:, kc, mc, :], H.ap[:, kc, :],
                    start=(kc == 0), stop=(kc == NC - 1)),
                    reads=[woT, H], writes=[P])
            ctx.rop("vector", lambda e, P=P, mc=mc: e.tensor_copy(o_f.ap[:, mc, :], P.ap),
                    reads=[P], writes=[o_f])
            ctx.vfence(P)
            if "dbg_o" in aps:
                ctx.dma("sync", aps["dbg_o"],
                        aps["dbg_o"].ap.rearrange("(c p) t -> p c t", p=128)[:, mc, tc * 512:(tc + 1) * 512],
                        o_f, o_f.ap[:, mc, :])
            ctx.rop("scalar", lambda e, mc=mc: e.square(osq.ap[:, mc, :], o_f.ap[:, mc, :]),
                    reads=[o_f], writes=[osq])
        for mc in range(NC):
            ctx.rop("tensor", lambda e, mc=mc: e.matmul(
                ps_r.ap[0:1, :], onesb.ap, osq.ap[:, mc, :],
                start=(mc == 0), stop=(mc == NC - 1)),
                reads=[onesb, osq], writes=[ps_r])
        ctx.rop("vector", lambda e: e.tensor_scalar_mul(rowA.ap, ps_r.ap[0:1, :], 1.0 / D),
                reads=[ps_r], writes=[rowA])
        ctx.rop("scalar", lambda e: e.activation(rowA.ap, rowA.ap, AF.Sqrt,
                                                 bias=ctx.constcol(1e-6)[0:1, :]),
                reads=[rowA, ctx.cconst], writes=[rowA])
        ctx.rop("vector", lambda e: e.reciprocal(rowA.ap, rowA.ap),
                reads=[rowA], writes=[rowA])
        ctx.rop("tensor", lambda e: e.matmul(ps_b.ap, onesf.ap, rowA.ap,
                                             start=True, stop=True),
                reads=[onesf, rowA], writes=[ps_b])
        ctx.rop("vector", lambda e: e.tensor_copy(rrep.ap, ps_b.ap),
                reads=[ps_b], writes=[rrep])
        for mc in range(NC):
            ctx.rop("vector", lambda e, mc=mc: e.scalar_tensor_tensor(
                sout.ap, o_f.ap[:, mc, :], gac.ap[:, mc:mc + 1], rrep.ap,
                ALU.mult, ALU.mult), reads=[o_f, gac, rrep], writes=[sout])
            ctx.dma("sync", s0_fm, s0_fm_v[:, mc, tc * 512:(tc + 1) * 512],
                    sout, sout.ap)


def launch_B(dbg=False):
    W = lambda name: (lambda st: np.ascontiguousarray(st["weights"][name]))
    extra = [("dbg_xcv", (D, T), BF16), ("dbg_h", (D, T), BF16), ("dbg_o", (D, T), F32),
             ("dbg_wT", (128, 8, 8, 128), BF16), ("dbg_pc", (128, 32), F32)] if dbg else []
    return Launch(
        "Bdbg" if dbg else "B",
        inputs=[
            ("xn0_fm", (D, T), F32, lambda st: st["xn0_fm"]),
            ("conv_w", (D, 4), F32, lambda st: np.ascontiguousarray(
                st["weights"]["L0_attn_conv_w"].reshape(D, 4))),
            ("conv_b", (D,), F32, W("L0_attn_conv_b")),
            ("W_r", (D, D), F32, W("L0_attn_W_r")),
            ("b_r", (D,), F32, W("L0_attn_b_r")),
            ("W_i", (D, D), F32, W("L0_attn_W_i")),
            ("b_i", (D,), F32, W("L0_attn_b_i")),
            ("log_a", (D,), F32, W("L0_attn_log_a")),
            ("W_out", (D, D), F32, W("L0_attn_W_out")),
            ("gamma", (D,), F32, W("L0_attn_gamma")),
        ],
        outputs=[("s0_fm", (D, T), F32)] + extra,
        emit=lambda ctx, aps: emit_stage_B(ctx, aps),
    )


def emit_ln_fm2(ctx, tag, get_tile, onesb, onesf, murep, rstdrep, row4, scr,
                banks=(6, 7), eps=1e-5):
    prow = ctx.bank(banks[0])
    pbc = ctx.bank(banks[1])
    for kc in range(NC):
        b, ap = get_tile(kc)
        ctx.rop("tensor", lambda e, ap=ap, kc=kc: e.matmul(
            prow.ap[0:1, :], onesb.ap, ap, start=(kc == 0), stop=(kc == NC - 1)),
            reads=[onesb, b], writes=[prow])
    ctx.rop("vector", lambda e: e.tensor_scalar_mul(row4.ap[0:1, 2, :], prow.ap[0:1, :], 1.0 / D),
            reads=[prow], writes=[row4])
    for kc in range(NC):
        b, ap = get_tile(kc)
        ctx.rop("scalar", lambda e, ap=ap: e.square(scr.ap, ap), reads=[b], writes=[scr])
        ctx.rop("tensor", lambda e, kc=kc: e.matmul(
            prow.ap[0:1, :], onesb.ap, scr.ap, start=(kc == 0), stop=(kc == NC - 1)),
            reads=[onesb, scr], writes=[prow])
    ctx.rop("vector", lambda e: e.tensor_scalar_mul(row4.ap[0:1, 1, :], prow.ap[0:1, :], 1.0 / D),
            reads=[prow], writes=[row4])
    ctx.rop("vector", lambda e: e.tensor_mul(row4.ap[0:1, 0, :], row4.ap[0:1, 2, :], row4.ap[0:1, 2, :]),
            reads=[row4], writes=[row4])
    ctx.rop("vector", lambda e: e.tensor_sub(row4.ap[0:1, 0, :], row4.ap[0:1, 1, :], row4.ap[0:1, 0, :]),
            reads=[row4], writes=[row4])
    ctx.rop("scalar", lambda e: e.activation(row4.ap[0:1, 0, :], row4.ap[0:1, 0, :], AF.Sqrt,
                                             bias=ctx.constcol(eps)[0:1, :]),
            reads=[row4, ctx.cconst], writes=[row4])
    ctx.rop("vector", lambda e: e.reciprocal(row4.ap[0:1, 3, :], row4.ap[0:1, 0, :]),
            reads=[row4], writes=[row4])
    ctx.rop("tensor", lambda e: e.matmul(pbc.ap, onesf.ap, row4.ap[0:1, 2, :],
                                         start=True, stop=True),
            reads=[onesf, row4], writes=[pbc])
    ctx.rop("vector", lambda e: e.tensor_copy(murep.ap, pbc.ap), reads=[pbc], writes=[murep])
    ctx.rop("tensor", lambda e: e.matmul(pbc.ap, onesf.ap, row4.ap[0:1, 3, :],
                                         start=True, stop=True),
            reads=[onesf, row4], writes=[pbc])
    ctx.rop("vector", lambda e: e.tensor_copy(rstdrep.ap, pbc.ap), reads=[pbc], writes=[rstdrep])


def emit_stage_C(ctx, aps, qkvT_dram):
    xn_fm = aps["xn0_fm"]
    xn_fm_v = xn_fm.ap.rearrange("(c p) t -> p c t", p=128)
    qkv_v = qkvT_dram.ap.rearrange("(c p) t -> p c t", p=128)
    wq = ctx.sbuf("C_wq", [128, NC, 24, 128], BF16)
    for kc in range(NC):
        ctx.dma("gpsimd", wq, wq.ap[:, kc, :, :].rearrange("p a b -> p (a b)"),
                aps["w_qkv"], aps["w_qkv"].ap[kc * 128:(kc + 1) * 128, :])
    bq = load_param_cols(ctx, "C_bq", aps["b_qkv"], n=3 * D)
    g1 = load_param_cols(ctx, "C_g1", aps["ln1_g"])
    b1 = load_param_cols(ctx, "C_b1", aps["ln1_b"])
    onesb, onesf = make_ones(ctx)

    xnt = [ctx.sbuf(f"C_xnt{i}", [128, NC, 512], BF16) for i in range(2)]
    h1t = [ctx.sbuf(f"C_h1t{i}", [128, NC, 512], BF16) for i in range(2)]
    murep = ctx.sbuf("C_mu", [128, 512], F32)
    rstdrep = ctx.sbuf("C_rs", [128, 512], F32)
    row4 = ctx.sbuf("C_row4", [1, 4, 512], F32)
    scr = ctx.sbuf("C_scr", [128, 512], BF16)
    tf = ctx.sbuf("C_tf", [128, 512], F32)
    qev = [ctx.sbuf(f"C_qev{i}", [128, 512], BF16) for i in range(2)]

    for tc in range(T // 512):
        X = xnt[tc % 2]
        H = h1t[tc % 2]
        ctx.dma("gpsimd", X, X.ap, xn_fm, xn_fm_v[:, :, tc * 512:(tc + 1) * 512])
        emit_ln_fm2(ctx, "C", lambda kc: (X, X.ap[:, kc, :]), onesb, onesf,
                    murep, rstdrep, row4, scr)
        for kc in range(NC):
            ctx.rop("vector", lambda e, X=X, kc=kc: e.tensor_sub(tf.ap, X.ap[:, kc, :], murep.ap),
                    reads=[X, murep], writes=[tf])
            ctx.rop("vector", lambda e: e.tensor_mul(tf.ap, tf.ap, rstdrep.ap),
                    reads=[tf, rstdrep], writes=[tf])
            ctx.rop("vector", lambda e, H=H, kc=kc: e.tensor_scalar(
                H.ap[:, kc, :], tf.ap, g1.ap[:, kc:kc + 1], b1.ap[:, kc:kc + 1],
                ALU.mult, ALU.add), reads=[tf, g1, b1], writes=[H])
        for mc in range(24):
            P = ctx.bank(mc % 2)
            for kc in range(NC):
                ctx.rop("tensor", lambda e, P=P, H=H, mc=mc, kc=kc: e.matmul(
                    P.ap, wq.ap[:, kc, mc, :], H.ap[:, kc, :],
                    start=(kc == 0), stop=(kc == NC - 1)),
                    reads=[wq, H], writes=[P])
            Q = qev[mc % 2]
            ctx.rop("vector", lambda e, P=P, Q=Q, mc=mc: e.tensor_scalar_add(
                Q.ap, P.ap, bq.ap[:, mc:mc + 1]),
                reads=[P, bq], writes=[Q])
            ctx.dma("sync", qkvT_dram, qkv_v[:, mc, tc * 512:(tc + 1) * 512], Q, Q.ap)


def launch_C():
    W = lambda name: (lambda st: np.ascontiguousarray(st["weights"][name]))
    return Launch(
        "C",
        inputs=[
            ("xn0_fm", (D, T), F32, lambda st: st["xn0_fm"]),
            ("ln1_g", (D,), F32, W("L0_ffn_ln1_g")),
            ("ln1_b", (D,), F32, W("L0_ffn_ln1_b")),
            ("w_qkv", (D, 3 * D), F32, W("L0_ffn_w_qkv")),
            ("b_qkv", (3 * D,), F32, W("L0_ffn_b_qkv")),
        ],
        outputs=[("qkvT", (3 * D, T), BF16)],
        emit=lambda ctx, aps: emit_stage_C(ctx, aps, aps["qkvT"]),
    )


def emit_attention(ctx, tag, qkvT, oT_dram, nheads=16, dk=64, scale=0.125):
    qv = qkvT.ap
    onesb, onesf = make_ones(ctx)
    qaug = ctx.sbuf(f"{tag}_qaug", [65, L], BF16)
    kaug = ctx.sbuf(f"{tag}_kaug", [65, L], BF16)
    vln = ctx.sbuf(f"{tag}_vln", [64, L], BF16)
    vtok = ctx.sbuf(f"{tag}_vtok", [128, 16, 64], BF16)
    expT = ctx.sbuf(f"{tag}_expT", [128, 16, 512], BF16)
    expT_s = [Buf(f"{tag}_expT{j}", expT.ap[:, j, :]) for j in range(16)]
    maxc = ctx.sbuf(f"{tag}_maxc", [128, 16], F32)
    mrow = ctx.sbuf(f"{tag}_mrow", [16, 128], BF16)
    maxb = ctx.sbuf(f"{tag}_maxb", [128, 16], BF16)
    sumr = ctx.sbuf(f"{tag}_sumr", [1, 512], F32)
    mrow_dram = ctx.dram(f"{tag}_mrowd", [16, 128], BF16)
    rrep = ctx.sbuf(f"{tag}_rrep", [64, 512], F32)
    oev = ctx.sbuf(f"{tag}_oev", [64, 512], BF16)
    oT_v = oT_dram.ap.rearrange("(c p) t -> p c t", p=128)

    for s in range(NSEQ):
        for h in range(nheads):
            t0 = s * L
            ctx.dma("sync", qaug, qaug.ap[0:64, :], qkvT, qv[h * dk:(h + 1) * dk, t0:t0 + L])
            ctx.dma("sync", kaug, kaug.ap[0:64, :], qkvT, qv[D + h * dk:D + (h + 1) * dk, t0:t0 + L])
            ctx.dma("sync", vln, vln.ap, qkvT, qv[2 * D + h * dk:2 * D + (h + 1) * dk, t0:t0 + L])
            ctx.rop("vector", lambda e: e.memset(kaug.ap[64:65, :], 1.0), writes=[kaug])
            for kb in range(16):
                P = ctx.bank(4 + kb % 2)
                pv = P.ap.bitcast(BF16)[:, 0:64]
                ctx.rop("tensor", lambda e, kb=kb, pv=pv: e.transpose(
                    pv, vln.ap[:, kb * 128:(kb + 1) * 128], ctx.ident_b.ap[0:64, 0:64]),
                    reads=[vln, ctx.ident_b], writes=[P])
                ctx.rop("vector", lambda e, kb=kb, pv=pv: e.tensor_copy(
                    vtok.ap[:, kb, :], pv), reads=[P], writes=[vtok])
            for qt in range(16):
                g0 = (qt % 2) * 4
                pb = [ctx.bank(g0 + i) for i in range(4)]
                for nb in range(4):
                    ctx.rop("tensor", lambda e, qt=qt, nb=nb, pb=pb: e.matmul(
                        pb[nb].ap, qaug.ap[0:64, qt * 128:(qt + 1) * 128],
                        kaug.ap[0:64, nb * 512:(nb + 1) * 512],
                        start=True, stop=True),
                        reads=[qaug, kaug], writes=[pb[nb]])
                ctx.rop("vector", lambda e, qt=qt, g0=g0: e.reduce_max(
                    maxc.ap[:, qt:qt + 1], ctx.pspan(g0, 4), AX.X),
                    reads=pb, writes=[maxc])
            ctx.rop("vector", lambda e: e.tensor_scalar_mul(maxb.ap, maxc.ap, -scale),
                    reads=[maxc], writes=[maxb])
            P = ctx.bank(4)
            pv = P.ap.bitcast(BF16)[0:16, 0:128]
            ctx.rop("tensor", lambda e, pv=pv: e.transpose(pv, maxb.ap, ctx.ident_b.ap),
                    reads=[maxb, ctx.ident_b], writes=[P])
            ctx.rop("vector", lambda e, pv=pv: e.tensor_copy(mrow.ap, pv),
                    reads=[P], writes=[mrow])
            ctx.dma("gpsimd", mrow_dram, mrow_dram.ap, mrow, mrow.ap)
            ctx.dma("gpsimd", qaug, qaug.ap[64:65, :],
                    mrow_dram, mrow_dram.ap.rearrange("j t -> () (j t)"))
            for nb in range(4):
                q0 = nb * 512
                for kb in range(16):
                    P = ctx.bank(kb % 2)
                    EB = expT_s[kb]
                    ctx.rop("tensor", lambda e, P=P, kb=kb, q0=q0: e.matmul(
                        P.ap, kaug.ap[:, kb * 128:(kb + 1) * 128],
                        qaug.ap[:, q0:q0 + 512],
                        start=True, stop=True),
                        reads=[kaug, qaug], writes=[P])
                    ctx.rop("scalar", lambda e, P=P, EB=EB: e.activation(
                        EB.ap, P.ap, AF.Exp, scale=scale),
                        reads=[P], writes=[EB])
                    ctx.rop("tensor", lambda e, kb=kb, EB=EB: e.matmul(
                        ctx.bank(3).ap[0:1, :], onesb.ap, EB.ap,
                        start=(kb == 0), stop=(kb == 15)),
                        reads=[onesb, EB], writes=[ctx.bank(3)])
                    ctx.rop("tensor", lambda e, kb=kb, EB=EB: e.matmul(
                        ctx.bank(2).ap[0:64, :], vtok.ap[:, kb, :], EB.ap,
                        start=(kb == 0), stop=(kb == 15)),
                        reads=[vtok, EB], writes=[ctx.bank(2)])
                ctx.rop("vector", lambda e: e.reciprocal(sumr.ap, ctx.bank(3).ap[0:1, :]),
                        reads=[ctx.bank(3)], writes=[sumr])
                ctx.rop("tensor", lambda e: e.matmul(
                    ctx.bank(5).ap[0:64, :], onesf.ap[:, 0:64], sumr.ap,
                    start=True, stop=True),
                    reads=[onesf, sumr], writes=[ctx.bank(5)])
                ctx.rop("vector", lambda e: e.tensor_copy(rrep.ap, ctx.bank(5).ap[0:64, :]),
                        reads=[ctx.bank(5)], writes=[rrep])
                ctx.rop("vector", lambda e: e.tensor_mul(oev.ap, ctx.bank(2).ap[0:64, :], rrep.ap),
                        reads=[ctx.bank(2), rrep], writes=[oev])
                ctx.dma("sync", oT_dram, oT_dram.ap[h * dk:(h + 1) * dk, t0 + q0:t0 + q0 + 512],
                        oev, oev.ap)


def emit_stage_C2(ctx, aps):
    qkvT = aps["qkvT"]
    x2_fm = aps["x2_fm"]
    oT_dram = ctx.dram("C2_oT", [D, T], BF16)
    emit_attention(ctx, "C2", qkvT, oT_dram)
    wo = ctx.sbuf("C2_wo", [128, NC, NC, 128], BF16)
    for kc in range(NC):
        ctx.dma("gpsimd", wo, wo.ap[:, kc, :, :].rearrange("p a b -> p (a b)"),
                aps["w_o"], aps["w_o"].ap[kc * 128:(kc + 1) * 128, :])
    bo = load_param_cols(ctx, "C2_bo", aps["b_o"])
    oT_v = oT_dram.ap.rearrange("(c p) t -> p c t", p=128)
    xn_v = aps["xn0_fm"].ap.rearrange("(c p) t -> p c t", p=128)
    x2_v = x2_fm.ap.rearrange("(c p) t -> p c t", p=128)
    ot = [ctx.sbuf(f"C2_ot{i}", [128, NC, 512], BF16) for i in range(2)]
    xnt = [ctx.sbuf(f"C2_xnt{i}", [128, 512], F32) for i in range(2)]
    x2t = [ctx.sbuf(f"C2_x2t{i}", [128, 512], F32) for i in range(2)]
    for tc in range(T // 512):
        O = ot[tc % 2]
        ctx.dma("sync", O, O.ap, oT_dram, oT_v[:, :, tc * 512:(tc + 1) * 512])
        for mc in range(NC):
            P = ctx.bank(6 + mc % 2)
            for kc in range(NC):
                ctx.rop("tensor", lambda e, P=P, O=O, mc=mc, kc=kc: e.matmul(
                    P.ap, wo.ap[:, kc, mc, :], O.ap[:, kc, :],
                    start=(kc == 0), stop=(kc == NC - 1)),
                    reads=[wo, O], writes=[P])
            XN = xnt[mc % 2]
            X2 = x2t[mc % 2]
            ctx.dma("gpsimd", XN, XN.ap, aps["xn0_fm"], xn_v[:, mc, tc * 512:(tc + 1) * 512])
            ctx.rop("vector", lambda e, P=P, XN=XN, X2=X2, mc=mc: e.scalar_tensor_tensor(
                X2.ap, P.ap, bo.ap[:, mc:mc + 1], XN.ap, ALU.add, ALU.add),
                reads=[P, bo, XN], writes=[X2])
            ctx.dma("sync", x2_fm, x2_v[:, mc, tc * 512:(tc + 1) * 512], X2, X2.ap)


def launch_C2():
    W = lambda name: (lambda st: np.ascontiguousarray(st["weights"][name]))
    return Launch(
        "C2",
        inputs=[
            ("qkvT", (3 * D, T), BF16, lambda st: st["qkvT"]),
            ("xn0_fm", (D, T), F32, lambda st: st["xn0_fm"]),
            ("w_o", (D, D), F32, W("L0_ffn_w_o")),
            ("b_o", (D,), F32, W("L0_ffn_b_o")),
        ],
        outputs=[("x2_fm", (D, T), F32)],
        emit=lambda ctx, aps: emit_stage_C2(ctx, aps),
    )


def emit_stage_D1(ctx, aps):
    x2 = aps["x2_fm"]
    x2_v = x2.ap.rearrange("(c p) t -> p c t", p=128)
    h2d = aps["h2T"]
    h2_v = h2d.ap.rearrange("(c p) t -> p c t", p=128)
    ewT = aps["ewT"]
    g2c = load_param_cols(ctx, "D_g2", aps["ln2_g"])
    b2c = load_param_cols(ctx, "D_b2", aps["ln2_b"])
    onesb, onesf = make_ones(ctx)
    gw = ctx.sbuf("D_gw", [128, NC, 8], F32)
    for kc in range(NC):
        ctx.dma("sync", gw, gw.ap[:, kc, :], aps["gate_w"],
                aps["gate_w"].ap[kc * 128:(kc + 1) * 128, :])
    gb_st = ctx.sbuf("D_gbst", [1, 8], F32)
    gb = ctx.sbuf("D_gb", [8, 1], F32)
    ctx.dma("sync", gb_st, gb_st.ap, aps["gate_b"], aps["gate_b"].ap.rearrange("e -> () e"))
    Pg = ctx.bank(7)
    ctx.rop("tensor", lambda e: e.transpose(Pg.ap[0:8, 0:1], gb_st.ap, ctx.ident_f.ap[0:1, 0:1]),
            reads=[gb_st, ctx.ident_f], writes=[Pg])
    ctx.rop("vector", lambda e: e.tensor_copy(gb.ap, Pg.ap[0:8, 0:1]), reads=[Pg], writes=[gb])

    xt = [ctx.sbuf(f"D_xt{i}", [128, NC, 512], BF16) for i in range(2)]
    ht = [ctx.sbuf(f"D_ht{i}", [128, NC, 512], BF16) for i in range(2)]
    hf = ctx.sbuf("D_hf", [128, NC, 512], F32)
    murep = ctx.sbuf("D_mu", [128, 512], F32)
    rstdrep = ctx.sbuf("D_rs", [128, 512], F32)
    row4 = ctx.sbuf("D_row4", [1, 4, 512], F32)
    scr = ctx.sbuf("D_scr", [128, 512], BF16)
    tf = ctx.sbuf("D_tf", [128, 512], F32)
    lg = ctx.sbuf("D_lg", [8, 512], F32)
    ltok = ctx.sbuf("D_ltok", [128, 8], F32)
    cols = ctx.sbuf("D_cols", [128, 8], F32)
    ewtok = ctx.sbuf("D_ewtok", [128, 8], BF16)
    ewev = ctx.sbuf("D_ewev", [8, 128], BF16)
    lmask = ctx.sbuf("D_lmask", [128, 8], F32)
    ewf = ctx.sbuf("D_ewf", [128, 8], F32)

    for tc in range(T // 512):
        X = xt[tc % 2]
        H = ht[tc % 2]
        ctx.dma("gpsimd", X, X.ap, x2, x2_v[:, :, tc * 512:(tc + 1) * 512])
        emit_ln_fm2(ctx, "D", lambda kc: (X, X.ap[:, kc, :]), onesb, onesf,
                    murep, rstdrep, row4, scr)
        for kc in range(NC):
            ctx.rop("vector", lambda e, X=X, kc=kc: e.tensor_sub(tf.ap, X.ap[:, kc, :], murep.ap),
                    reads=[X, murep], writes=[tf])
            ctx.rop("vector", lambda e: e.tensor_mul(tf.ap, tf.ap, rstdrep.ap),
                    reads=[tf, rstdrep], writes=[tf])
            ctx.rop("vector", lambda e, kc=kc: e.tensor_scalar(
                hf.ap[:, kc, :], tf.ap, g2c.ap[:, kc:kc + 1], b2c.ap[:, kc:kc + 1],
                ALU.mult, ALU.add), reads=[tf, g2c, b2c], writes=[hf])
            ctx.rop("vector", lambda e, H=H, kc=kc: e.tensor_copy(
                H.ap[:, kc, :], hf.ap[:, kc, :]), reads=[hf], writes=[H])
        ctx.dma("sync", h2d, h2_v[:, :, tc * 512:(tc + 1) * 512], H, H.ap)
        Pl = ctx.bank(6)
        for kc in range(NC):
            ctx.rop("tensor", lambda e, kc=kc: e.matmul(
                Pl.ap[0:8, :], gw.ap[:, kc, :], hf.ap[:, kc, :],
                start=(kc == 0), stop=(kc == NC - 1)),
                reads=[gw, hf], writes=[Pl])
        ctx.rop("vector", lambda e: e.tensor_scalar_add(lg.ap, Pl.ap[0:8, :], gb.ap),
                reads=[Pl, gb], writes=[lg])
        for j in range(4):
            Pt = ctx.bank(7)
            ctx.rop("tensor", lambda e, j=j: e.transpose(
                Pt.ap[:, 0:8], lg.ap[:, j * 128:(j + 1) * 128], ctx.ident_f.ap[0:8, 0:8]),
                reads=[lg, ctx.ident_f], writes=[Pt])
            ctx.rop("vector", lambda e: e.tensor_copy(ltok.ap, Pt.ap[:, 0:8]),
                    reads=[Pt], writes=[ltok])
            ctx.rop("vector", lambda e: e.reduce_max(cols.ap[:, 0:1], ltok.ap, AX.X),
                    reads=[ltok], writes=[cols])
            ctx.rop("vector", lambda e: e.tensor_scalar(
                lmask.ap, ltok.ap, cols.ap[:, 0:1], None, ALU.is_equal),
                reads=[ltok, cols], writes=[lmask])
            ctx.rop("vector", lambda e: e.scalar_tensor_tensor(
                ewf.ap, lmask.ap, -1e38, ltok.ap, ALU.mult, ALU.add),
                reads=[lmask, ltok], writes=[ewf])
            ctx.rop("vector", lambda e: e.reduce_max(cols.ap[:, 1:2], ewf.ap, AX.X),
                    reads=[ewf], writes=[cols])
            ctx.rop("vector", lambda e: e.tensor_sub(cols.ap[:, 2:3], cols.ap[:, 1:2], cols.ap[:, 0:1]),
                    reads=[cols], writes=[cols])
            ctx.rop("scalar", lambda e: e.activation(cols.ap[:, 3:4], cols.ap[:, 2:3], AF.Exp),
                    reads=[cols], writes=[cols])
            ctx.rop("vector", lambda e: e.tensor_scalar_add(cols.ap[:, 4:5], cols.ap[:, 3:4], 1.0),
                    reads=[cols], writes=[cols])
            ctx.rop("vector", lambda e: e.reciprocal(cols.ap[:, 5:6], cols.ap[:, 4:5]),
                    reads=[cols], writes=[cols])
            ctx.rop("vector", lambda e: e.tensor_mul(cols.ap[:, 6:7], cols.ap[:, 3:4], cols.ap[:, 5:6]),
                    reads=[cols], writes=[cols])
            ctx.rop("vector", lambda e: e.tensor_scalar(
                cols.ap[:, 7:8], ltok.ap[:, 0:1], 0.0, 0.0, ALU.mult, ALU.mult),
                reads=[ltok], writes=[cols])
            ctx.rop("vector", lambda e: e.tensor_scalar(
                ewf.ap, ewf.ap, cols.ap[:, 1:2], None, ALU.is_equal),
                reads=[ewf, cols], writes=[ewf])
            ctx.rop("vector", lambda e: e.tensor_scalar(
                ewf.ap, ewf.ap, cols.ap[:, 6:7], None, ALU.mult),
                reads=[ewf, cols], writes=[ewf])
            ctx.rop("vector", lambda e: e.scalar_tensor_tensor(
                ewf.ap, lmask.ap, cols.ap[:, 5:6], ewf.ap, ALU.mult, ALU.add),
                reads=[lmask, cols, ewf], writes=[ewf])
            ctx.rop("vector", lambda e: e.tensor_copy(ewtok.ap, ewf.ap),
                    reads=[ewf], writes=[ewtok])
            Pe = ctx.bank(5)
            pv = Pe.ap.bitcast(BF16)[0:8, 0:128]
            ctx.rop("tensor", lambda e, pv=pv: e.transpose(pv, ewtok.ap, ctx.ident_b.ap),
                    reads=[ewtok, ctx.ident_b], writes=[Pe])
            ctx.rop("vector", lambda e, pv=pv: e.tensor_copy(ewev.ap, pv),
                    reads=[Pe], writes=[ewev])
            ctx.dma("sync", ewT, ewT.ap[:, tc * 512 + j * 128: tc * 512 + (j + 1) * 128],
                    ewev, ewev.ap)


def launch_D1():
    W = lambda name: (lambda st: np.ascontiguousarray(st["weights"][name]))
    return Launch(
        "D1",
        inputs=[
            ("x2_fm", (D, T), F32, lambda st: st["x2_fm"]),
            ("ln2_g", (D,), F32, W("L0_ffn_ln2_g")),
            ("ln2_b", (D,), F32, W("L0_ffn_ln2_b")),
            ("gate_w", (D, 8), F32, W("L0_ffn_gate_w")),
            ("gate_b", (8,), F32, W("L0_ffn_gate_b")),
        ],
        outputs=[("h2T", (D, T), BF16), ("ewT", (8, T), BF16)],
        emit=lambda ctx, aps: emit_stage_D1(ctx, aps),
    )


def emit_stage_D2(ctx, aps):
    E = 8
    DFF = 4096
    h2_v = aps["h2T"].ap.rearrange("(c p) t -> p c t", p=128)
    ffn_v = aps["ffn0_fm"].ap.rearrange("(c p) t -> p c t", p=128)
    x2_v = aps["x2_fm"].ap.rearrange("(c p) t -> p c t", p=128)
    onesb, onesf = make_ones(ctx)
    onesb_row = ctx.sbuf("E_onesrow", [1, 128], BF16)
    ctx.rop("vector", lambda e: e.memset(onesb_row.ap, 1.0), writes=[onesb_row])

    b1c = ctx.sbuf("E_b1c", [128, E, 32], F32)
    b1st = ctx.sbuf("E_b1st", [32, 128], F32)
    for e_ in range(E):
        ctx.dma("sync", b1st, b1st.ap, aps["b1"],
                aps["b1"].ap[e_, :].rearrange("(a b) -> a b", b=128))
        Pb = ctx.bank(5)
        ctx.rop("tensor", lambda en, e_=e_: en.transpose(
            Pb.ap[:, 0:32], b1st.ap, ctx.ident_f.ap[0:32, 0:32]),
            reads=[b1st, ctx.ident_f], writes=[Pb])
        ctx.rop("vector", lambda en, e_=e_: en.tensor_copy(b1c.ap[:, e_, :], Pb.ap[:, 0:32]),
                reads=[Pb], writes=[b1c])
    b2w = ctx.sbuf("E_b2w", [8, D], BF16)
    ctx.dma("gpsimd", b2w, b2w.ap, aps["b2"], aps["b2"].ap)
    ewTs = ctx.sbuf("E_ewT", [8, 2048], BF16)

    w1b = [ctx.sbuf(f"E_w1b{i}", [128, NC, 16, 128], BF16) for i in range(2)]
    w2b = [ctx.sbuf(f"E_w2b{i}", [128, 16, NC, 128], BF16) for i in range(2)]
    h2t = ctx.sbuf("E_h2t", [128, NC, 512], BF16)
    hid = ctx.sbuf("E_hid", [128, 16, 512], BF16)
    hid_s = [Buf(f"E_hid{j}", hid.ap[:, j, :]) for j in range(16)]
    acc = ctx.sbuf("E_acc", [128, NC, 2048], BF16)
    ewrep = ctx.sbuf("E_ewrep", [128, 2048], BF16)
    ewrow = ctx.sbuf("E_ewrow", [1, 2048], BF16)
    tmp = ctx.sbuf("E_tmp", [128, 512], BF16)
    x2t = ctx.sbuf("E_x2t", [128, 512], F32)
    ff = ctx.sbuf("E_ff", [128, 512], F32)

    HALF = 2048
    small = bool(int(os.environ.get("MK_D2_SMALL", "0")))
    for half in range(1 if small else 2):
        t0 = half * HALF
        ctx.dma("sync", ewTs, ewTs.ap, aps["ewT"], aps["ewT"].ap[:, t0:t0 + HALF])
        for dc in range(NC):
            for q in range(4):
                Pq = ctx.bank(4 + (q % 2))
                ctx.rop("tensor", lambda e, dc=dc, q=q, Pq=Pq: e.matmul(
                    Pq.ap, b2w.ap[:, dc * 128:(dc + 1) * 128],
                    ewTs.ap[:, q * 512:(q + 1) * 512],
                    start=True, stop=True),
                    reads=[b2w, ewTs], writes=[Pq])
                ctx.rop("vector", lambda e, dc=dc, q=q, Pq=Pq: e.tensor_copy(
                    acc.ap[:, dc, q * 512:(q + 1) * 512], Pq.ap),
                    reads=[Pq], writes=[acc])
        for e_ in range(2 if small else E):
            ctx.dma("sync", ewrow, ewrow.ap, aps["ewT"],
                    aps["ewT"].ap[e_:e_ + 1, t0:t0 + HALF])
            for q in range(4):
                Pq = ctx.bank(4 + (q % 2))
                ctx.rop("tensor", lambda en, e_=e_, q=q, Pq=Pq: en.matmul(
                    Pq.ap, onesb_row.ap, ewrow.ap[:, q * 512:(q + 1) * 512],
                    start=True, stop=True),
                    reads=[onesb_row, ewrow], writes=[Pq])
                ctx.rop("vector", lambda en, q=q, Pq=Pq: en.tensor_copy(
                    ewrep.ap[:, q * 512:(q + 1) * 512], Pq.ap),
                    reads=[Pq], writes=[ewrep])
            for fh in range(2):
                W1 = w1b[fh % 2]
                W2 = w2b[fh % 2]
                f0 = fh * 2048
                for kc in range(NC):
                    ctx.dma("gpsimd", W1, W1.ap[:, kc, :, :].rearrange("p a b -> p (a b)"),
                            aps["W1"], aps["W1"].ap[e_, kc * 128:(kc + 1) * 128, f0:f0 + 2048])
                for fc in range(16):
                    ctx.dma("gpsimd", W2, W2.ap[:, fc, :, :].rearrange("p a b -> p (a b)"),
                            aps["W2"], aps["W2"].ap[e_, f0 + fc * 128: f0 + (fc + 1) * 128, :])
                for tcq in range(1 if small else 4):
                    q0 = t0 + tcq * 512
                    ctx.dma("gpsimd", h2t, h2t.ap, aps["h2T"], h2_v[:, :, q0:q0 + 512])
                    for fc in range(16):
                        P = ctx.bank(fc % 2)
                        for kc in range(NC):
                            ctx.rop("tensor", lambda e, P=P, fc=fc, kc=kc, W1=W1: e.matmul(
                                P.ap, W1.ap[:, kc, fc, :], h2t.ap[:, kc, :],
                                start=(kc == 0), stop=(kc == NC - 1)),
                                reads=[W1, h2t], writes=[P])
                        gfc = fh * 16 + fc
                        ctx.rop("scalar", lambda e, P=P, fc=fc, e_=e_, gfc=gfc: e.activation(
                            hid_s[fc].ap, P.ap, AF.Relu, bias=b1c.ap[:, e_, gfc:gfc + 1]),
                            reads=[P, b1c], writes=[hid_s[fc]])
                    for dc in range(NC):
                        P = ctx.bank(2 + dc % 2)
                        for fc in range(16):
                            ctx.rop("tensor", lambda e, P=P, dc=dc, fc=fc, W2=W2: e.matmul(
                                P.ap, W2.ap[:, fc, dc, :], hid_s[fc].ap,
                                start=(fc == 0), stop=(fc == 15)),
                                reads=[W2, hid_s[fc]], writes=[P])
                        ctx.rop("vector", lambda e, P=P, tcq=tcq: e.tensor_mul(
                            tmp.ap, P.ap, ewrep.ap[:, tcq * 512:(tcq + 1) * 512]),
                            reads=[P, ewrep], writes=[tmp])
                        ctx.rop("gpsimd", lambda e, dc=dc, tcq=tcq: e.tensor_add(
                            acc.ap[:, dc, tcq * 512:(tcq + 1) * 512],
                            acc.ap[:, dc, tcq * 512:(tcq + 1) * 512], tmp.ap),
                            reads=[acc, tmp], writes=[acc])
        for dc in range(NC):
            for q in range(4):
                ctx.dma("sync", x2t, x2t.ap, aps["x2_fm"], x2_v[:, dc, t0 + q * 512:t0 + (q + 1) * 512])
                ctx.rop("vector", lambda e, dc=dc, q=q: e.tensor_add(
                    ff.ap, acc.ap[:, dc, q * 512:(q + 1) * 512], x2t.ap),
                    reads=[acc, x2t], writes=[ff])
                ctx.dma("sync", aps["ffn0_fm"], ffn_v[:, dc, t0 + q * 512:t0 + (q + 1) * 512],
                        ff, ff.ap)


def launch_D2():
    W = lambda name: (lambda st: np.ascontiguousarray(st["weights"][name]))
    return Launch(
        "D2",
        inputs=[
            ("h2T", (D, T), BF16, lambda st: st["h2T"]),
            ("ewT", (8, T), BF16, lambda st: st["ewT"]),
            ("x2_fm", (D, T), F32, lambda st: st["x2_fm"]),
            ("W1", (8, D, 4096), F32, W("L0_ffn_W1")),
            ("b1", (8, 4096), F32, W("L0_ffn_b1")),
            ("W2", (8, 4096, D), F32, W("L0_ffn_W2")),
            ("b2", (8, D), F32, W("L0_ffn_b2")),
        ],
        outputs=[("ffn0_fm", (D, T), F32)],
        emit=lambda ctx, aps: emit_stage_D2(ctx, aps),
    )


def emit_stage_E(ctx, aps):
    x_v = aps["x_fm"].ap.rearrange("(c p) t -> p c t", p=128)
    s_v = aps["s0_fm"].ap.rearrange("(c p) t -> p c t", p=128)
    f_v = aps["ffn0_fm"].ap.rearrange("(c p) t -> p c t", p=128)
    x1_v = aps["x1_fm"].ap.rearrange("(c p) t -> p c t", p=128)
    xn1_v = aps["xn1_fm"].ap.rearrange("(c p) t -> p c t", p=128)
    onesb, onesf = make_ones(ctx)
    xt = ctx.sbuf("Ex_xt", [128, 512], F32)
    st_ = ctx.sbuf("Ex_st", [128, 512], F32)
    ft = ctx.sbuf("Ex_ft", [128, 512], F32)
    x1f = ctx.sbuf("Ex_x1f", [128, NC, 512], F32)
    x1b = ctx.sbuf("Ex_x1b", [128, NC, 512], BF16)
    murep = ctx.sbuf("Ex_mu", [128, 512], F32)
    rstdrep = ctx.sbuf("Ex_rs", [128, 512], F32)
    row4 = ctx.sbuf("Ex_row4", [1, 4, 512], F32)
    scr = ctx.sbuf("Ex_scr", [128, 512], BF16)
    xnt = ctx.sbuf("Ex_xnt", [128, 512], F32)
    for tc in range(T // 512):
        for kc in range(NC):
            ctx.dma("sync", xt, xt.ap, aps["x_fm"], x_v[:, kc, tc * 512:(tc + 1) * 512])
            ctx.dma("sync", st_, st_.ap, aps["s0_fm"], s_v[:, kc, tc * 512:(tc + 1) * 512])
            ctx.dma("sync", ft, ft.ap, aps["ffn0_fm"], f_v[:, kc, tc * 512:(tc + 1) * 512])
            ctx.rop("scalar", lambda e: e.activation(ft.ap, ft.ap, AF.Silu),
                    reads=[ft], writes=[ft])
            ctx.rop("vector", lambda e: e.tensor_mul(ft.ap, ft.ap, st_.ap),
                    reads=[ft, st_], writes=[ft])
            ctx.rop("vector", lambda e, kc=kc: e.tensor_add(x1f.ap[:, kc, :], xt.ap, ft.ap),
                    reads=[xt, ft], writes=[x1f])
            ctx.rop("vector", lambda e, kc=kc: e.tensor_copy(x1b.ap[:, kc, :], x1f.ap[:, kc, :]),
                    reads=[x1f], writes=[x1b])
            ctx.dma("sync", aps["x1_fm"], x1_v[:, kc, tc * 512:(tc + 1) * 512],
                    x1f, x1f.ap[:, kc, :])
        emit_ln_fm2(ctx, "E", lambda kc: (x1b, x1b.ap[:, kc, :]), onesb, onesf,
                    murep, rstdrep, row4, scr)
        for kc in range(NC):
            ctx.rop("vector", lambda e, kc=kc: e.tensor_sub(xnt.ap, x1f.ap[:, kc, :], murep.ap),
                    reads=[x1f, murep], writes=[xnt])
            ctx.rop("vector", lambda e: e.tensor_mul(xnt.ap, xnt.ap, rstdrep.ap),
                    reads=[xnt, rstdrep], writes=[xnt])
            ctx.dma("sync", aps["xn1_fm"], xn1_v[:, kc, tc * 512:(tc + 1) * 512],
                    xnt, xnt.ap)


def launch_E():
    return Launch(
        "E",
        inputs=[
            ("x_fm", (D, T), F32, lambda st: st["x_fm"]),
            ("s0_fm", (D, T), F32, lambda st: st["s0_fm"]),
            ("ffn0_fm", (D, T), F32, lambda st: st["ffn0_fm"]),
        ],
        outputs=[("x1_fm", (D, T), F32), ("xn1_fm", (D, T), F32)],
        emit=lambda ctx, aps: emit_stage_E(ctx, aps),
    )


def emit_stage_F(ctx, aps):
    qkvT = aps["qkvT1"]
    qkv_v = qkvT.ap.rearrange("(c p) t -> p c t", p=128)
    xn_v = aps["xn1_fm"].ap.rearrange("(c p) t -> p c t", p=128)
    wqT = load_wT_bf(ctx, "F_wq", aps["qkv_w"], 3 * D, D)
    onesb, onesf = make_ones(ctx)
    qw = ctx.sbuf("F_qw", [128, 4], F32)
    for i, nm in enumerate(("q_norm_w", "q_norm_b", "k_norm_w", "k_norm_b")):
        for hh in range(2):
            ctx.dma("sync", qw, qw.ap[hh * 64:(hh + 1) * 64, i:i + 1],
                    aps[nm], aps[nm].ap.rearrange("d -> d ()"),
                    allow_slow_non_contiguous=True)
    maskb = ctx.sbuf("F_maskb", [128, 2], BF16)
    maskf = ctx.sbuf("F_maskf", [128, 2], F32)
    mask2f = ctx.sbuf("F_m2f", [2, 128], F32)
    ctx.rop("vector", lambda e: e.memset(maskf.ap, 0.0), writes=[maskf])
    ctx.rop("vector", lambda e: e.memset(maskf.ap[0:64, 0:1], 1.0), writes=[maskf])
    ctx.rop("vector", lambda e: e.memset(maskf.ap[64:128, 1:2], 1.0), writes=[maskf])
    ctx.rop("vector", lambda e: e.tensor_copy(maskb.ap, maskf.ap), reads=[maskf], writes=[maskb])
    _Pm = ctx.bank(7)
    ctx.rop("tensor", lambda e: e.transpose(_Pm.ap[0:2, 0:128], maskf.ap, ctx.ident_f.ap),
            reads=[maskf, ctx.ident_f], writes=[_Pm])
    ctx.rop("vector", lambda e: e.tensor_copy(mask2f.ap, _Pm.ap[0:2, 0:128]),
            reads=[_Pm], writes=[mask2f])

    xnt = [ctx.sbuf(f"F_xnt{i}", [128, NC, 512], BF16) for i in range(2)]
    qf = ctx.sbuf("F_qf", [128, 512], F32)
    qb_ = ctx.sbuf("F_qb", [128, 512], BF16)
    sqb = ctx.sbuf("F_sqb", [128, 512], BF16)
    rows = ctx.sbuf("F_rows", [2, 6, 512], F32)
    murep = ctx.sbuf("F_mur", [128, 512], F32)
    rsrep = ctx.sbuf("F_rsr", [128, 512], F32)
    kn_all = ctx.sbuf("F_knall", [128, NC, L], BF16)
    kmean = ctx.sbuf("F_kmean", [128, 2], F32)
    outb = ctx.sbuf("F_outb", [128, 512], BF16)

    def ln_head(P, dst_rows, tc, is_k, kc):
        ctx.rop("vector", lambda e: e.tensor_copy(qf.ap, P.ap), reads=[P], writes=[qf])
        ctx.rop("vector", lambda e: e.tensor_copy(qb_.ap, qf.ap), reads=[qf], writes=[qb_])
        ctx.rop("scalar", lambda e: e.square(sqb.ap, qb_.ap), reads=[qb_], writes=[sqb])
        Pr = ctx.bank(6)
        ctx.rop("tensor", lambda e: e.matmul(Pr.ap[0:2, :], maskb.ap, qb_.ap,
                                             start=True, stop=True),
                reads=[maskb, qb_], writes=[Pr])
        ctx.rop("vector", lambda e: e.tensor_scalar_mul(rows.ap[0:2, 2, :], Pr.ap[0:2, :], 1.0 / 64),
                reads=[Pr], writes=[rows])
        ctx.rop("tensor", lambda e: e.matmul(Pr.ap[0:2, :], maskb.ap, sqb.ap,
                                             start=True, stop=True),
                reads=[maskb, sqb], writes=[Pr])
        ctx.rop("vector", lambda e: e.tensor_scalar_mul(rows.ap[0:2, 1, :], Pr.ap[0:2, :], 1.0 / 64),
                reads=[Pr], writes=[rows])
        ctx.rop("vector", lambda e: e.tensor_mul(rows.ap[0:2, 0, :], rows.ap[0:2, 2, :], rows.ap[0:2, 2, :]),
                reads=[rows], writes=[rows])
        ctx.rop("vector", lambda e: e.tensor_sub(rows.ap[0:2, 0, :], rows.ap[0:2, 1, :], rows.ap[0:2, 0, :]),
                reads=[rows], writes=[rows])
        ctx.rop("scalar", lambda e: e.activation(rows.ap[0:2, 0, :], rows.ap[0:2, 0, :], AF.Sqrt,
                                                 bias=ctx.constcol(1e-5)[0:2, :]),
                reads=[rows, ctx.cconst], writes=[rows])
        ctx.rop("vector", lambda e: e.reciprocal(rows.ap[0:2, 3, :], rows.ap[0:2, 0, :]),
                reads=[rows], writes=[rows])
        Pb = ctx.bank(7)
        ctx.rop("tensor", lambda e: e.matmul(Pb.ap, mask2f.ap, rows.ap[0:2, 2, :],
                                             start=True, stop=True),
                reads=[mask2f, rows], writes=[Pb])
        ctx.rop("vector", lambda e: e.tensor_copy(murep.ap, Pb.ap), reads=[Pb], writes=[murep])
        ctx.rop("tensor", lambda e: e.matmul(Pb.ap, mask2f.ap, rows.ap[0:2, 3, :],
                                             start=True, stop=True),
                reads=[mask2f, rows], writes=[Pb])
        ctx.rop("vector", lambda e: e.tensor_copy(rsrep.ap, Pb.ap), reads=[Pb], writes=[rsrep])
        ctx.rop("vector", lambda e: e.tensor_sub(qf.ap, qf.ap, murep.ap),
                reads=[qf, murep], writes=[qf])
        ctx.rop("vector", lambda e: e.tensor_mul(qf.ap, qf.ap, rsrep.ap),
                reads=[qf, rsrep], writes=[qf])
        wcol = 2 if is_k else 0
        ctx.rop("vector", lambda e, wcol=wcol: e.tensor_scalar(
            outb.ap, qf.ap, qw.ap[:, wcol:wcol + 1], qw.ap[:, wcol + 1:wcol + 2],
            ALU.mult, ALU.add), reads=[qf, qw], writes=[outb])

    for tc in range(T // 512):
        X = xnt[tc % 2]
        ctx.dma("gpsimd", X, X.ap, aps["xn1_fm"], xn_v[:, :, tc * 512:(tc + 1) * 512])
        for mc in range(24):
            P = ctx.bank(mc % 2)
            for kc in range(NC):
                ctx.rop("tensor", lambda e, P=P, X=X, mc=mc, kc=kc: e.matmul(
                    P.ap, wqT.ap[:, kc, mc, :], X.ap[:, kc, :],
                    start=(kc == 0), stop=(kc == NC - 1)),
                    reads=[wqT, X], writes=[P])
            if mc < 8:
                ln_head(P, None, tc, False, mc)
                ctx.dma("sync", qkvT, qkv_v[:, mc, tc * 512:(tc + 1) * 512], outb, outb.ap)
            elif mc < 16:
                ln_head(P, None, tc, True, mc)
                s = tc // 4
                j = tc % 4
                kc_ = mc - 8
                ctx.rop("vector", lambda e, j=j, kc_=kc_: e.tensor_copy(
                    kn_all.ap[:, kc_, j * 512:(j + 1) * 512], outb.ap),
                    reads=[outb], writes=[kn_all])
                if j == 3:
                    ctx.rop("vector", lambda e, kc_=kc_: e.reduce_sum(
                        kmean.ap[:, 0:1], kn_all.ap[:, kc_, :], AX.X),
                        reads=[kn_all], writes=[kmean])
                    ctx.rop("vector", lambda e: e.tensor_scalar_mul(
                        kmean.ap[:, 1:2], kmean.ap[:, 0:1], 1.0 / L),
                        reads=[kmean], writes=[kmean])
                    for jj in range(4):
                        ctx.rop("vector", lambda e, jj=jj, kc_=kc_: e.tensor_scalar(
                            outb.ap, kn_all.ap[:, kc_, jj * 512:(jj + 1) * 512],
                            kmean.ap[:, 1:2], None, ALU.subtract),
                            reads=[kn_all, kmean], writes=[outb])
                        ctx.dma("sync", qkvT,
                                qkv_v[:, mc, s * L + jj * 512: s * L + (jj + 1) * 512],
                                outb, outb.ap)
            else:
                ctx.rop("vector", lambda e, P=P: e.tensor_copy(outb.ap, P.ap),
                        reads=[P], writes=[outb])
                ctx.dma("sync", qkvT, qkv_v[:, mc, tc * 512:(tc + 1) * 512], outb, outb.ap)


def launch_F():
    W = lambda name: (lambda st: np.ascontiguousarray(st["weights"][name]))
    return Launch(
        "F",
        inputs=[
            ("xn1_fm", (D, T), F32, lambda st: st["xn1_fm"]),
            ("qkv_w", (3 * D, D), F32, W("L1_attn_qkv_w")),
            ("q_norm_w", (64,), F32, W("L1_attn_q_norm_w")),
            ("q_norm_b", (64,), F32, W("L1_attn_q_norm_b")),
            ("k_norm_w", (64,), F32, W("L1_attn_k_norm_w")),
            ("k_norm_b", (64,), F32, W("L1_attn_k_norm_b")),
        ],
        outputs=[("qkvT1", (3 * D, T), BF16)],
        emit=lambda ctx, aps: emit_stage_F(ctx, aps),
    )


def emit_stage_F2(ctx, aps):
    oT_dram = ctx.dram("F2_oT", [D, T], BF16)
    emit_attention(ctx, "F2", aps["qkvT1"], oT_dram)
    wpT = load_wT_bf(ctx, "F2_wp", aps["proj_w"], D, D)
    pb = load_param_cols(ctx, "F2_pb", aps["proj_b"])
    oT_v = oT_dram.ap.rearrange("(c p) t -> p c t", p=128)
    s1_v = aps["s1_fm"].ap.rearrange("(c p) t -> p c t", p=128)
    ot = [ctx.sbuf(f"F2_ot{i}", [128, NC, 512], BF16) for i in range(2)]
    s1t = ctx.sbuf("F2_s1t", [128, 512], F32)
    for tc in range(T // 512):
        O = ot[tc % 2]
        ctx.dma("sync", O, O.ap, oT_dram, oT_v[:, :, tc * 512:(tc + 1) * 512])
        for mc in range(NC):
            P = ctx.bank(mc % 2)
            for kc in range(NC):
                ctx.rop("tensor", lambda e, P=P, O=O, mc=mc, kc=kc: e.matmul(
                    P.ap, wpT.ap[:, kc, mc, :], O.ap[:, kc, :],
                    start=(kc == 0), stop=(kc == NC - 1)),
                    reads=[wpT, O], writes=[P])
            ctx.rop("vector", lambda e, P=P, mc=mc: e.tensor_scalar_add(
                s1t.ap, P.ap, pb.ap[:, mc:mc + 1]), reads=[P, pb], writes=[s1t])
            ctx.dma("sync", aps["s1_fm"], s1_v[:, mc, tc * 512:(tc + 1) * 512], s1t, s1t.ap)


def launch_F2():
    W = lambda name: (lambda st: np.ascontiguousarray(st["weights"][name]))
    return Launch(
        "F2",
        inputs=[
            ("qkvT1", (3 * D, T), BF16, lambda st: st["qkvT1"]),
            ("proj_w", (D, D), F32, W("L1_attn_proj_w")),
            ("proj_b", (D,), F32, W("L1_attn_proj_b")),
        ],
        outputs=[("s1_fm", (D, T), F32)],
        emit=lambda ctx, aps: emit_stage_F2(ctx, aps),
    )


def emit_fp4_quant(ctx, xb, x_ap, ob, o_ap, F, bufs):
    nb = F // 16
    bm, sc, iv, vv, cc, gg, sg = (bufs[k] for k in ("bm", "sc", "iv", "vv", "cc", "gg", "sg"))
    x3 = x_ap.rearrange("p (a b) -> p a b", b=16)
    ctx.rop("vector", lambda e: e.reduce_max(bm.ap[:, 0:nb], x3, AX.X,
                                             apply_absolute_value=True),
            reads=[xb], writes=[bm])
    ctx.rop("vector", lambda e: e.tensor_scalar(sc.ap[:, 0:nb], bm.ap[:, 0:nb],
                                                1.0 / 6.0, 1e-12, ALU.mult, ALU.max),
            reads=[bm], writes=[sc])
    ctx.rop("vector", lambda e: e.reciprocal(iv.ap[:, 0:nb], sc.ap[:, 0:nb]),
            reads=[sc], writes=[iv])
    ivrep = iv.ap[:, 0:nb].rearrange("p a -> p a ()").broadcast_to([128, nb, 16])
    screp = sc.ap[:, 0:nb].rearrange("p a -> p a ()").broadcast_to([128, nb, 16])
    vv3 = vv.ap[:, 0:F].rearrange("p (a b) -> p a b", b=16)
    ctx.rop("scalar", lambda e: e.activation(vv.ap[:, 0:F], x_ap, AF.Abs),
            reads=[xb], writes=[vv])
    ctx.rop("vector", lambda e: e.tensor_tensor(vv3, vv3, ivrep, ALU.mult),
            reads=[vv, iv], writes=[vv])
    ctx.rop("vector", lambda e: e.tensor_scalar(cc.ap[:, 0:F], vv.ap[:, 0:F], 0.25, None, ALU.is_gt),
            reads=[vv], writes=[cc])
    for thr in (0.75, 1.25, 1.75):
        ctx.rop("vector", lambda e, thr=thr: e.scalar_tensor_tensor(
            cc.ap[:, 0:F], vv.ap[:, 0:F], thr, cc.ap[:, 0:F], ALU.is_gt, ALU.add),
            reads=[vv, cc], writes=[cc])
    ctx.rop("vector", lambda e: e.tensor_scalar(gg.ap[:, 0:F], vv.ap[:, 0:F], 2.5, None, ALU.is_gt),
            reads=[vv], writes=[gg])
    ctx.rop("vector", lambda e: e.scalar_tensor_tensor(
        gg.ap[:, 0:F], vv.ap[:, 0:F], 3.5, gg.ap[:, 0:F], ALU.is_gt, ALU.add),
        reads=[vv, gg], writes=[gg])
    ctx.rop("vector", lambda e: e.tensor_scalar(sg.ap[:, 0:F], vv.ap[:, 0:F], 5.0, 2.0, ALU.is_gt, ALU.mult),
            reads=[vv], writes=[sg])
    ctx.rop("vector", lambda e: e.tensor_add(gg.ap[:, 0:F], gg.ap[:, 0:F], sg.ap[:, 0:F]),
            reads=[gg, sg], writes=[gg])
    ctx.rop("vector", lambda e: e.scalar_tensor_tensor(
        gg.ap[:, 0:F], cc.ap[:, 0:F], 0.5, gg.ap[:, 0:F], ALU.mult, ALU.add),
        reads=[cc, gg], writes=[gg])
    ctx.rop("scalar", lambda e: e.sign(sg.ap[:, 0:F], x_ap), reads=[xb], writes=[sg])
    ctx.rop("vector", lambda e: e.tensor_mul(gg.ap[:, 0:F], gg.ap[:, 0:F], sg.ap[:, 0:F]),
            reads=[gg, sg], writes=[gg])
    gg3 = gg.ap[:, 0:F].rearrange("p (a b) -> p a b", b=16)
    o3 = o_ap.rearrange("p (a b) -> p a b", b=16)
    ctx.rop("vector", lambda e: e.tensor_tensor(o3, gg3, screp, ALU.mult),
            reads=[gg, sc], writes=[ob])


def _quant_bufs(ctx, tag, F):
    return {
        "bm": ctx.sbuf(f"{tag}_bm", [128, F // 16], F32),
        "sc": ctx.sbuf(f"{tag}_sc", [128, F // 16], F32),
        "iv": ctx.sbuf(f"{tag}_iv", [128, F // 16], F32),
        "vv": ctx.sbuf(f"{tag}_vv", [128, F], F32),
        "cc": ctx.sbuf(f"{tag}_cc", [128, F], F32),
        "gg": ctx.sbuf(f"{tag}_gg", [128, F], F32),
        "sg": ctx.sbuf(f"{tag}_sg", [128, F], F32),
    }


def emit_stage_G0(ctx, aps):
    IC = 2816
    ICT = IC // 128
    bufs = _quant_bufs(ctx, "G0", IC)
    wrow = ctx.sbuf("G0_wrow", [128, IC], F32)
    wq = ctx.sbuf("G0_wq", [128, IC], BF16)
    ev = ctx.sbuf("G0_ev", [128, 128], BF16)
    for (nm, dst, nrt, ncol) in (("w_gate", "wgT", ICT, D), ("w_up", "wuT", ICT, D),
                                 ("w_down", "wdT", NC, IC)):
        for rt in range(nrt):
            ctx.dma("sync", wrow, wrow.ap[:, 0:ncol], aps[nm],
                    aps[nm].ap[rt * 128:(rt + 1) * 128, :])
            emit_fp4_quant(ctx, wrow, wrow.ap[:, 0:ncol], wq, wq.ap[:, 0:ncol], ncol, bufs)
            for kc in range(ncol // 128):
                P = ctx.bank(4 + kc % 2)
                pv = P.ap.bitcast(BF16)[:, 0:128]
                ctx.rop("tensor", lambda e, kc=kc, pv=pv: e.transpose(
                    pv, wq.ap[:, kc * 128:(kc + 1) * 128], ctx.ident_b.ap),
                    reads=[wq, ctx.ident_b], writes=[P])
                ctx.rop("vector", lambda e, pv=pv: e.tensor_copy(ev.ap, pv),
                        reads=[P], writes=[ev])
                ctx.dma("sync", aps[dst],
                        aps[dst].ap[kc * 128:(kc + 1) * 128, rt * 128:(rt + 1) * 128],
                        ev, ev.ap)


def launch_G0():
    W = lambda name: (lambda st: np.ascontiguousarray(st["weights"][name]))
    IC = 2816
    return Launch(
        "G0",
        inputs=[
            ("w_gate", (IC, D), F32, W("L1_ffn_w_gate")),
            ("w_up", (IC, D), F32, W("L1_ffn_w_up")),
            ("w_down", (D, IC), F32, W("L1_ffn_w_down")),
        ],
        outputs=[("wgT", (D, IC), BF16), ("wuT", (D, IC), BF16), ("wdT", (IC, D), BF16)],
        emit=lambda ctx, aps: emit_stage_G0(ctx, aps),
    )


def emit_stage_G1(ctx, aps):
    IC = 2816
    ICT = IC // 128
    xn_v = aps["xn1_fm"].ap.rearrange("(c p) t -> p c t", p=128)
    hT_v = aps["hT"].ap.rearrange("(c p) t -> p c t", p=128)
    bufs = _quant_bufs(ctx, "G1", D)
    xnt = ctx.sbuf("G1_xnt", [128, NC, 512], F32)
    xtok = ctx.sbuf("G1_xtok", [128, D], F32)
    xqt = ctx.sbuf("G1_xqt", [128, D], BF16)
    xq = ctx.sbuf("G1_xq", [128, NC, T], BF16)
    for tc in range(T // 512):
        ctx.dma("sync", xnt, xnt.ap, aps["xn1_fm"], xn_v[:, :, tc * 512:(tc + 1) * 512])
        for sub in range(4):
            for kc in range(NC):
                P = ctx.bank(kc % 2)
                ctx.rop("tensor", lambda e, P=P, kc=kc, sub=sub: e.transpose(
                    P.ap[:, 0:128], xnt.ap[:, kc, sub * 128:(sub + 1) * 128], ctx.ident_f.ap),
                    reads=[xnt, ctx.ident_f], writes=[P])
                ctx.rop("vector", lambda e, P=P, kc=kc: e.tensor_copy(
                    xtok.ap[:, kc * 128:(kc + 1) * 128], P.ap[:, 0:128]),
                    reads=[P], writes=[xtok])
            emit_fp4_quant(ctx, xtok, xtok.ap, xqt, xqt.ap, D, bufs)
            for kc in range(NC):
                P = ctx.bank(2 + kc % 2)
                pv = P.ap.bitcast(BF16)[:, 0:128]
                ctx.rop("tensor", lambda e, kc=kc, pv=pv: e.transpose(
                    pv, xqt.ap[:, kc * 128:(kc + 1) * 128], ctx.ident_b.ap),
                    reads=[xqt, ctx.ident_b], writes=[P])
                ctx.rop("vector", lambda e, pv=pv, kc=kc, tc=tc, sub=sub: e.tensor_copy(
                    xq.ap[:, kc, tc * 512 + sub * 128: tc * 512 + (sub + 1) * 128], pv),
                    reads=[P], writes=[xq])
    wg = ctx.sbuf("G1_wg", [128, NC, 128], BF16)
    wu = ctx.sbuf("G1_wu", [128, NC, 128], BF16)
    gs = ctx.sbuf("G1_gs", [128, 512], F32)
    hb = ctx.sbuf("G1_hb", [128, 512], F32)
    for mc in range(ICT):
        ctx.dma("gpsimd", wg, wg.ap, aps["wgT"],
                aps["wgT"].ap.rearrange("(c p) m -> p c m", p=128)[:, :, mc * 128:(mc + 1) * 128])
        ctx.dma("gpsimd", wu, wu.ap, aps["wuT"],
                aps["wuT"].ap.rearrange("(c p) m -> p c m", p=128)[:, :, mc * 128:(mc + 1) * 128])
        for tc in range(T // 512):
            Pg = ctx.bank(4 + tc % 2)
            Pu = ctx.bank(6 + tc % 2)
            for kc in range(NC):
                ctx.rop("tensor", lambda e, Pg=Pg, mc=mc, kc=kc, tc=tc: e.matmul(
                    Pg.ap, wg.ap[:, kc, :], xq.ap[:, kc, tc * 512:(tc + 1) * 512],
                    start=(kc == 0), stop=(kc == NC - 1)),
                    reads=[wg, xq], writes=[Pg])
            for kc in range(NC):
                ctx.rop("tensor", lambda e, Pu=Pu, mc=mc, kc=kc, tc=tc: e.matmul(
                    Pu.ap, wu.ap[:, kc, :], xq.ap[:, kc, tc * 512:(tc + 1) * 512],
                    start=(kc == 0), stop=(kc == NC - 1)),
                    reads=[wu, xq], writes=[Pu])
            ctx.rop("scalar", lambda e, Pg=Pg: e.activation(gs.ap, Pg.ap, AF.Silu),
                    reads=[Pg], writes=[gs])
            ctx.rop("vector", lambda e, Pu=Pu: e.tensor_mul(hb.ap, gs.ap, Pu.ap),
                    reads=[gs, Pu], writes=[hb])
            ctx.dma("sync", aps["hT"], hT_v[:, mc, tc * 512:(tc + 1) * 512], hb, hb.ap)


def launch_G1():
    IC = 2816
    return Launch(
        "G1",
        inputs=[
            ("xn1_fm", (D, T), F32, lambda st: st["xn1_fm"]),
            ("wgT", (D, IC), BF16, lambda st: st["wgT"]),
            ("wuT", (D, IC), BF16, lambda st: st["wuT"]),
        ],
        outputs=[("hT", (IC, T), F32)],
        emit=lambda ctx, aps: emit_stage_G1(ctx, aps),
    )


def emit_stage_G3(ctx, aps):
    IC = 2816
    ICT = IC // 128
    hT_v = aps["hT"].ap.rearrange("(c p) t -> p c t", p=128)
    g1_v = aps["g1_fm"].ap.rearrange("(c p) t -> p c t", p=128)
    bufs = _quant_bufs(ctx, "G3", IC)
    wd = ctx.sbuf("G3_wd", [128, ICT, NC, 128], BF16)
    for kc in range(ICT):
        ctx.dma("gpsimd", wd, wd.ap[:, kc, :, :].rearrange("p a b -> p (a b)"),
                aps["wdT"], aps["wdT"].ap[kc * 128:(kc + 1) * 128, :])
    ht = ctx.sbuf("G3_ht", [128, ICT, 128], F32)
    htok = ctx.sbuf("G3_htok", [128, IC], F32)
    hqt = ctx.sbuf("G3_hqt", [128, IC], BF16)
    hq = ctx.sbuf("G3_hq", [128, ICT, 512], BF16)
    g1t = ctx.sbuf("G3_g1t", [128, 512], F32)
    for tc in range(T // 512):
        for sub in range(4):
            s0_ = tc * 512 + sub * 128
            ctx.dma("sync", ht, ht.ap, aps["hT"], hT_v[:, :, s0_:s0_ + 128])
            for kc in range(ICT):
                P = ctx.bank(kc % 2)
                ctx.rop("tensor", lambda e, kc=kc, P=P: e.transpose(
                    P.ap[:, 0:128], ht.ap[:, kc, :], ctx.ident_f.ap),
                    reads=[ht, ctx.ident_f], writes=[P])
                ctx.rop("vector", lambda e, P=P, kc=kc: e.tensor_copy(
                    htok.ap[:, kc * 128:(kc + 1) * 128], P.ap[:, 0:128]),
                    reads=[P], writes=[htok])
            emit_fp4_quant(ctx, htok, htok.ap, hqt, hqt.ap, IC, bufs)
            for kc in range(ICT):
                P = ctx.bank(2 + kc % 2)
                pv = P.ap.bitcast(BF16)[:, 0:128]
                ctx.rop("tensor", lambda e, kc=kc, pv=pv: e.transpose(
                    pv, hqt.ap[:, kc * 128:(kc + 1) * 128], ctx.ident_b.ap),
                    reads=[hqt, ctx.ident_b], writes=[P])
                ctx.rop("vector", lambda e, pv=pv, kc=kc, sub=sub: e.tensor_copy(
                    hq.ap[:, kc, sub * 128:(sub + 1) * 128], pv),
                    reads=[P], writes=[hq])
        for mc in range(NC):
            P = ctx.bank(4 + mc % 2)
            for kc in range(ICT):
                ctx.rop("tensor", lambda e, P=P, mc=mc, kc=kc: e.matmul(
                    P.ap, wd.ap[:, kc, mc, :], hq.ap[:, kc, :],
                    start=(kc == 0), stop=(kc == ICT - 1)),
                    reads=[wd, hq], writes=[P])
            ctx.rop("vector", lambda e, P=P: e.tensor_copy(g1t.ap, P.ap),
                    reads=[P], writes=[g1t])
            ctx.dma("sync", aps["g1_fm"], g1_v[:, mc, tc * 512:(tc + 1) * 512], g1t, g1t.ap)


def launch_G3():
    IC = 2816
    return Launch(
        "G3",
        inputs=[
            ("hT", (IC, T), F32, lambda st: st["hT"]),
            ("wdT", (IC, D), BF16, lambda st: st["wdT"]),
        ],
        outputs=[("g1_fm", (D, T), F32)],
        emit=lambda ctx, aps: emit_stage_G3(ctx, aps),
    )


def emit_stage_H(ctx, aps):
    x1_v = aps["x1_fm"].ap.rearrange("(c p) t -> p c t", p=128)
    s1_v = aps["s1_fm"].ap.rearrange("(c p) t -> p c t", p=128)
    g1_v = aps["g1_fm"].ap.rearrange("(c p) t -> p c t", p=128)
    xt = ctx.sbuf("H_xt", [128, 512], F32)
    st_ = ctx.sbuf("H_st", [128, 512], F32)
    gt = ctx.sbuf("H_gt", [128, 512], F32)
    ofm = ctx.sbuf("H_ofm", [128, NC, 512], F32)
    otok = [ctx.sbuf(f"H_otok{i}", [128, D], F32) for i in range(2)]
    for tc in range(T // 512):
        for kc in range(NC):
            ctx.dma("sync", xt, xt.ap, aps["x1_fm"], x1_v[:, kc, tc * 512:(tc + 1) * 512])
            ctx.dma("sync", st_, st_.ap, aps["s1_fm"], s1_v[:, kc, tc * 512:(tc + 1) * 512])
            ctx.dma("sync", gt, gt.ap, aps["g1_fm"], g1_v[:, kc, tc * 512:(tc + 1) * 512])
            ctx.rop("scalar", lambda e: e.activation(gt.ap, gt.ap, AF.Silu),
                    reads=[gt], writes=[gt])
            ctx.rop("vector", lambda e: e.tensor_mul(gt.ap, gt.ap, st_.ap),
                    reads=[gt, st_], writes=[gt])
            ctx.rop("vector", lambda e, kc=kc: e.tensor_add(ofm.ap[:, kc, :], xt.ap, gt.ap),
                    reads=[xt, gt], writes=[ofm])
        for sub in range(4):
            O = otok[sub % 2]
            for kc in range(NC):
                P = ctx.bank(kc % 2)
                ctx.rop("tensor", lambda e, P=P, kc=kc, sub=sub: e.transpose(
                    P.ap[:, 0:128], ofm.ap[:, kc, sub * 128:(sub + 1) * 128], ctx.ident_f.ap),
                    reads=[ofm, ctx.ident_f], writes=[P])
                ctx.rop("vector", lambda e, P=P, kc=kc, O=O: e.tensor_copy(
                    O.ap[:, kc * 128:(kc + 1) * 128], P.ap[:, 0:128]),
                    reads=[P], writes=[O])
            r0 = tc * 512 + sub * 128
            ctx.dma("sync", aps["out"], aps["out"].ap[r0:r0 + 128, :], O, O.ap)


def launch_H():
    return Launch(
        "H",
        inputs=[
            ("x1_fm", (D, T), F32, lambda st: st["x1_fm"]),
            ("s1_fm", (D, T), F32, lambda st: st["s1_fm"]),
            ("g1_fm", (D, T), F32, lambda st: st["g1_fm"]),
        ],
        outputs=[("out", (T, D), F32)],
        emit=lambda ctx, aps: emit_stage_H(ctx, aps),
    )


def kernel_chain(**inputs):
    states = shard_inputs(inputs)
    for mk in (launch_A, launch_B, launch_C, launch_C2, launch_D1, launch_D2,
               launch_E, launch_F, launch_F2, launch_G0, launch_G1, launch_G3,
               launch_H):
        mk().run(states)
    out = np.stack([st["out"].reshape(NSEQ, L, D) for st in states])
    return np.ascontiguousarray(out.reshape(16, L, D))


def emit_fused(ctx, aps):
    IC = 2816
    inter = {}
    for nm, shape, dt in (
        ("x_fm", (D, T), F32), ("xn0_fm", (D, T), F32), ("s0_fm", (D, T), F32),
        ("qkvT", (3 * D, T), BF16), ("x2_fm", (D, T), F32),
        ("h2T", (D, T), BF16), ("ewT", (8, T), BF16), ("ffn0_fm", (D, T), F32),
        ("x1_fm", (D, T), F32), ("xn1_fm", (D, T), F32),
        ("qkvT1", (3 * D, T), BF16), ("s1_fm", (D, T), F32),
        ("wgT", (D, IC), BF16), ("wuT", (D, IC), BF16), ("wdT", (IC, D), BF16),
        ("hT", (IC, T), F32), ("g1_fm", (D, T), F32),
    ):
        inter[nm] = ctx.dram("i_" + nm, list(shape), dt, kind="Internal")
    env = dict(aps)
    env.update(inter)
    make_ones(ctx)
    ctx.bank(0)
    ctx.vfence(ctx.bank(0))

    stages = [
        lambda: emit_stage_A(ctx, env["xin"], env["x_fm"], env["xn0_fm"]),
        lambda: emit_stage_B(ctx, env),
        lambda: emit_stage_C(ctx, env, env["qkvT"]),
        lambda: emit_stage_C2(ctx, env),
        lambda: emit_stage_D1(ctx, env),
        lambda: emit_stage_D2(ctx, env),
        lambda: emit_stage_E(ctx, env),
        lambda: emit_stage_F(ctx, env),
        lambda: emit_stage_F2(ctx, env),
        lambda: emit_stage_G0(ctx, env),
        lambda: emit_stage_G1(ctx, env),
        lambda: emit_stage_G3(ctx, env),
        lambda: emit_stage_H(ctx, env),
    ]
    for st in stages:
        with ctx.scope():
            st()
        ctx.barrier()


def launch_FUSED():
    W = lambda name: (lambda st: np.ascontiguousarray(st["weights"][name]))
    IC = 2816
    return Launch(
        "FUSED",
        inputs=[
            ("xin", (T, D), F32, lambda st: st["xin"]),
            ("conv_w", (D, 4), F32, lambda st: np.ascontiguousarray(
                st["weights"]["L0_attn_conv_w"].reshape(D, 4))),
            ("conv_b", (D,), F32, W("L0_attn_conv_b")),
            ("W_r", (D, D), F32, W("L0_attn_W_r")),
            ("b_r", (D,), F32, W("L0_attn_b_r")),
            ("W_i", (D, D), F32, W("L0_attn_W_i")),
            ("b_i", (D,), F32, W("L0_attn_b_i")),
            ("log_a", (D,), F32, W("L0_attn_log_a")),
            ("W_out", (D, D), F32, W("L0_attn_W_out")),
            ("gamma", (D,), F32, W("L0_attn_gamma")),
            ("ln1_g", (D,), F32, W("L0_ffn_ln1_g")),
            ("ln1_b", (D,), F32, W("L0_ffn_ln1_b")),
            ("w_qkv", (D, 3 * D), F32, W("L0_ffn_w_qkv")),
            ("b_qkv", (3 * D,), F32, W("L0_ffn_b_qkv")),
            ("w_o", (D, D), F32, W("L0_ffn_w_o")),
            ("b_o", (D,), F32, W("L0_ffn_b_o")),
            ("ln2_g", (D,), F32, W("L0_ffn_ln2_g")),
            ("ln2_b", (D,), F32, W("L0_ffn_ln2_b")),
            ("gate_w", (D, 8), F32, W("L0_ffn_gate_w")),
            ("gate_b", (8,), F32, W("L0_ffn_gate_b")),
            ("W1", (8, D, 4096), F32, W("L0_ffn_W1")),
            ("b1", (8, 4096), F32, W("L0_ffn_b1")),
            ("W2", (8, 4096, D), F32, W("L0_ffn_W2")),
            ("b2", (8, D), F32, W("L0_ffn_b2")),
            ("qkv_w", (3 * D, D), F32, W("L1_attn_qkv_w")),
            ("q_norm_w", (64,), F32, W("L1_attn_q_norm_w")),
            ("q_norm_b", (64,), F32, W("L1_attn_q_norm_b")),
            ("k_norm_w", (64,), F32, W("L1_attn_k_norm_w")),
            ("k_norm_b", (64,), F32, W("L1_attn_k_norm_b")),
            ("proj_w", (D, D), F32, W("L1_attn_proj_w")),
            ("proj_b", (D,), F32, W("L1_attn_proj_b")),
            ("w_gate", (IC, D), F32, W("L1_ffn_w_gate")),
            ("w_up", (IC, D), F32, W("L1_ffn_w_up")),
            ("w_down", (D, IC), F32, W("L1_ffn_w_down")),
        ],
        outputs=[("out", (T, D), F32)],
        emit=lambda ctx, aps: emit_fused(ctx, aps),
    )


def kernel(**inputs):
    states = shard_inputs(inputs)
    launch_FUSED().run(states)
    out = np.stack([st["out"].reshape(NSEQ, L, D) for st in states])
    return np.ascontiguousarray(out.reshape(16, L, D))


# version 05 (final)
# speedup vs baseline: 1.1112x; 1.0445x over previous
import os
import numpy as np
import ml_dtypes
from contextlib import ExitStack, contextmanager

import concourse.bass as bass
import concourse.mybir as mybir
from concourse.bass_utils import run_bass_kernel_spmd

F32 = mybir.dt.float32
BF16 = mybir.dt.bfloat16
AF = mybir.ActivationFunctionType
ALU = mybir.AluOpType
AX = mybir.AxisListType
BF = ml_dtypes.bfloat16

ENGINES = ("sync", "tensor", "vector", "scalar", "gpsimd")

D = 1024
L = 2048
NSEQ = 2
T = NSEQ * L
NC = D // 128
NCORE = 8

IDENTF = np.eye(128, dtype=np.float32)
IDENTB = np.eye(128, dtype=BF)


class Dep:
    __slots__ = ("sem", "val", "eng")

    def __init__(self, sem, val, eng):
        self.sem = sem
        self.val = val
        self.eng = eng


class Buf:
    __slots__ = ("name", "ap", "last_write", "reads")

    def __init__(self, name, ap):
        self.name = name
        self.ap = ap
        self.last_write = None
        self.reads = []

    def __getitem__(self, idx):
        return self.ap.__getitem__(idx)


class Ctx:

    def __init__(self, name="prog"):
        self.nc = bass.Bass("TRN2", target_bir_lowering=False)
        self.stack = ExitStack()
        self.sems = {}
        self.semval = {}
        self.thunks = {e: [] for e in ENGINES}
        self.dma_sems = {}
        self._buf_names = set()
        self.constvals = []
        self._constmap = {}
        self.cconst = None
        self.ident_f = None
        self.ident_b = None
        self.waited = {e: {} for e in ENGINES}
        self._banks = {}
        self._dmaring = {}
        self._dmaslot = {}
        self.DMA_RING = 24

    def sem(self, name):
        if name not in self.sems:
            self.sems[name] = self.stack.enter_context(self.nc.semaphore(name))
            self.semval[name] = 0
        return self.sems[name]

    def sbuf(self, name, shape, dtype):
        assert name not in self._buf_names, name
        self._buf_names.add(name)
        cur = getattr(self, "_cur", None) or self.stack
        h = cur.enter_context(self.nc.sbuf_tensor(name, shape, dtype))
        return Buf(name, h[:])

    @contextmanager
    def scope(self):
        old = getattr(self, "_cur", None)
        self._cur = ExitStack()
        try:
            yield
        finally:
            self._cur.close()
            self._cur = old

    def psum(self, name, shape, dtype=F32):
        assert name not in self._buf_names, name
        self._buf_names.add(name)
        h = self.stack.enter_context(self.nc.psum_tensor(name, shape, dtype))
        return Buf(name, h[:])

    def vfence(self, buf):
        if not hasattr(self, "_junk"):
            cur = getattr(self, "_cur", None)
            self._cur = None
            self._junk = self.sbuf("junk_fence", [1, 8], F32)
            self._cur = cur
        return self.rop("vector", lambda e: e.memset(self._junk.ap, 0.0),
                        reads=[], writes=[buf, self._junk])

    def _psum_all(self):
        if not hasattr(self, "_psall"):
            self._psall = self.stack.enter_context(
                self.nc.psum_tensor("psum_all", [128, 8, 512], F32))
        return self._psall

    def bank(self, i):
        if i not in self._banks:
            assert 0 <= i < 8
            self._banks[i] = Buf(f"pbank{i}", self._psum_all()[:][:, i, :])
        return self._banks[i]

    def pspan(self, i, k):
        return self._psum_all()[:][:, i:i + k, :].rearrange("p a b -> p (a b)")

    def dram(self, name, shape, dtype, kind="Internal"):
        ap = self.nc.dram_tensor(name, shape, dtype, kind=kind).ap()
        return Buf(name, ap)

    def _wait(self, engine, sem_name, val):
        if val <= 0:
            return
        w = self.waited[engine]
        if w.get(sem_name, 0) >= val:
            return
        w[sem_name] = val
        self.sem(sem_name)
        sems = self.sems
        self.thunks[engine].append(lambda eng: eng.wait_ge(sems[sem_name], val))

    def _emit(self, engine, build, inc):
        name, n = inc
        self.sem(name)
        self.semval[name] += n
        tgt = self.semval[name]
        sems = self.sems

        def thunk(eng, build=build, name=name, n=n):
            build(eng).then_inc(sems[name], n)
        self.thunks[engine].append(thunk)
        return tgt

    def rop(self, engine, build, reads=(), writes=(), dma=False, sem=None):
        if sem is None:
            if dma:
                ring = self._dmaring.setdefault(engine, [0] * self.DMA_RING)
                slot = self._dmaslot.get(engine, 0)
                self._dmaslot[engine] = (slot + 1) % self.DMA_RING
                sem = f"d_{engine}_{slot}"
                if ring[slot] > 0:
                    self._wait(engine, sem, 16 * ring[slot])
                ring[slot] += 1
            else:
                sem = "c_" + engine
        if dma:
            self.dma_sems[sem] = engine
        n = 16 if dma else 1
        deps = []
        for b in reads:
            if b.last_write is not None:
                deps.append(b.last_write)
        for b in writes:
            if b.last_write is not None:
                deps.append(b.last_write)
            deps.extend(b.reads)
        for d in deps:
            if (d.eng == engine == "tensor" and not dma
                    and not d.sem.startswith("d_")):
                continue
            self._wait(engine, d.sem, d.val)
        tgt = self._emit(engine, build, (sem, n))
        dep = Dep(sem, tgt, engine)
        for b in reads:
            b.reads.append(dep)
        for b in writes:
            b.last_write = dep
            b.reads = []
        return dep

    def dma(self, engine, dst, dst_ap, src, src_ap, **kw):
        reads = [src] if src is not None else []
        writes = [dst] if dst is not None else []
        return self.rop(engine,
                        lambda e: e.dma_start(out=dst_ap, in_=src_ap, **kw),
                        reads=reads, writes=writes, dma=True)

    def constcol(self, v):
        v = float(v)
        if v not in self._constmap:
            self._constmap[v] = len(self.constvals)
            self.constvals.append(v)
            assert len(self.constvals) <= 16
        i = self._constmap[v]
        return self.cconst[:, i:i + 1]

    def barrier(self):
        for sem, eng in self.dma_sems.items():
            self._wait(eng, sem, self.semval[sem])
        self.sem("bar")
        for e in ENGINES:
            self.semval["bar"] += 1
            sems = self.sems
            self.thunks[e].append(lambda eng, s=sems: eng.sem_inc(s["bar"], 1))
        tgt = self.semval["bar"]
        for e in ENGINES:
            self._wait(e, "bar", tgt)

    def build(self):
        with self.stack:
            with self.nc.Block() as block:
                for e in ENGINES:
                    ths = self.thunks[e]
                    getattr(block, e)(lambda eng, ths=ths: [t(eng) for t in ths])
        return self.nc


class Launch:
    def __init__(self, name, inputs, outputs, emit):
        self.name = name
        self.inputs = inputs
        self.outputs = outputs
        self.emit = emit
        self._nc = None

    def build(self):
        if self._nc is not None:
            return self._nc
        ctx = Ctx(self.name)
        self._ctx = ctx
        aps = {}
        std = [("cconst", (128, 16), F32, None),
               ("identf", (128, 128), F32, None),
               ("identb", (128, 128), BF16, None)]
        for (n, shape, dt, _src) in std + list(self.inputs):
            aps[n] = ctx.dram(n, list(shape), dt, kind="ExternalInput")
        for (n, shape, dt) in self.outputs:
            aps[n] = ctx.dram(n, list(shape), dt, kind="ExternalOutput")
        ctx.cconst = ctx.sbuf("cconst_sb", [128, 16], F32)
        idf = ctx.sbuf("identf_sb", [128, 128], F32)
        idb = ctx.sbuf("identb_sb", [128, 128], BF16)
        ctx.ident_f = idf
        ctx.ident_b = idb
        ctx.dma("sync", ctx.cconst, ctx.cconst.ap, aps["cconst"], aps["cconst"].ap)
        ctx.dma("sync", idf, idf.ap, aps["identf"], aps["identf"].ap)
        ctx.dma("sync", idb, idb.ap, aps["identb"], aps["identb"].ap)
        self.emit(ctx, aps)
        ctx.barrier()
        self._nc = ctx.build()
        return self._nc

    def run(self, states, **kw):
        nc = self.build()
        cvals = np.zeros((128, 16), np.float32)
        for i, v in enumerate(self._ctx.constvals):
            cvals[:, i] = v
        in_maps = []
        for st in states:
            m = {"cconst": cvals, "identf": IDENTF, "identb": IDENTB}
            for (n, shape, dt, src) in self.inputs:
                a = src(st)
                assert tuple(a.shape) == tuple(shape), (self.name, n, a.shape, shape)
                m[n] = a
            in_maps.append(m)
        res = run_bass_kernel_spmd(nc, in_maps, core_ids=list(range(len(states))), **kw)
        for st, r in zip(states, res.results):
            for (n, shape, dt) in self.outputs:
                st[n] = r[n]
        return res


def emit_stage_A(ctx, xin, x_fm, xn0_fm):
    NT = T // 128
    xt = [ctx.sbuf(f"A_xt{i}", [128, D], F32) for i in range(2)]
    sq = [ctx.sbuf(f"A_sq{i}", [128, D], F32) for i in range(2)]
    xnt = [ctx.sbuf(f"A_xn{i}", [128, D], F32) for i in range(2)]
    st = [ctx.sbuf(f"A_st{i}", [128, 8], F32) for i in range(2)]
    xev = [ctx.sbuf(f"A_xe{i}", [128, D], F32) for i in range(2)]
    xnev = [ctx.sbuf(f"A_xne{i}", [128, D], F32) for i in range(2)]
    pss = [ctx.bank(0), ctx.bank(1)]

    x_fm_v = x_fm.ap.rearrange("(c p) t -> p c t", p=128)
    xn_fm_v = xn0_fm.ap.rearrange("(c p) t -> p c t", p=128)
    epsc = ctx.constcol(1e-5)

    for ti in range(NT):
        b = ti % 2
        X, SQ, XN, S = xt[b], sq[b], xnt[b], st[b]
        ctx.dma("sync", X, X.ap, xin, xin.ap[ti * 128:(ti + 1) * 128, :])
        ctx.rop("vector", lambda e, X=X, SQ=SQ: e.tensor_mul(SQ.ap, X.ap, X.ap),
                reads=[X], writes=[SQ])
        ctx.rop("vector", lambda e, X=X, S=S: e.reduce_sum(S[:, 0:1], X.ap, AX.X),
                reads=[X], writes=[S])
        ctx.rop("vector", lambda e, SQ=SQ, S=S: e.reduce_sum(S[:, 1:2], SQ.ap, AX.X),
                reads=[SQ], writes=[S])
        ctx.rop("vector", lambda e, S=S: e.tensor_scalar_mul(S[:, 2:3], S[:, 0:1], 1.0 / D),
                reads=[S], writes=[S])
        ctx.rop("vector", lambda e, S=S: e.tensor_scalar_mul(S[:, 3:4], S[:, 1:2], 1.0 / D),
                reads=[S], writes=[S])
        ctx.rop("vector", lambda e, S=S: e.tensor_mul(S[:, 4:5], S[:, 2:3], S[:, 2:3]),
                reads=[S], writes=[S])
        ctx.rop("vector", lambda e, S=S: e.tensor_sub(S[:, 5:6], S[:, 3:4], S[:, 4:5]),
                reads=[S], writes=[S])
        ctx.rop("scalar", lambda e, S=S, epsc=epsc: e.activation(S[:, 6:7], S[:, 5:6], AF.Sqrt, bias=epsc),
                reads=[S, ctx.cconst], writes=[S])
        ctx.rop("vector", lambda e, S=S: e.reciprocal(S[:, 7:8], S[:, 6:7]),
                reads=[S], writes=[S])
        ctx.rop("vector", lambda e, X=X, XN=XN, S=S: e.tensor_scalar(
            XN.ap, X.ap, S[:, 2:3], S[:, 7:8], ALU.subtract, ALU.mult),
            reads=[X, S], writes=[XN])
        for g in range(4):
            pb = (ti * 4 + g) % 2
            srcbuf = X if g < 2 else XN
            P = pss[pb]
            for k in range(4):
                dc = (g % 2) * 4 + k
                ctx.rop("tensor", lambda e, sb=srcbuf, dc=dc, P=P, k=k: e.transpose(
                    P[:, k * 128:(k + 1) * 128], sb.ap[:, dc * 128:(dc + 1) * 128],
                    ctx.ident_f.ap),
                    reads=[srcbuf, ctx.ident_f], writes=[P])
            dstbuf = xev[b] if g < 2 else xnev[b]
            off = (g % 2) * 512
            ctx.rop("vector", lambda e, db=dstbuf, off=off, P=P: e.tensor_copy(
                db.ap[:, off:off + 512], P.ap), reads=[P], writes=[dstbuf])
        ctx.dma("sync", x_fm, x_fm_v[:, :, ti * 128:(ti + 1) * 128],
                xev[b], xev[b].ap.rearrange("p (c t) -> p c t", t=128))
        ctx.dma("sync", xn0_fm, xn_fm_v[:, :, ti * 128:(ti + 1) * 128],
                xnev[b], xnev[b].ap.rearrange("p (c t) -> p c t", t=128))


def launch_A():
    return Launch(
        "A",
        inputs=[("xin", (T, D), F32, lambda st: st["xin"])],
        outputs=[("x_fm", (D, T), F32), ("xn0_fm", (D, T), F32)],
        emit=lambda ctx, aps: emit_stage_A(ctx, aps["xin"], aps["x_fm"], aps["xn0_fm"]),
    )


def shard_inputs(inputs):
    x = np.ascontiguousarray(inputs["L0_attn_x"])
    states = []
    for c in range(NCORE):
        xs = x[c * NSEQ:(c + 1) * NSEQ].reshape(T, D)
        states.append({"xin": np.ascontiguousarray(xs), "weights": inputs})
    return states





def load_param_cols(ctx, tag, src_buf, n=D):
    nc_ = n // 128
    stage = ctx.sbuf(f"{tag}_pst", [nc_, 128], F32)
    ps = ctx.bank(0)
    out = ctx.sbuf(f"{tag}_pc", [128, nc_], F32)
    ctx.dma("sync", stage, stage.ap,
            src_buf, src_buf.ap.rearrange("(c p) -> c p", p=128))
    ctx.rop("tensor", lambda e: e.transpose(ps.ap[:, 0:nc_], stage.ap, ctx.ident_f.ap[0:nc_, 0:nc_]),
            reads=[stage, ctx.ident_f], writes=[ps])
    ctx.rop("vector", lambda e: e.tensor_copy(out.ap, ps.ap[:, 0:nc_]), reads=[ps], writes=[out])
    return out


def load_wT_bf(ctx, tag, w_buf, M, K):
    mt, kt = M // 128, K // 128
    wT = ctx.sbuf(f"{tag}_wT", [128, kt, mt, 128], BF16)
    stg = [ctx.sbuf(f"{tag}_ws{i}", [128, 128], BF16) for i in range(2)]
    pss = [ctx.bank(0), ctx.bank(1)]
    i = 0
    for mc in range(mt):
        for kc in range(kt):
            b = i % 2
            i += 1
            S, P = stg[b], pss[b]
            pv = P.ap.bitcast(BF16)[:, 0:128]
            ctx.dma("gpsimd", S, S.ap,
                    w_buf, w_buf.ap[mc * 128:(mc + 1) * 128, kc * 128:(kc + 1) * 128])
            ctx.rop("tensor", lambda e, S=S, pv=pv: e.transpose(pv, S.ap, ctx.ident_b.ap),
                    reads=[S, ctx.ident_b], writes=[P])
            ctx.rop("vector", lambda e, pv=pv, kc=kc, mc=mc: e.tensor_copy(
                wT.ap[:, kc, mc, :], pv), reads=[P], writes=[wT])
    return wT


def make_ones(ctx):
    if getattr(ctx, "_onesbf", None) is not None:
        return ctx._onesbf
    cur = getattr(ctx, "_cur", None)
    ctx._cur = None
    onesb = ctx.sbuf("onesb", [128, 1], BF16)
    onesf = ctx.sbuf("onesf_row", [1, 128], F32)
    ctx._cur = cur
    ctx.rop("vector", lambda e: e.memset(onesb.ap, 1.0), writes=[onesb])
    ctx.rop("vector", lambda e: e.memset(onesf.ap, 1.0), writes=[onesf])
    ctx._onesbf = (onesb, onesf)
    return onesb, onesf


def emit_stage_B(ctx, aps):
    xn_fm = aps["xn0_fm"]
    s0_fm = aps["s0_fm"]
    wrT = load_wT_bf(ctx, "wr", aps["W_r"], D, D)
    wiT = load_wT_bf(ctx, "wi", aps["W_i"], D, D)
    woT = load_wT_bf(ctx, "wo", aps["W_out"], D, D)
    cw = ctx.sbuf("B_cw", [128, NC, 4], F32)
    for c in range(NC):
        ctx.dma("sync", cw, cw.ap[:, c, :], aps["conv_w"],
                aps["conv_w"].ap[c * 128:(c + 1) * 128, :])
    cb = load_param_cols(ctx, "B_cb", aps["conv_b"])
    brc = load_param_cols(ctx, "B_br", aps["b_r"])
    bic = load_param_cols(ctx, "B_bi", aps["b_i"])
    lac = load_param_cols(ctx, "B_la", aps["log_a"])
    gac = load_param_cols(ctx, "B_ga", aps["gamma"])
    onesb, onesf = make_ones(ctx)

    la8 = ctx.sbuf("B_la8", [128, NC], F32)
    ctx.rop("scalar", lambda e: e.activation(la8.ap, lac.ap, AF.Sigmoid),
            reads=[lac], writes=[la8])
    ctx.rop("vector", lambda e: e.tensor_scalar_max(la8.ap, la8.ap, 1e-38),
            reads=[la8], writes=[la8])
    ctx.rop("scalar", lambda e: e.activation(la8.ap, la8.ap, AF.Ln),
            reads=[la8], writes=[la8])
    ctx.rop("vector", lambda e: e.tensor_scalar_mul(la8.ap, la8.ap, 8.0),
            reads=[la8], writes=[la8])

    xcv = ctx.sbuf("B_xcv", [128, NC, T], BF16)
    xnp = [ctx.sbuf(f"B_xnp{i}", [128, L + 3], BF16) for i in range(2)]
    xn_fm_v = xn_fm.ap.rearrange("(c p) t -> p c t", p=128)
    for c in range(NC):
        for s in range(NSEQ):
            b = (c * NSEQ + s) % 2
            XP = xnp[b]
            ctx.rop("vector", lambda e, XP=XP: e.memset(XP.ap[:, 0:3], 0.0), writes=[XP])
            ctx.dma("gpsimd", XP, XP.ap[:, 3:L + 3],
                    xn_fm, xn_fm_v[:, c, s * L:(s + 1) * L])
            tacc = ctx.sbuf(f"B_cvt{c}_{s}", [128, L], F32) if c == 0 and s == 0 else tacc
            ctx.rop("vector", lambda e, XP=XP, c=c: e.tensor_scalar(
                tacc.ap, XP.ap[:, 0:L], cw.ap[:, c, 0:1], None, ALU.mult),
                reads=[XP, cw], writes=[tacc])
            for j in range(1, 4):
                ctx.rop("vector", lambda e, XP=XP, c=c, j=j: e.scalar_tensor_tensor(
                    tacc.ap, XP.ap[:, j:j + L], cw.ap[:, c, j:j + 1], tacc.ap,
                    ALU.mult, ALU.add), reads=[XP, cw, tacc], writes=[tacc])
            ctx.rop("vector", lambda e, c=c, s=s: e.tensor_scalar_add(
                xcv.ap[:, c, s * L:(s + 1) * L], tacc.ap, cb.ap[:, c:c + 1]),
                reads=[tacc, cb], writes=[xcv])
            if "dbg_xcv" in aps:
                ctx.dma("sync", aps["dbg_xcv"],
                        aps["dbg_xcv"].ap.rearrange("(c p) t -> p c t", p=128)[:, c, s * L:(s + 1) * L],
                        xcv, xcv.ap[:, c, s * L:(s + 1) * L])

    h_dram = aps.get("dbg_h") or ctx.dram("B_h_dram", [D, T], BF16)
    h_dram_v = h_dram.ap.rearrange("(c p) t -> p c t", p=128)
    hs = [ctx.sbuf("B_hs0", [128, L], BF16)]
    zr_banks = [ctx.bank(i) for i in range(4)]
    zi_banks = [ctx.bank(4 + i) for i in range(4)]
    rt = ctx.sbuf("B_rt", [128, L], F32)
    it = ctx.sbuf("B_it", [128, L], F32)
    at = ctx.sbuf("B_at", [128, L], F32)
    t1 = ctx.sbuf("B_t1", [128, L], F32)
    t2 = rt
    for oc in range(NC):
        for s in range(NSEQ):
            for (banks, wT) in ((zr_banks, wrT), (zi_banks, wiT)):
                for nb in range(4):
                    ps = banks[nb]
                    for kc in range(NC):
                        ctx.rop("tensor", lambda e, ps=ps, wT=wT, nb=nb, kc=kc, oc=oc, s=s: e.matmul(
                            ps.ap,
                            wT.ap[:, kc, oc, :],
                            xcv.ap[:, kc, s * L + nb * 512: s * L + (nb + 1) * 512],
                            start=(kc == 0), stop=(kc == NC - 1)),
                            reads=[wT, xcv], writes=[ps])
            for nb in range(4):
                ctx.rop("scalar", lambda e, oc=oc, nb=nb: e.activation(
                    rt.ap[:, nb * 512:(nb + 1) * 512], zr_banks[nb].ap, AF.Sigmoid,
                    bias=brc.ap[:, oc:oc + 1]),
                    reads=[zr_banks[nb], brc], writes=[rt])
                ctx.rop("scalar", lambda e, oc=oc, nb=nb: e.activation(
                    it.ap[:, nb * 512:(nb + 1) * 512], zi_banks[nb].ap, AF.Sigmoid,
                    bias=bic.ap[:, oc:oc + 1]),
                    reads=[zi_banks[nb], bic], writes=[it])
            ctx.rop("scalar", lambda e, oc=oc: e.activation(at.ap, rt.ap, AF.Exp,
                                                            scale=la8.ap[:, oc:oc + 1]),
                    reads=[rt, la8], writes=[at])
            ctx.rop("vector", lambda e: e.tensor_mul(t1.ap, at.ap, at.ap),
                    reads=[at], writes=[t1])
            ctx.rop("vector", lambda e: e.tensor_scalar(t1.ap, t1.ap, -1.0, 1.0, ALU.mult, ALU.add),
                    reads=[t1], writes=[t1])
            ctx.rop("vector", lambda e: e.tensor_scalar_max(t1.ap, t1.ap, 0.0),
                    reads=[t1], writes=[t1])
            ctx.rop("scalar", lambda e: e.activation(t1.ap, t1.ap, AF.Sqrt),
                    reads=[t1], writes=[t1])
            ctx.rop("vector", lambda e, oc=oc, s=s: e.tensor_copy(
                t2.ap, xcv.ap[:, oc, s * L:(s + 1) * L]), reads=[xcv], writes=[t2])
            ctx.rop("vector", lambda e: e.tensor_mul(t1.ap, t1.ap, it.ap),
                    reads=[t1, it], writes=[t1])
            ctx.rop("vector", lambda e: e.tensor_mul(t2.ap, t1.ap, t2.ap),
                    reads=[t1, t2], writes=[t2])
            hb = hs[0]
            ctx.rop("vector", lambda e, hb=hb: e.tensor_tensor_scan(
                hb.ap, at.ap, t2.ap, 0.0, ALU.mult, ALU.add),
                reads=[at, t2], writes=[hb])
            ctx.dma("sync", h_dram, h_dram_v[:, oc, s * L:(s + 1) * L], hb, hb.ap)

    if "dbg_pc" in aps:
        ctx.dma("sync", aps["dbg_pc"], aps["dbg_pc"].ap[:, 0:NC], gac, gac.ap)
        ctx.dma("sync", aps["dbg_pc"], aps["dbg_pc"].ap[:, 8:8 + NC], la8, la8.ap)
        ctx.dma("sync", aps["dbg_pc"], aps["dbg_pc"].ap[:, 16:16 + NC], brc, brc.ap)
        ctx.dma("sync", aps["dbg_pc"], aps["dbg_pc"].ap[:, 24:24 + NC], cb, cb.ap)
    if "dbg_wT" in aps:
        ctx.dma("sync", aps["dbg_wT"], aps["dbg_wT"].ap.rearrange("p a b q -> p (a b q)"),
                woT, woT.ap.rearrange("p a b q -> p (a b q)"))
    ht = [ctx.sbuf("B_ht0", [128, NC, 512], BF16)]
    o_f = ctx.sbuf("B_of", [128, NC, 512], F32)
    osq = ctx.sbuf("B_osq", [128, NC, 512], BF16)
    ps_o = [ctx.bank(0), ctx.bank(1)]
    ps_r = ctx.bank(2)
    ps_b = ctx.bank(3)
    rowA = ctx.sbuf("B_rowA", [1, 512], F32)
    rrep = ctx.sbuf("B_rrep", [128, 512], F32)
    sout = ctx.sbuf("B_sout", [128, 512], F32)
    s0_fm_v = s0_fm.ap.rearrange("(c p) t -> p c t", p=128)
    for tc in range(T // 512):
        H = ht[0]
        ctx.dma("sync", H, H.ap, h_dram, h_dram_v[:, :, tc * 512:(tc + 1) * 512])
        for mc in range(NC):
            P = ps_o[mc % 2]
            for kc in range(NC):
                ctx.rop("tensor", lambda e, P=P, H=H, mc=mc, kc=kc: e.matmul(
                    P.ap, woT.ap[:, kc, mc, :], H.ap[:, kc, :],
                    start=(kc == 0), stop=(kc == NC - 1)),
                    reads=[woT, H], writes=[P])
            ctx.rop("vector", lambda e, P=P, mc=mc: e.tensor_copy(o_f.ap[:, mc, :], P.ap),
                    reads=[P], writes=[o_f])
            ctx.vfence(P)
            if "dbg_o" in aps:
                ctx.dma("sync", aps["dbg_o"],
                        aps["dbg_o"].ap.rearrange("(c p) t -> p c t", p=128)[:, mc, tc * 512:(tc + 1) * 512],
                        o_f, o_f.ap[:, mc, :])
            ctx.rop("scalar", lambda e, mc=mc: e.square(osq.ap[:, mc, :], o_f.ap[:, mc, :]),
                    reads=[o_f], writes=[osq])
        for mc in range(NC):
            ctx.rop("tensor", lambda e, mc=mc: e.matmul(
                ps_r.ap[0:1, :], onesb.ap, osq.ap[:, mc, :],
                start=(mc == 0), stop=(mc == NC - 1)),
                reads=[onesb, osq], writes=[ps_r])
        ctx.rop("vector", lambda e: e.tensor_scalar_mul(rowA.ap, ps_r.ap[0:1, :], 1.0 / D),
                reads=[ps_r], writes=[rowA])
        ctx.rop("scalar", lambda e: e.activation(rowA.ap, rowA.ap, AF.Sqrt,
                                                 bias=ctx.constcol(1e-6)[0:1, :]),
                reads=[rowA, ctx.cconst], writes=[rowA])
        ctx.rop("vector", lambda e: e.reciprocal(rowA.ap, rowA.ap),
                reads=[rowA], writes=[rowA])
        ctx.rop("tensor", lambda e: e.matmul(ps_b.ap, onesf.ap, rowA.ap,
                                             start=True, stop=True),
                reads=[onesf, rowA], writes=[ps_b])
        ctx.rop("vector", lambda e: e.tensor_copy(rrep.ap, ps_b.ap),
                reads=[ps_b], writes=[rrep])
        for mc in range(NC):
            ctx.rop("vector", lambda e, mc=mc: e.scalar_tensor_tensor(
                sout.ap, o_f.ap[:, mc, :], gac.ap[:, mc:mc + 1], rrep.ap,
                ALU.mult, ALU.mult), reads=[o_f, gac, rrep], writes=[sout])
            ctx.dma("sync", s0_fm, s0_fm_v[:, mc, tc * 512:(tc + 1) * 512],
                    sout, sout.ap)


def launch_B(dbg=False):
    W = lambda name: (lambda st: np.ascontiguousarray(st["weights"][name]))
    extra = [("dbg_xcv", (D, T), BF16), ("dbg_h", (D, T), BF16), ("dbg_o", (D, T), F32),
             ("dbg_wT", (128, 8, 8, 128), BF16), ("dbg_pc", (128, 32), F32)] if dbg else []
    return Launch(
        "Bdbg" if dbg else "B",
        inputs=[
            ("xn0_fm", (D, T), F32, lambda st: st["xn0_fm"]),
            ("conv_w", (D, 4), F32, lambda st: np.ascontiguousarray(
                st["weights"]["L0_attn_conv_w"].reshape(D, 4))),
            ("conv_b", (D,), F32, W("L0_attn_conv_b")),
            ("W_r", (D, D), F32, W("L0_attn_W_r")),
            ("b_r", (D,), F32, W("L0_attn_b_r")),
            ("W_i", (D, D), F32, W("L0_attn_W_i")),
            ("b_i", (D,), F32, W("L0_attn_b_i")),
            ("log_a", (D,), F32, W("L0_attn_log_a")),
            ("W_out", (D, D), F32, W("L0_attn_W_out")),
            ("gamma", (D,), F32, W("L0_attn_gamma")),
        ],
        outputs=[("s0_fm", (D, T), F32)] + extra,
        emit=lambda ctx, aps: emit_stage_B(ctx, aps),
    )


def emit_ln_fm2(ctx, tag, get_tile, onesb, onesf, murep, rstdrep, row4, scr,
                banks=(6, 7), eps=1e-5):
    prow = ctx.bank(banks[0])
    pbc = ctx.bank(banks[1])
    for kc in range(NC):
        b, ap = get_tile(kc)
        ctx.rop("tensor", lambda e, ap=ap, kc=kc: e.matmul(
            prow.ap[0:1, :], onesb.ap, ap, start=(kc == 0), stop=(kc == NC - 1)),
            reads=[onesb, b], writes=[prow])
    ctx.rop("vector", lambda e: e.tensor_scalar_mul(row4.ap[0:1, 2, :], prow.ap[0:1, :], 1.0 / D),
            reads=[prow], writes=[row4])
    for kc in range(NC):
        b, ap = get_tile(kc)
        ctx.rop("scalar", lambda e, ap=ap: e.square(scr.ap, ap), reads=[b], writes=[scr])
        ctx.rop("tensor", lambda e, kc=kc: e.matmul(
            prow.ap[0:1, :], onesb.ap, scr.ap, start=(kc == 0), stop=(kc == NC - 1)),
            reads=[onesb, scr], writes=[prow])
    ctx.rop("vector", lambda e: e.tensor_scalar_mul(row4.ap[0:1, 1, :], prow.ap[0:1, :], 1.0 / D),
            reads=[prow], writes=[row4])
    ctx.rop("vector", lambda e: e.tensor_mul(row4.ap[0:1, 0, :], row4.ap[0:1, 2, :], row4.ap[0:1, 2, :]),
            reads=[row4], writes=[row4])
    ctx.rop("vector", lambda e: e.tensor_sub(row4.ap[0:1, 0, :], row4.ap[0:1, 1, :], row4.ap[0:1, 0, :]),
            reads=[row4], writes=[row4])
    ctx.rop("scalar", lambda e: e.activation(row4.ap[0:1, 0, :], row4.ap[0:1, 0, :], AF.Sqrt,
                                             bias=ctx.constcol(eps)[0:1, :]),
            reads=[row4, ctx.cconst], writes=[row4])
    ctx.rop("vector", lambda e: e.reciprocal(row4.ap[0:1, 3, :], row4.ap[0:1, 0, :]),
            reads=[row4], writes=[row4])
    ctx.rop("tensor", lambda e: e.matmul(pbc.ap, onesf.ap, row4.ap[0:1, 2, :],
                                         start=True, stop=True),
            reads=[onesf, row4], writes=[pbc])
    ctx.rop("vector", lambda e: e.tensor_copy(murep.ap, pbc.ap), reads=[pbc], writes=[murep])
    ctx.rop("tensor", lambda e: e.matmul(pbc.ap, onesf.ap, row4.ap[0:1, 3, :],
                                         start=True, stop=True),
            reads=[onesf, row4], writes=[pbc])
    ctx.rop("vector", lambda e: e.tensor_copy(rstdrep.ap, pbc.ap), reads=[pbc], writes=[rstdrep])


def emit_stage_C(ctx, aps, qkvT_dram):
    xn_fm = aps["xn0_fm"]
    xn_fm_v = xn_fm.ap.rearrange("(c p) t -> p c t", p=128)
    qkv_v = qkvT_dram.ap.rearrange("(c p) t -> p c t", p=128)
    wq = ctx.sbuf("C_wq", [128, NC, 24, 128], BF16)
    for kc in range(NC):
        ctx.dma("gpsimd", wq, wq.ap[:, kc, :, :].rearrange("p a b -> p (a b)"),
                aps["w_qkv"], aps["w_qkv"].ap[kc * 128:(kc + 1) * 128, :])
    bq = load_param_cols(ctx, "C_bq", aps["b_qkv"], n=3 * D)
    g1 = load_param_cols(ctx, "C_g1", aps["ln1_g"])
    b1 = load_param_cols(ctx, "C_b1", aps["ln1_b"])
    onesb, onesf = make_ones(ctx)

    xnt = [ctx.sbuf(f"C_xnt{i}", [128, NC, 512], BF16) for i in range(2)]
    h1t = [ctx.sbuf(f"C_h1t{i}", [128, NC, 512], BF16) for i in range(2)]
    murep = ctx.sbuf("C_mu", [128, 512], F32)
    rstdrep = ctx.sbuf("C_rs", [128, 512], F32)
    row4 = ctx.sbuf("C_row4", [1, 4, 512], F32)
    scr = ctx.sbuf("C_scr", [128, 512], BF16)
    tf = ctx.sbuf("C_tf", [128, 512], F32)
    qev = [ctx.sbuf(f"C_qev{i}", [128, 512], BF16) for i in range(2)]

    for tc in range(T // 512):
        X = xnt[tc % 2]
        H = h1t[tc % 2]
        ctx.dma("gpsimd", X, X.ap, xn_fm, xn_fm_v[:, :, tc * 512:(tc + 1) * 512])
        emit_ln_fm2(ctx, "C", lambda kc: (X, X.ap[:, kc, :]), onesb, onesf,
                    murep, rstdrep, row4, scr)
        for kc in range(NC):
            ctx.rop("vector", lambda e, X=X, kc=kc: e.tensor_sub(tf.ap, X.ap[:, kc, :], murep.ap),
                    reads=[X, murep], writes=[tf])
            ctx.rop("vector", lambda e: e.tensor_mul(tf.ap, tf.ap, rstdrep.ap),
                    reads=[tf, rstdrep], writes=[tf])
            ctx.rop("vector", lambda e, H=H, kc=kc: e.tensor_scalar(
                H.ap[:, kc, :], tf.ap, g1.ap[:, kc:kc + 1], b1.ap[:, kc:kc + 1],
                ALU.mult, ALU.add), reads=[tf, g1, b1], writes=[H])
        for mc in range(24):
            P = ctx.bank(mc % 2)
            for kc in range(NC):
                ctx.rop("tensor", lambda e, P=P, H=H, mc=mc, kc=kc: e.matmul(
                    P.ap, wq.ap[:, kc, mc, :], H.ap[:, kc, :],
                    start=(kc == 0), stop=(kc == NC - 1)),
                    reads=[wq, H], writes=[P])
            Q = qev[mc % 2]
            ctx.rop("vector", lambda e, P=P, Q=Q, mc=mc: e.tensor_scalar_add(
                Q.ap, P.ap, bq.ap[:, mc:mc + 1]),
                reads=[P, bq], writes=[Q])
            ctx.dma("sync", qkvT_dram, qkv_v[:, mc, tc * 512:(tc + 1) * 512], Q, Q.ap)


def launch_C():
    W = lambda name: (lambda st: np.ascontiguousarray(st["weights"][name]))
    return Launch(
        "C",
        inputs=[
            ("xn0_fm", (D, T), F32, lambda st: st["xn0_fm"]),
            ("ln1_g", (D,), F32, W("L0_ffn_ln1_g")),
            ("ln1_b", (D,), F32, W("L0_ffn_ln1_b")),
            ("w_qkv", (D, 3 * D), F32, W("L0_ffn_w_qkv")),
            ("b_qkv", (3 * D,), F32, W("L0_ffn_b_qkv")),
        ],
        outputs=[("qkvT", (3 * D, T), BF16)],
        emit=lambda ctx, aps: emit_stage_C(ctx, aps, aps["qkvT"]),
    )


def emit_attention(ctx, tag, qkvT, oT_dram, nheads=16, dk=64, scale=0.125):
    qv = qkvT.ap
    onesb, onesf = make_ones(ctx)
    qaug = ctx.sbuf(f"{tag}_qaug", [65, L], BF16)
    kaug = ctx.sbuf(f"{tag}_kaug", [65, L], BF16)
    vln = ctx.sbuf(f"{tag}_vln", [64, L], BF16)
    vtok = ctx.sbuf(f"{tag}_vtok", [128, 16, 64], BF16)
    expT = ctx.sbuf(f"{tag}_expT", [128, 16, 512], BF16)
    expT_s = [Buf(f"{tag}_expT{j}", expT.ap[:, j, :]) for j in range(16)]
    maxc = ctx.sbuf(f"{tag}_maxc", [128, 16], F32)
    mrow = ctx.sbuf(f"{tag}_mrow", [16, 128], BF16)
    maxb = ctx.sbuf(f"{tag}_maxb", [128, 16], BF16)
    sumr = ctx.sbuf(f"{tag}_sumr", [1, 512], F32)
    mrow_dram = ctx.dram(f"{tag}_mrowd", [16, 128], BF16)
    rrep = ctx.sbuf(f"{tag}_rrep", [64, 512], F32)
    oev = ctx.sbuf(f"{tag}_oev", [64, 512], BF16)
    oT_v = oT_dram.ap.rearrange("(c p) t -> p c t", p=128)

    for s in range(NSEQ):
        for h in range(nheads):
            t0 = s * L
            ctx.dma("sync", qaug, qaug.ap[0:64, :], qkvT, qv[h * dk:(h + 1) * dk, t0:t0 + L])
            ctx.dma("sync", kaug, kaug.ap[0:64, :], qkvT, qv[D + h * dk:D + (h + 1) * dk, t0:t0 + L])
            ctx.dma("sync", vln, vln.ap, qkvT, qv[2 * D + h * dk:2 * D + (h + 1) * dk, t0:t0 + L])
            ctx.rop("vector", lambda e: e.memset(kaug.ap[64:65, :], 1.0), writes=[kaug])
            for kb in range(16):
                P = ctx.bank(4 + kb % 2)
                pv = P.ap.bitcast(BF16)[:, 0:64]
                ctx.rop("tensor", lambda e, kb=kb, pv=pv: e.transpose(
                    pv, vln.ap[:, kb * 128:(kb + 1) * 128], ctx.ident_b.ap[0:64, 0:64]),
                    reads=[vln, ctx.ident_b], writes=[P])
                ctx.rop("vector", lambda e, kb=kb, pv=pv: e.tensor_copy(
                    vtok.ap[:, kb, :], pv), reads=[P], writes=[vtok])
            for qt in range(16):
                g0 = (qt % 2) * 4
                pb = [ctx.bank(g0 + i) for i in range(4)]
                for nb in range(4):
                    ctx.rop("tensor", lambda e, qt=qt, nb=nb, pb=pb: e.matmul(
                        pb[nb].ap, qaug.ap[0:64, qt * 128:(qt + 1) * 128],
                        kaug.ap[0:64, nb * 512:(nb + 1) * 512],
                        start=True, stop=True),
                        reads=[qaug, kaug], writes=[pb[nb]])
                ctx.rop("vector", lambda e, qt=qt, g0=g0: e.reduce_max(
                    maxc.ap[:, qt:qt + 1], ctx.pspan(g0, 4), AX.X),
                    reads=pb, writes=[maxc])
            ctx.rop("vector", lambda e: e.tensor_scalar_mul(maxb.ap, maxc.ap, -scale),
                    reads=[maxc], writes=[maxb])
            P = ctx.bank(4)
            pv = P.ap.bitcast(BF16)[0:16, 0:128]
            ctx.rop("tensor", lambda e, pv=pv: e.transpose(pv, maxb.ap, ctx.ident_b.ap),
                    reads=[maxb, ctx.ident_b], writes=[P])
            ctx.rop("vector", lambda e, pv=pv: e.tensor_copy(mrow.ap, pv),
                    reads=[P], writes=[mrow])
            ctx.dma("gpsimd", mrow_dram, mrow_dram.ap, mrow, mrow.ap)
            ctx.dma("gpsimd", qaug, qaug.ap[64:65, :],
                    mrow_dram, mrow_dram.ap.rearrange("j t -> () (j t)"))
            for nb in range(4):
                q0 = nb * 512
                for kb in range(16):
                    P = ctx.bank(kb % 2)
                    EB = expT_s[kb]
                    ctx.rop("tensor", lambda e, P=P, kb=kb, q0=q0: e.matmul(
                        P.ap, kaug.ap[:, kb * 128:(kb + 1) * 128],
                        qaug.ap[:, q0:q0 + 512],
                        start=True, stop=True),
                        reads=[kaug, qaug], writes=[P])
                    ctx.rop("scalar", lambda e, P=P, EB=EB: e.activation(
                        EB.ap, P.ap, AF.Exp, scale=scale),
                        reads=[P], writes=[EB])
                    ctx.rop("tensor", lambda e, kb=kb, EB=EB: e.matmul(
                        ctx.bank(3).ap[0:1, :], onesb.ap, EB.ap,
                        start=(kb == 0), stop=(kb == 15)),
                        reads=[onesb, EB], writes=[ctx.bank(3)])
                    ctx.rop("tensor", lambda e, kb=kb, EB=EB: e.matmul(
                        ctx.bank(2).ap[0:64, :], vtok.ap[:, kb, :], EB.ap,
                        start=(kb == 0), stop=(kb == 15)),
                        reads=[vtok, EB], writes=[ctx.bank(2)])
                ctx.rop("vector", lambda e: e.reciprocal(sumr.ap, ctx.bank(3).ap[0:1, :]),
                        reads=[ctx.bank(3)], writes=[sumr])
                ctx.rop("tensor", lambda e: e.matmul(
                    ctx.bank(5).ap[0:64, :], onesf.ap[:, 0:64], sumr.ap,
                    start=True, stop=True),
                    reads=[onesf, sumr], writes=[ctx.bank(5)])
                ctx.rop("vector", lambda e: e.tensor_copy(rrep.ap, ctx.bank(5).ap[0:64, :]),
                        reads=[ctx.bank(5)], writes=[rrep])
                ctx.rop("vector", lambda e: e.tensor_mul(oev.ap, ctx.bank(2).ap[0:64, :], rrep.ap),
                        reads=[ctx.bank(2), rrep], writes=[oev])
                ctx.dma("sync", oT_dram, oT_dram.ap[h * dk:(h + 1) * dk, t0 + q0:t0 + q0 + 512],
                        oev, oev.ap)


def emit_stage_C2(ctx, aps):
    qkvT = aps["qkvT"]
    x2_fm = aps["x2_fm"]
    oT_dram = ctx.dram("C2_oT", [D, T], BF16)
    emit_attention(ctx, "C2", qkvT, oT_dram)
    wo = ctx.sbuf("C2_wo", [128, NC, NC, 128], BF16)
    for kc in range(NC):
        ctx.dma("gpsimd", wo, wo.ap[:, kc, :, :].rearrange("p a b -> p (a b)"),
                aps["w_o"], aps["w_o"].ap[kc * 128:(kc + 1) * 128, :])
    bo = load_param_cols(ctx, "C2_bo", aps["b_o"])
    oT_v = oT_dram.ap.rearrange("(c p) t -> p c t", p=128)
    xn_v = aps["xn0_fm"].ap.rearrange("(c p) t -> p c t", p=128)
    x2_v = x2_fm.ap.rearrange("(c p) t -> p c t", p=128)
    ot = [ctx.sbuf(f"C2_ot{i}", [128, NC, 512], BF16) for i in range(2)]
    xnt = [ctx.sbuf(f"C2_xnt{i}", [128, 512], F32) for i in range(2)]
    x2t = [ctx.sbuf(f"C2_x2t{i}", [128, 512], F32) for i in range(2)]
    for tc in range(T // 512):
        O = ot[tc % 2]
        ctx.dma("sync", O, O.ap, oT_dram, oT_v[:, :, tc * 512:(tc + 1) * 512])
        for mc in range(NC):
            P = ctx.bank(6 + mc % 2)
            for kc in range(NC):
                ctx.rop("tensor", lambda e, P=P, O=O, mc=mc, kc=kc: e.matmul(
                    P.ap, wo.ap[:, kc, mc, :], O.ap[:, kc, :],
                    start=(kc == 0), stop=(kc == NC - 1)),
                    reads=[wo, O], writes=[P])
            XN = xnt[mc % 2]
            X2 = x2t[mc % 2]
            ctx.dma("gpsimd", XN, XN.ap, aps["xn0_fm"], xn_v[:, mc, tc * 512:(tc + 1) * 512])
            ctx.rop("vector", lambda e, P=P, XN=XN, X2=X2, mc=mc: e.scalar_tensor_tensor(
                X2.ap, P.ap, bo.ap[:, mc:mc + 1], XN.ap, ALU.add, ALU.add),
                reads=[P, bo, XN], writes=[X2])
            ctx.dma("sync", x2_fm, x2_v[:, mc, tc * 512:(tc + 1) * 512], X2, X2.ap)


def launch_C2():
    W = lambda name: (lambda st: np.ascontiguousarray(st["weights"][name]))
    return Launch(
        "C2",
        inputs=[
            ("qkvT", (3 * D, T), BF16, lambda st: st["qkvT"]),
            ("xn0_fm", (D, T), F32, lambda st: st["xn0_fm"]),
            ("w_o", (D, D), F32, W("L0_ffn_w_o")),
            ("b_o", (D,), F32, W("L0_ffn_b_o")),
        ],
        outputs=[("x2_fm", (D, T), F32)],
        emit=lambda ctx, aps: emit_stage_C2(ctx, aps),
    )


def emit_stage_D1(ctx, aps):
    x2 = aps["x2_fm"]
    x2_v = x2.ap.rearrange("(c p) t -> p c t", p=128)
    h2d = aps["h2T"]
    h2_v = h2d.ap.rearrange("(c p) t -> p c t", p=128)
    ewT = aps["ewT"]
    g2c = load_param_cols(ctx, "D_g2", aps["ln2_g"])
    b2c = load_param_cols(ctx, "D_b2", aps["ln2_b"])
    onesb, onesf = make_ones(ctx)
    gw = ctx.sbuf("D_gw", [128, NC, 8], F32)
    for kc in range(NC):
        ctx.dma("sync", gw, gw.ap[:, kc, :], aps["gate_w"],
                aps["gate_w"].ap[kc * 128:(kc + 1) * 128, :])
    gb_st = ctx.sbuf("D_gbst", [1, 8], F32)
    gb = ctx.sbuf("D_gb", [8, 1], F32)
    ctx.dma("sync", gb_st, gb_st.ap, aps["gate_b"], aps["gate_b"].ap.rearrange("e -> () e"))
    Pg = ctx.bank(7)
    ctx.rop("tensor", lambda e: e.transpose(Pg.ap[0:8, 0:1], gb_st.ap, ctx.ident_f.ap[0:1, 0:1]),
            reads=[gb_st, ctx.ident_f], writes=[Pg])
    ctx.rop("vector", lambda e: e.tensor_copy(gb.ap, Pg.ap[0:8, 0:1]), reads=[Pg], writes=[gb])

    xt = [ctx.sbuf(f"D_xt{i}", [128, NC, 512], BF16) for i in range(2)]
    ht = [ctx.sbuf(f"D_ht{i}", [128, NC, 512], BF16) for i in range(2)]
    hf = ctx.sbuf("D_hf", [128, NC, 512], F32)
    murep = ctx.sbuf("D_mu", [128, 512], F32)
    rstdrep = ctx.sbuf("D_rs", [128, 512], F32)
    row4 = ctx.sbuf("D_row4", [1, 4, 512], F32)
    scr = ctx.sbuf("D_scr", [128, 512], BF16)
    tf = ctx.sbuf("D_tf", [128, 512], F32)
    lg = ctx.sbuf("D_lg", [8, 512], F32)
    ltok = ctx.sbuf("D_ltok", [128, 8], F32)
    cols = ctx.sbuf("D_cols", [128, 8], F32)
    ewtok = ctx.sbuf("D_ewtok", [128, 8], BF16)
    ewev = ctx.sbuf("D_ewev", [8, 128], BF16)
    lmask = ctx.sbuf("D_lmask", [128, 8], F32)
    ewf = ctx.sbuf("D_ewf", [128, 8], F32)

    for tc in range(T // 512):
        X = xt[tc % 2]
        H = ht[tc % 2]
        ctx.dma("gpsimd", X, X.ap, x2, x2_v[:, :, tc * 512:(tc + 1) * 512])
        emit_ln_fm2(ctx, "D", lambda kc: (X, X.ap[:, kc, :]), onesb, onesf,
                    murep, rstdrep, row4, scr)
        for kc in range(NC):
            ctx.rop("vector", lambda e, X=X, kc=kc: e.tensor_sub(tf.ap, X.ap[:, kc, :], murep.ap),
                    reads=[X, murep], writes=[tf])
            ctx.rop("vector", lambda e: e.tensor_mul(tf.ap, tf.ap, rstdrep.ap),
                    reads=[tf, rstdrep], writes=[tf])
            ctx.rop("vector", lambda e, kc=kc: e.tensor_scalar(
                hf.ap[:, kc, :], tf.ap, g2c.ap[:, kc:kc + 1], b2c.ap[:, kc:kc + 1],
                ALU.mult, ALU.add), reads=[tf, g2c, b2c], writes=[hf])
            ctx.rop("vector", lambda e, H=H, kc=kc: e.tensor_copy(
                H.ap[:, kc, :], hf.ap[:, kc, :]), reads=[hf], writes=[H])
        ctx.dma("sync", h2d, h2_v[:, :, tc * 512:(tc + 1) * 512], H, H.ap)
        Pl = ctx.bank(6)
        for kc in range(NC):
            ctx.rop("tensor", lambda e, kc=kc: e.matmul(
                Pl.ap[0:8, :], gw.ap[:, kc, :], hf.ap[:, kc, :],
                start=(kc == 0), stop=(kc == NC - 1)),
                reads=[gw, hf], writes=[Pl])
        ctx.rop("vector", lambda e: e.tensor_scalar_add(lg.ap, Pl.ap[0:8, :], gb.ap),
                reads=[Pl, gb], writes=[lg])
        for j in range(4):
            Pt = ctx.bank(7)
            ctx.rop("tensor", lambda e, j=j: e.transpose(
                Pt.ap[:, 0:8], lg.ap[:, j * 128:(j + 1) * 128], ctx.ident_f.ap[0:8, 0:8]),
                reads=[lg, ctx.ident_f], writes=[Pt])
            ctx.rop("vector", lambda e: e.tensor_copy(ltok.ap, Pt.ap[:, 0:8]),
                    reads=[Pt], writes=[ltok])
            ctx.rop("vector", lambda e: e.reduce_max(cols.ap[:, 0:1], ltok.ap, AX.X),
                    reads=[ltok], writes=[cols])
            ctx.rop("vector", lambda e: e.tensor_scalar(
                lmask.ap, ltok.ap, cols.ap[:, 0:1], None, ALU.is_equal),
                reads=[ltok, cols], writes=[lmask])
            ctx.rop("vector", lambda e: e.scalar_tensor_tensor(
                ewf.ap, lmask.ap, -1e38, ltok.ap, ALU.mult, ALU.add),
                reads=[lmask, ltok], writes=[ewf])
            ctx.rop("vector", lambda e: e.reduce_max(cols.ap[:, 1:2], ewf.ap, AX.X),
                    reads=[ewf], writes=[cols])
            ctx.rop("vector", lambda e: e.tensor_sub(cols.ap[:, 2:3], cols.ap[:, 1:2], cols.ap[:, 0:1]),
                    reads=[cols], writes=[cols])
            ctx.rop("scalar", lambda e: e.activation(cols.ap[:, 3:4], cols.ap[:, 2:3], AF.Exp),
                    reads=[cols], writes=[cols])
            ctx.rop("vector", lambda e: e.tensor_scalar_add(cols.ap[:, 4:5], cols.ap[:, 3:4], 1.0),
                    reads=[cols], writes=[cols])
            ctx.rop("vector", lambda e: e.reciprocal(cols.ap[:, 5:6], cols.ap[:, 4:5]),
                    reads=[cols], writes=[cols])
            ctx.rop("vector", lambda e: e.tensor_mul(cols.ap[:, 6:7], cols.ap[:, 3:4], cols.ap[:, 5:6]),
                    reads=[cols], writes=[cols])
            ctx.rop("vector", lambda e: e.tensor_scalar(
                cols.ap[:, 7:8], ltok.ap[:, 0:1], 0.0, 0.0, ALU.mult, ALU.mult),
                reads=[ltok], writes=[cols])
            ctx.rop("vector", lambda e: e.tensor_scalar(
                ewf.ap, ewf.ap, cols.ap[:, 1:2], None, ALU.is_equal),
                reads=[ewf, cols], writes=[ewf])
            ctx.rop("vector", lambda e: e.tensor_scalar(
                ewf.ap, ewf.ap, cols.ap[:, 6:7], None, ALU.mult),
                reads=[ewf, cols], writes=[ewf])
            ctx.rop("vector", lambda e: e.scalar_tensor_tensor(
                ewf.ap, lmask.ap, cols.ap[:, 5:6], ewf.ap, ALU.mult, ALU.add),
                reads=[lmask, cols, ewf], writes=[ewf])
            ctx.rop("vector", lambda e: e.tensor_copy(ewtok.ap, ewf.ap),
                    reads=[ewf], writes=[ewtok])
            Pe = ctx.bank(5)
            pv = Pe.ap.bitcast(BF16)[0:8, 0:128]
            ctx.rop("tensor", lambda e, pv=pv: e.transpose(pv, ewtok.ap, ctx.ident_b.ap),
                    reads=[ewtok, ctx.ident_b], writes=[Pe])
            ctx.rop("vector", lambda e, pv=pv: e.tensor_copy(ewev.ap, pv),
                    reads=[Pe], writes=[ewev])
            ctx.dma("sync", ewT, ewT.ap[:, tc * 512 + j * 128: tc * 512 + (j + 1) * 128],
                    ewev, ewev.ap)


def launch_D1():
    W = lambda name: (lambda st: np.ascontiguousarray(st["weights"][name]))
    return Launch(
        "D1",
        inputs=[
            ("x2_fm", (D, T), F32, lambda st: st["x2_fm"]),
            ("ln2_g", (D,), F32, W("L0_ffn_ln2_g")),
            ("ln2_b", (D,), F32, W("L0_ffn_ln2_b")),
            ("gate_w", (D, 8), F32, W("L0_ffn_gate_w")),
            ("gate_b", (8,), F32, W("L0_ffn_gate_b")),
        ],
        outputs=[("h2T", (D, T), BF16), ("ewT", (8, T), BF16)],
        emit=lambda ctx, aps: emit_stage_D1(ctx, aps),
    )


def emit_stage_D2(ctx, aps):
    E = 8
    DFF = 4096
    h2_v = aps["h2T"].ap.rearrange("(c p) t -> p c t", p=128)
    ffn_v = aps["ffn0_fm"].ap.rearrange("(c p) t -> p c t", p=128)
    x2_v = aps["x2_fm"].ap.rearrange("(c p) t -> p c t", p=128)
    onesb, onesf = make_ones(ctx)
    onesb_row = ctx.sbuf("E_onesrow", [1, 128], BF16)
    ctx.rop("vector", lambda e: e.memset(onesb_row.ap, 1.0), writes=[onesb_row])

    b1c = ctx.sbuf("E_b1c", [128, E, 32], F32)
    b1st = ctx.sbuf("E_b1st", [32, 128], F32)
    for e_ in range(E):
        ctx.dma("sync", b1st, b1st.ap, aps["b1"],
                aps["b1"].ap[e_, :].rearrange("(a b) -> a b", b=128))
        Pb = ctx.bank(5)
        ctx.rop("tensor", lambda en, e_=e_: en.transpose(
            Pb.ap[:, 0:32], b1st.ap, ctx.ident_f.ap[0:32, 0:32]),
            reads=[b1st, ctx.ident_f], writes=[Pb])
        ctx.rop("vector", lambda en, e_=e_: en.tensor_copy(b1c.ap[:, e_, :], Pb.ap[:, 0:32]),
                reads=[Pb], writes=[b1c])
    b2w = ctx.sbuf("E_b2w", [8, D], BF16)
    ctx.dma("gpsimd", b2w, b2w.ap, aps["b2"], aps["b2"].ap)
    ewTs = ctx.sbuf("E_ewT", [8, 2048], BF16)

    w1b = [ctx.sbuf(f"E_w1b{i}", [128, NC, 16, 128], BF16) for i in range(2)]
    w2b = [ctx.sbuf(f"E_w2b{i}", [128, 16, NC, 128], BF16) for i in range(2)]
    h2t2 = [ctx.sbuf("E_h2t0", [128, NC, 512], BF16)]
    hid = ctx.sbuf("E_hid", [128, 16, 512], BF16)
    hid_s = [Buf(f"E_hid{j}", hid.ap[:, j, :]) for j in range(16)]
    acc = ctx.sbuf("E_acc", [128, NC, 2048], BF16)
    ewrep = ctx.sbuf("E_ewrep", [128, 2048], BF16)
    ewrow = ctx.sbuf("E_ewrow", [1, 2048], BF16)
    tmp = ctx.sbuf("E_tmp", [128, 512], BF16)
    x2t = ctx.sbuf("E_x2t", [128, 512], F32)
    ff = ctx.sbuf("E_ff", [128, 512], F32)

    HALF = 2048
    small = bool(int(os.environ.get("MK_D2_SMALL", "0")))
    for half in range(1 if small else 2):
        t0 = half * HALF
        ctx.dma("sync", ewTs, ewTs.ap, aps["ewT"], aps["ewT"].ap[:, t0:t0 + HALF])
        for dc in range(NC):
            for q in range(4):
                Pq = ctx.bank(4 + (q % 2))
                ctx.rop("tensor", lambda e, dc=dc, q=q, Pq=Pq: e.matmul(
                    Pq.ap, b2w.ap[:, dc * 128:(dc + 1) * 128],
                    ewTs.ap[:, q * 512:(q + 1) * 512],
                    start=True, stop=True),
                    reads=[b2w, ewTs], writes=[Pq])
                ctx.rop("vector", lambda e, dc=dc, q=q, Pq=Pq: e.tensor_copy(
                    acc.ap[:, dc, q * 512:(q + 1) * 512], Pq.ap),
                    reads=[Pq], writes=[acc])
        for e_ in range(2 if small else E):
            ctx.dma("sync", ewrow, ewrow.ap, aps["ewT"],
                    aps["ewT"].ap[e_:e_ + 1, t0:t0 + HALF])
            for q in range(4):
                Pq = ctx.bank(4 + (q % 2))
                ctx.rop("tensor", lambda en, e_=e_, q=q, Pq=Pq: en.matmul(
                    Pq.ap, onesb_row.ap, ewrow.ap[:, q * 512:(q + 1) * 512],
                    start=True, stop=True),
                    reads=[onesb_row, ewrow], writes=[Pq])
                ctx.rop("vector", lambda en, q=q, Pq=Pq: en.tensor_copy(
                    ewrep.ap[:, q * 512:(q + 1) * 512], Pq.ap),
                    reads=[Pq], writes=[ewrep])
            for fh in range(2):
                W1 = w1b[fh % 2]
                W2 = w2b[fh % 2]
                f0 = fh * 2048
                for kc in range(NC):
                    ctx.dma("gpsimd", W1, W1.ap[:, kc, :, :].rearrange("p a b -> p (a b)"),
                            aps["W1"], aps["W1"].ap[e_, kc * 128:(kc + 1) * 128, f0:f0 + 2048])
                for fc in range(16):
                    ctx.dma("gpsimd", W2, W2.ap[:, fc, :, :].rearrange("p a b -> p (a b)"),
                            aps["W2"], aps["W2"].ap[e_, f0 + fc * 128: f0 + (fc + 1) * 128, :])
                for tcq in range(1 if small else 4):
                    q0 = t0 + tcq * 512
                    h2t = h2t2[0]
                    ctx.dma("sync", h2t, h2t.ap, aps["h2T"], h2_v[:, :, q0:q0 + 512])
                    for fc in range(16):
                        P = ctx.bank(fc % 2)
                        for kc in range(NC):
                            ctx.rop("tensor", lambda e, P=P, fc=fc, kc=kc, W1=W1, h2t=h2t: e.matmul(
                                P.ap, W1.ap[:, kc, fc, :], h2t.ap[:, kc, :],
                                start=(kc == 0), stop=(kc == NC - 1)),
                                reads=[W1, h2t], writes=[P])
                        gfc = fh * 16 + fc
                        ctx.rop("scalar", lambda e, P=P, fc=fc, e_=e_, gfc=gfc: e.activation(
                            hid_s[fc].ap, P.ap, AF.Relu, bias=b1c.ap[:, e_, gfc:gfc + 1]),
                            reads=[P, b1c], writes=[hid_s[fc]])
                    for dc in range(NC):
                        P = ctx.bank(2 + dc % 2)
                        for fc in range(16):
                            ctx.rop("tensor", lambda e, P=P, dc=dc, fc=fc, W2=W2: e.matmul(
                                P.ap, W2.ap[:, fc, dc, :], hid_s[fc].ap,
                                start=(fc == 0), stop=(fc == 15)),
                                reads=[W2, hid_s[fc]], writes=[P])
                        ctx.rop("vector", lambda e, P=P, tcq=tcq: e.tensor_mul(
                            tmp.ap, P.ap, ewrep.ap[:, tcq * 512:(tcq + 1) * 512]),
                            reads=[P, ewrep], writes=[tmp])
                        ctx.rop("gpsimd", lambda e, dc=dc, tcq=tcq: e.tensor_add(
                            acc.ap[:, dc, tcq * 512:(tcq + 1) * 512],
                            acc.ap[:, dc, tcq * 512:(tcq + 1) * 512], tmp.ap),
                            reads=[acc, tmp], writes=[acc])
        for dc in range(NC):
            for q in range(4):
                ctx.dma("sync", x2t, x2t.ap, aps["x2_fm"], x2_v[:, dc, t0 + q * 512:t0 + (q + 1) * 512])
                ctx.rop("vector", lambda e, dc=dc, q=q: e.tensor_add(
                    ff.ap, acc.ap[:, dc, q * 512:(q + 1) * 512], x2t.ap),
                    reads=[acc, x2t], writes=[ff])
                ctx.dma("sync", aps["ffn0_fm"], ffn_v[:, dc, t0 + q * 512:t0 + (q + 1) * 512],
                        ff, ff.ap)


def launch_D2():
    W = lambda name: (lambda st: np.ascontiguousarray(st["weights"][name]))
    return Launch(
        "D2",
        inputs=[
            ("h2T", (D, T), BF16, lambda st: st["h2T"]),
            ("ewT", (8, T), BF16, lambda st: st["ewT"]),
            ("x2_fm", (D, T), F32, lambda st: st["x2_fm"]),
            ("W1", (8, D, 4096), F32, W("L0_ffn_W1")),
            ("b1", (8, 4096), F32, W("L0_ffn_b1")),
            ("W2", (8, 4096, D), F32, W("L0_ffn_W2")),
            ("b2", (8, D), F32, W("L0_ffn_b2")),
        ],
        outputs=[("ffn0_fm", (D, T), F32)],
        emit=lambda ctx, aps: emit_stage_D2(ctx, aps),
    )


def emit_stage_E(ctx, aps):
    x_v = aps["x_fm"].ap.rearrange("(c p) t -> p c t", p=128)
    s_v = aps["s0_fm"].ap.rearrange("(c p) t -> p c t", p=128)
    f_v = aps["ffn0_fm"].ap.rearrange("(c p) t -> p c t", p=128)
    x1_v = aps["x1_fm"].ap.rearrange("(c p) t -> p c t", p=128)
    xn1_v = aps["xn1_fm"].ap.rearrange("(c p) t -> p c t", p=128)
    onesb, onesf = make_ones(ctx)
    xt = ctx.sbuf("Ex_xt", [128, 512], F32)
    st_ = ctx.sbuf("Ex_st", [128, 512], F32)
    ft = ctx.sbuf("Ex_ft", [128, 512], F32)
    x1f = ctx.sbuf("Ex_x1f", [128, NC, 512], F32)
    x1b = ctx.sbuf("Ex_x1b", [128, NC, 512], BF16)
    murep = ctx.sbuf("Ex_mu", [128, 512], F32)
    rstdrep = ctx.sbuf("Ex_rs", [128, 512], F32)
    row4 = ctx.sbuf("Ex_row4", [1, 4, 512], F32)
    scr = ctx.sbuf("Ex_scr", [128, 512], BF16)
    xnt = ctx.sbuf("Ex_xnt", [128, 512], F32)
    for tc in range(T // 512):
        for kc in range(NC):
            ctx.dma("sync", xt, xt.ap, aps["x_fm"], x_v[:, kc, tc * 512:(tc + 1) * 512])
            ctx.dma("sync", st_, st_.ap, aps["s0_fm"], s_v[:, kc, tc * 512:(tc + 1) * 512])
            ctx.dma("sync", ft, ft.ap, aps["ffn0_fm"], f_v[:, kc, tc * 512:(tc + 1) * 512])
            ctx.rop("scalar", lambda e: e.activation(ft.ap, ft.ap, AF.Silu),
                    reads=[ft], writes=[ft])
            ctx.rop("vector", lambda e: e.tensor_mul(ft.ap, ft.ap, st_.ap),
                    reads=[ft, st_], writes=[ft])
            ctx.rop("vector", lambda e, kc=kc: e.tensor_add(x1f.ap[:, kc, :], xt.ap, ft.ap),
                    reads=[xt, ft], writes=[x1f])
            ctx.rop("vector", lambda e, kc=kc: e.tensor_copy(x1b.ap[:, kc, :], x1f.ap[:, kc, :]),
                    reads=[x1f], writes=[x1b])
            ctx.dma("sync", aps["x1_fm"], x1_v[:, kc, tc * 512:(tc + 1) * 512],
                    x1f, x1f.ap[:, kc, :])
        emit_ln_fm2(ctx, "E", lambda kc: (x1b, x1b.ap[:, kc, :]), onesb, onesf,
                    murep, rstdrep, row4, scr)
        for kc in range(NC):
            ctx.rop("vector", lambda e, kc=kc: e.tensor_sub(xnt.ap, x1f.ap[:, kc, :], murep.ap),
                    reads=[x1f, murep], writes=[xnt])
            ctx.rop("vector", lambda e: e.tensor_mul(xnt.ap, xnt.ap, rstdrep.ap),
                    reads=[xnt, rstdrep], writes=[xnt])
            ctx.dma("sync", aps["xn1_fm"], xn1_v[:, kc, tc * 512:(tc + 1) * 512],
                    xnt, xnt.ap)


def launch_E():
    return Launch(
        "E",
        inputs=[
            ("x_fm", (D, T), F32, lambda st: st["x_fm"]),
            ("s0_fm", (D, T), F32, lambda st: st["s0_fm"]),
            ("ffn0_fm", (D, T), F32, lambda st: st["ffn0_fm"]),
        ],
        outputs=[("x1_fm", (D, T), F32), ("xn1_fm", (D, T), F32)],
        emit=lambda ctx, aps: emit_stage_E(ctx, aps),
    )


def emit_stage_F(ctx, aps):
    qkvT = aps["qkvT1"]
    qkv_v = qkvT.ap.rearrange("(c p) t -> p c t", p=128)
    xn_v = aps["xn1_fm"].ap.rearrange("(c p) t -> p c t", p=128)
    wqT = load_wT_bf(ctx, "F_wq", aps["qkv_w"], 3 * D, D)
    onesb, onesf = make_ones(ctx)
    qw = ctx.sbuf("F_qw", [128, 4], F32)
    for i, nm in enumerate(("q_norm_w", "q_norm_b", "k_norm_w", "k_norm_b")):
        for hh in range(2):
            ctx.dma("sync", qw, qw.ap[hh * 64:(hh + 1) * 64, i:i + 1],
                    aps[nm], aps[nm].ap.rearrange("d -> d ()"),
                    allow_slow_non_contiguous=True)
    maskb = ctx.sbuf("F_maskb", [128, 2], BF16)
    maskf = ctx.sbuf("F_maskf", [128, 2], F32)
    mask2f = ctx.sbuf("F_m2f", [2, 128], F32)
    ctx.rop("vector", lambda e: e.memset(maskf.ap, 0.0), writes=[maskf])
    ctx.rop("vector", lambda e: e.memset(maskf.ap[0:64, 0:1], 1.0), writes=[maskf])
    ctx.rop("vector", lambda e: e.memset(maskf.ap[64:128, 1:2], 1.0), writes=[maskf])
    ctx.rop("vector", lambda e: e.tensor_copy(maskb.ap, maskf.ap), reads=[maskf], writes=[maskb])
    _Pm = ctx.bank(7)
    ctx.rop("tensor", lambda e: e.transpose(_Pm.ap[0:2, 0:128], maskf.ap, ctx.ident_f.ap),
            reads=[maskf, ctx.ident_f], writes=[_Pm])
    ctx.rop("vector", lambda e: e.tensor_copy(mask2f.ap, _Pm.ap[0:2, 0:128]),
            reads=[_Pm], writes=[mask2f])

    xnt = [ctx.sbuf(f"F_xnt{i}", [128, NC, 512], BF16) for i in range(2)]
    qf = ctx.sbuf("F_qf", [128, 512], F32)
    qb_ = ctx.sbuf("F_qb", [128, 512], BF16)
    sqb = ctx.sbuf("F_sqb", [128, 512], BF16)
    rows = ctx.sbuf("F_rows", [2, 6, 512], F32)
    murep = ctx.sbuf("F_mur", [128, 512], F32)
    rsrep = ctx.sbuf("F_rsr", [128, 512], F32)
    kn_all = ctx.sbuf("F_knall", [128, NC, L], BF16)
    kmean = ctx.sbuf("F_kmean", [128, 2], F32)
    outb = ctx.sbuf("F_outb", [128, 512], BF16)

    def ln_head(P, dst_rows, tc, is_k, kc):
        ctx.rop("vector", lambda e: e.tensor_copy(qf.ap, P.ap), reads=[P], writes=[qf])
        ctx.rop("vector", lambda e: e.tensor_copy(qb_.ap, qf.ap), reads=[qf], writes=[qb_])
        ctx.rop("scalar", lambda e: e.square(sqb.ap, qb_.ap), reads=[qb_], writes=[sqb])
        Pr = ctx.bank(6)
        ctx.rop("tensor", lambda e: e.matmul(Pr.ap[0:2, :], maskb.ap, qb_.ap,
                                             start=True, stop=True),
                reads=[maskb, qb_], writes=[Pr])
        ctx.rop("vector", lambda e: e.tensor_scalar_mul(rows.ap[0:2, 2, :], Pr.ap[0:2, :], 1.0 / 64),
                reads=[Pr], writes=[rows])
        ctx.rop("tensor", lambda e: e.matmul(Pr.ap[0:2, :], maskb.ap, sqb.ap,
                                             start=True, stop=True),
                reads=[maskb, sqb], writes=[Pr])
        ctx.rop("vector", lambda e: e.tensor_scalar_mul(rows.ap[0:2, 1, :], Pr.ap[0:2, :], 1.0 / 64),
                reads=[Pr], writes=[rows])
        ctx.rop("vector", lambda e: e.tensor_mul(rows.ap[0:2, 0, :], rows.ap[0:2, 2, :], rows.ap[0:2, 2, :]),
                reads=[rows], writes=[rows])
        ctx.rop("vector", lambda e: e.tensor_sub(rows.ap[0:2, 0, :], rows.ap[0:2, 1, :], rows.ap[0:2, 0, :]),
                reads=[rows], writes=[rows])
        ctx.rop("scalar", lambda e: e.activation(rows.ap[0:2, 0, :], rows.ap[0:2, 0, :], AF.Sqrt,
                                                 bias=ctx.constcol(1e-5)[0:2, :]),
                reads=[rows, ctx.cconst], writes=[rows])
        ctx.rop("vector", lambda e: e.reciprocal(rows.ap[0:2, 3, :], rows.ap[0:2, 0, :]),
                reads=[rows], writes=[rows])
        Pb = ctx.bank(7)
        ctx.rop("tensor", lambda e: e.matmul(Pb.ap, mask2f.ap, rows.ap[0:2, 2, :],
                                             start=True, stop=True),
                reads=[mask2f, rows], writes=[Pb])
        ctx.rop("vector", lambda e: e.tensor_copy(murep.ap, Pb.ap), reads=[Pb], writes=[murep])
        ctx.rop("tensor", lambda e: e.matmul(Pb.ap, mask2f.ap, rows.ap[0:2, 3, :],
                                             start=True, stop=True),
                reads=[mask2f, rows], writes=[Pb])
        ctx.rop("vector", lambda e: e.tensor_copy(rsrep.ap, Pb.ap), reads=[Pb], writes=[rsrep])
        ctx.rop("vector", lambda e: e.tensor_sub(qf.ap, qf.ap, murep.ap),
                reads=[qf, murep], writes=[qf])
        ctx.rop("vector", lambda e: e.tensor_mul(qf.ap, qf.ap, rsrep.ap),
                reads=[qf, rsrep], writes=[qf])
        wcol = 2 if is_k else 0
        ctx.rop("vector", lambda e, wcol=wcol: e.tensor_scalar(
            outb.ap, qf.ap, qw.ap[:, wcol:wcol + 1], qw.ap[:, wcol + 1:wcol + 2],
            ALU.mult, ALU.add), reads=[qf, qw], writes=[outb])

    for tc in range(T // 512):
        X = xnt[tc % 2]
        ctx.dma("gpsimd", X, X.ap, aps["xn1_fm"], xn_v[:, :, tc * 512:(tc + 1) * 512])
        for mc in range(24):
            P = ctx.bank(mc % 2)
            for kc in range(NC):
                ctx.rop("tensor", lambda e, P=P, X=X, mc=mc, kc=kc: e.matmul(
                    P.ap, wqT.ap[:, kc, mc, :], X.ap[:, kc, :],
                    start=(kc == 0), stop=(kc == NC - 1)),
                    reads=[wqT, X], writes=[P])
            if mc < 8:
                ln_head(P, None, tc, False, mc)
                ctx.dma("sync", qkvT, qkv_v[:, mc, tc * 512:(tc + 1) * 512], outb, outb.ap)
            elif mc < 16:
                ln_head(P, None, tc, True, mc)
                s = tc // 4
                j = tc % 4
                kc_ = mc - 8
                ctx.rop("vector", lambda e, j=j, kc_=kc_: e.tensor_copy(
                    kn_all.ap[:, kc_, j * 512:(j + 1) * 512], outb.ap),
                    reads=[outb], writes=[kn_all])
                if j == 3:
                    ctx.rop("vector", lambda e, kc_=kc_: e.reduce_sum(
                        kmean.ap[:, 0:1], kn_all.ap[:, kc_, :], AX.X),
                        reads=[kn_all], writes=[kmean])
                    ctx.rop("vector", lambda e: e.tensor_scalar_mul(
                        kmean.ap[:, 1:2], kmean.ap[:, 0:1], 1.0 / L),
                        reads=[kmean], writes=[kmean])
                    for jj in range(4):
                        ctx.rop("vector", lambda e, jj=jj, kc_=kc_: e.tensor_scalar(
                            outb.ap, kn_all.ap[:, kc_, jj * 512:(jj + 1) * 512],
                            kmean.ap[:, 1:2], None, ALU.subtract),
                            reads=[kn_all, kmean], writes=[outb])
                        ctx.dma("sync", qkvT,
                                qkv_v[:, mc, s * L + jj * 512: s * L + (jj + 1) * 512],
                                outb, outb.ap)
            else:
                ctx.rop("vector", lambda e, P=P: e.tensor_copy(outb.ap, P.ap),
                        reads=[P], writes=[outb])
                ctx.dma("sync", qkvT, qkv_v[:, mc, tc * 512:(tc + 1) * 512], outb, outb.ap)


def launch_F():
    W = lambda name: (lambda st: np.ascontiguousarray(st["weights"][name]))
    return Launch(
        "F",
        inputs=[
            ("xn1_fm", (D, T), F32, lambda st: st["xn1_fm"]),
            ("qkv_w", (3 * D, D), F32, W("L1_attn_qkv_w")),
            ("q_norm_w", (64,), F32, W("L1_attn_q_norm_w")),
            ("q_norm_b", (64,), F32, W("L1_attn_q_norm_b")),
            ("k_norm_w", (64,), F32, W("L1_attn_k_norm_w")),
            ("k_norm_b", (64,), F32, W("L1_attn_k_norm_b")),
        ],
        outputs=[("qkvT1", (3 * D, T), BF16)],
        emit=lambda ctx, aps: emit_stage_F(ctx, aps),
    )


def emit_stage_F2(ctx, aps):
    oT_dram = ctx.dram("F2_oT", [D, T], BF16)
    emit_attention(ctx, "F2", aps["qkvT1"], oT_dram)
    wpT = load_wT_bf(ctx, "F2_wp", aps["proj_w"], D, D)
    pb = load_param_cols(ctx, "F2_pb", aps["proj_b"])
    oT_v = oT_dram.ap.rearrange("(c p) t -> p c t", p=128)
    s1_v = aps["s1_fm"].ap.rearrange("(c p) t -> p c t", p=128)
    ot = [ctx.sbuf(f"F2_ot{i}", [128, NC, 512], BF16) for i in range(2)]
    s1t = ctx.sbuf("F2_s1t", [128, 512], F32)
    for tc in range(T // 512):
        O = ot[tc % 2]
        ctx.dma("sync", O, O.ap, oT_dram, oT_v[:, :, tc * 512:(tc + 1) * 512])
        for mc in range(NC):
            P = ctx.bank(mc % 2)
            for kc in range(NC):
                ctx.rop("tensor", lambda e, P=P, O=O, mc=mc, kc=kc: e.matmul(
                    P.ap, wpT.ap[:, kc, mc, :], O.ap[:, kc, :],
                    start=(kc == 0), stop=(kc == NC - 1)),
                    reads=[wpT, O], writes=[P])
            ctx.rop("vector", lambda e, P=P, mc=mc: e.tensor_scalar_add(
                s1t.ap, P.ap, pb.ap[:, mc:mc + 1]), reads=[P, pb], writes=[s1t])
            ctx.dma("sync", aps["s1_fm"], s1_v[:, mc, tc * 512:(tc + 1) * 512], s1t, s1t.ap)


def launch_F2():
    W = lambda name: (lambda st: np.ascontiguousarray(st["weights"][name]))
    return Launch(
        "F2",
        inputs=[
            ("qkvT1", (3 * D, T), BF16, lambda st: st["qkvT1"]),
            ("proj_w", (D, D), F32, W("L1_attn_proj_w")),
            ("proj_b", (D,), F32, W("L1_attn_proj_b")),
        ],
        outputs=[("s1_fm", (D, T), F32)],
        emit=lambda ctx, aps: emit_stage_F2(ctx, aps),
    )


def emit_fp4_quant(ctx, xb, x_ap, ob, o_ap, F, bufs):
    nb = F // 16
    bm, sc, iv, vv, cc, gg, sg = (bufs[k] for k in ("bm", "sc", "iv", "vv", "cc", "gg", "sg"))
    x3 = x_ap.rearrange("p (a b) -> p a b", b=16)
    ctx.rop("vector", lambda e: e.reduce_max(bm.ap[:, 0:nb], x3, AX.X,
                                             apply_absolute_value=True),
            reads=[xb], writes=[bm])
    ctx.rop("vector", lambda e: e.tensor_scalar(sc.ap[:, 0:nb], bm.ap[:, 0:nb],
                                                1.0 / 6.0, 1e-12, ALU.mult, ALU.max),
            reads=[bm], writes=[sc])
    ctx.rop("vector", lambda e: e.reciprocal(iv.ap[:, 0:nb], sc.ap[:, 0:nb]),
            reads=[sc], writes=[iv])
    ivrep = iv.ap[:, 0:nb].rearrange("p a -> p a ()").broadcast_to([128, nb, 16])
    screp = sc.ap[:, 0:nb].rearrange("p a -> p a ()").broadcast_to([128, nb, 16])
    vv3 = vv.ap[:, 0:F].rearrange("p (a b) -> p a b", b=16)
    ctx.rop("scalar", lambda e: e.activation(vv.ap[:, 0:F], x_ap, AF.Abs),
            reads=[xb], writes=[vv])
    ctx.rop("vector", lambda e: e.tensor_tensor(vv3, vv3, ivrep, ALU.mult),
            reads=[vv, iv], writes=[vv])
    ctx.rop("vector", lambda e: e.tensor_scalar(cc.ap[:, 0:F], vv.ap[:, 0:F], 0.25, None, ALU.is_gt),
            reads=[vv], writes=[cc])
    for thr in (0.75, 1.25, 1.75):
        ctx.rop("vector", lambda e, thr=thr: e.scalar_tensor_tensor(
            cc.ap[:, 0:F], vv.ap[:, 0:F], thr, cc.ap[:, 0:F], ALU.is_gt, ALU.add),
            reads=[vv, cc], writes=[cc])
    ctx.rop("vector", lambda e: e.tensor_scalar(gg.ap[:, 0:F], vv.ap[:, 0:F], 2.5, None, ALU.is_gt),
            reads=[vv], writes=[gg])
    ctx.rop("vector", lambda e: e.scalar_tensor_tensor(
        gg.ap[:, 0:F], vv.ap[:, 0:F], 3.5, gg.ap[:, 0:F], ALU.is_gt, ALU.add),
        reads=[vv, gg], writes=[gg])
    ctx.rop("vector", lambda e: e.tensor_scalar(sg.ap[:, 0:F], vv.ap[:, 0:F], 5.0, 2.0, ALU.is_gt, ALU.mult),
            reads=[vv], writes=[sg])
    ctx.rop("vector", lambda e: e.tensor_add(gg.ap[:, 0:F], gg.ap[:, 0:F], sg.ap[:, 0:F]),
            reads=[gg, sg], writes=[gg])
    ctx.rop("vector", lambda e: e.scalar_tensor_tensor(
        gg.ap[:, 0:F], cc.ap[:, 0:F], 0.5, gg.ap[:, 0:F], ALU.mult, ALU.add),
        reads=[cc, gg], writes=[gg])
    ctx.rop("scalar", lambda e: e.sign(sg.ap[:, 0:F], x_ap), reads=[xb], writes=[sg])
    ctx.rop("vector", lambda e: e.tensor_mul(gg.ap[:, 0:F], gg.ap[:, 0:F], sg.ap[:, 0:F]),
            reads=[gg, sg], writes=[gg])
    gg3 = gg.ap[:, 0:F].rearrange("p (a b) -> p a b", b=16)
    o3 = o_ap.rearrange("p (a b) -> p a b", b=16)
    ctx.rop("vector", lambda e: e.tensor_tensor(o3, gg3, screp, ALU.mult),
            reads=[gg, sc], writes=[ob])


def _quant_bufs(ctx, tag, F):
    return {
        "bm": ctx.sbuf(f"{tag}_bm", [128, F // 16], F32),
        "sc": ctx.sbuf(f"{tag}_sc", [128, F // 16], F32),
        "iv": ctx.sbuf(f"{tag}_iv", [128, F // 16], F32),
        "vv": ctx.sbuf(f"{tag}_vv", [128, F], F32),
        "cc": ctx.sbuf(f"{tag}_cc", [128, F], F32),
        "gg": ctx.sbuf(f"{tag}_gg", [128, F], F32),
        "sg": ctx.sbuf(f"{tag}_sg", [128, F], F32),
    }


def emit_stage_G0(ctx, aps):
    IC = 2816
    ICT = IC // 128
    bufs = _quant_bufs(ctx, "G0", IC)
    wrow = ctx.sbuf("G0_wrow", [128, IC], F32)
    wq = ctx.sbuf("G0_wq", [128, IC], BF16)
    ev = ctx.sbuf("G0_ev", [128, 128], BF16)
    for (nm, dst, nrt, ncol) in (("w_gate", "wgT", ICT, D), ("w_up", "wuT", ICT, D),
                                 ("w_down", "wdT", NC, IC)):
        for rt in range(nrt):
            ctx.dma("sync", wrow, wrow.ap[:, 0:ncol], aps[nm],
                    aps[nm].ap[rt * 128:(rt + 1) * 128, :])
            emit_fp4_quant(ctx, wrow, wrow.ap[:, 0:ncol], wq, wq.ap[:, 0:ncol], ncol, bufs)
            for kc in range(ncol // 128):
                P = ctx.bank(4 + kc % 2)
                pv = P.ap.bitcast(BF16)[:, 0:128]
                ctx.rop("tensor", lambda e, kc=kc, pv=pv: e.transpose(
                    pv, wq.ap[:, kc * 128:(kc + 1) * 128], ctx.ident_b.ap),
                    reads=[wq, ctx.ident_b], writes=[P])
                ctx.rop("vector", lambda e, pv=pv: e.tensor_copy(ev.ap, pv),
                        reads=[P], writes=[ev])
                ctx.dma("sync", aps[dst],
                        aps[dst].ap[kc * 128:(kc + 1) * 128, rt * 128:(rt + 1) * 128],
                        ev, ev.ap)


def launch_G0():
    W = lambda name: (lambda st: np.ascontiguousarray(st["weights"][name]))
    IC = 2816
    return Launch(
        "G0",
        inputs=[
            ("w_gate", (IC, D), F32, W("L1_ffn_w_gate")),
            ("w_up", (IC, D), F32, W("L1_ffn_w_up")),
            ("w_down", (D, IC), F32, W("L1_ffn_w_down")),
        ],
        outputs=[("wgT", (D, IC), BF16), ("wuT", (D, IC), BF16), ("wdT", (IC, D), BF16)],
        emit=lambda ctx, aps: emit_stage_G0(ctx, aps),
    )


def emit_stage_G1(ctx, aps):
    IC = 2816
    ICT = IC // 128
    xn_v = aps["xn1_fm"].ap.rearrange("(c p) t -> p c t", p=128)
    hT_v = aps["hT"].ap.rearrange("(c p) t -> p c t", p=128)
    bufs = _quant_bufs(ctx, "G1", D)
    xnt = ctx.sbuf("G1_xnt", [128, NC, 512], F32)
    xtok = ctx.sbuf("G1_xtok", [128, D], F32)
    xqt = ctx.sbuf("G1_xqt", [128, D], BF16)
    xq = ctx.sbuf("G1_xq", [128, NC, T], BF16)
    for tc in range(T // 512):
        ctx.dma("sync", xnt, xnt.ap, aps["xn1_fm"], xn_v[:, :, tc * 512:(tc + 1) * 512])
        for sub in range(4):
            for kc in range(NC):
                P = ctx.bank(kc % 2)
                ctx.rop("tensor", lambda e, P=P, kc=kc, sub=sub: e.transpose(
                    P.ap[:, 0:128], xnt.ap[:, kc, sub * 128:(sub + 1) * 128], ctx.ident_f.ap),
                    reads=[xnt, ctx.ident_f], writes=[P])
                ctx.rop("vector", lambda e, P=P, kc=kc: e.tensor_copy(
                    xtok.ap[:, kc * 128:(kc + 1) * 128], P.ap[:, 0:128]),
                    reads=[P], writes=[xtok])
            emit_fp4_quant(ctx, xtok, xtok.ap, xqt, xqt.ap, D, bufs)
            for kc in range(NC):
                P = ctx.bank(2 + kc % 2)
                pv = P.ap.bitcast(BF16)[:, 0:128]
                ctx.rop("tensor", lambda e, kc=kc, pv=pv: e.transpose(
                    pv, xqt.ap[:, kc * 128:(kc + 1) * 128], ctx.ident_b.ap),
                    reads=[xqt, ctx.ident_b], writes=[P])
                ctx.rop("vector", lambda e, pv=pv, kc=kc, tc=tc, sub=sub: e.tensor_copy(
                    xq.ap[:, kc, tc * 512 + sub * 128: tc * 512 + (sub + 1) * 128], pv),
                    reads=[P], writes=[xq])
    wg = ctx.sbuf("G1_wg", [128, NC, 128], BF16)
    wu = ctx.sbuf("G1_wu", [128, NC, 128], BF16)
    gs = ctx.sbuf("G1_gs", [128, 512], F32)
    hb = ctx.sbuf("G1_hb", [128, 512], F32)
    for mc in range(ICT):
        ctx.dma("gpsimd", wg, wg.ap, aps["wgT"],
                aps["wgT"].ap.rearrange("(c p) m -> p c m", p=128)[:, :, mc * 128:(mc + 1) * 128])
        ctx.dma("gpsimd", wu, wu.ap, aps["wuT"],
                aps["wuT"].ap.rearrange("(c p) m -> p c m", p=128)[:, :, mc * 128:(mc + 1) * 128])
        for tc in range(T // 512):
            Pg = ctx.bank(4 + tc % 2)
            Pu = ctx.bank(6 + tc % 2)
            for kc in range(NC):
                ctx.rop("tensor", lambda e, Pg=Pg, mc=mc, kc=kc, tc=tc: e.matmul(
                    Pg.ap, wg.ap[:, kc, :], xq.ap[:, kc, tc * 512:(tc + 1) * 512],
                    start=(kc == 0), stop=(kc == NC - 1)),
                    reads=[wg, xq], writes=[Pg])
            for kc in range(NC):
                ctx.rop("tensor", lambda e, Pu=Pu, mc=mc, kc=kc, tc=tc: e.matmul(
                    Pu.ap, wu.ap[:, kc, :], xq.ap[:, kc, tc * 512:(tc + 1) * 512],
                    start=(kc == 0), stop=(kc == NC - 1)),
                    reads=[wu, xq], writes=[Pu])
            ctx.rop("scalar", lambda e, Pg=Pg: e.activation(gs.ap, Pg.ap, AF.Silu),
                    reads=[Pg], writes=[gs])
            ctx.rop("vector", lambda e, Pu=Pu: e.tensor_mul(hb.ap, gs.ap, Pu.ap),
                    reads=[gs, Pu], writes=[hb])
            ctx.dma("sync", aps["hT"], hT_v[:, mc, tc * 512:(tc + 1) * 512], hb, hb.ap)


def launch_G1():
    IC = 2816
    return Launch(
        "G1",
        inputs=[
            ("xn1_fm", (D, T), F32, lambda st: st["xn1_fm"]),
            ("wgT", (D, IC), BF16, lambda st: st["wgT"]),
            ("wuT", (D, IC), BF16, lambda st: st["wuT"]),
        ],
        outputs=[("hT", (IC, T), F32)],
        emit=lambda ctx, aps: emit_stage_G1(ctx, aps),
    )


def emit_stage_G3(ctx, aps):
    IC = 2816
    ICT = IC // 128
    hT_v = aps["hT"].ap.rearrange("(c p) t -> p c t", p=128)
    g1_v = aps["g1_fm"].ap.rearrange("(c p) t -> p c t", p=128)
    bufs = _quant_bufs(ctx, "G3", IC)
    wd = ctx.sbuf("G3_wd", [128, ICT, NC, 128], BF16)
    for kc in range(ICT):
        ctx.dma("gpsimd", wd, wd.ap[:, kc, :, :].rearrange("p a b -> p (a b)"),
                aps["wdT"], aps["wdT"].ap[kc * 128:(kc + 1) * 128, :])
    ht = ctx.sbuf("G3_ht", [128, ICT, 128], F32)
    htok = ctx.sbuf("G3_htok", [128, IC], F32)
    hqt = ctx.sbuf("G3_hqt", [128, IC], BF16)
    hq = ctx.sbuf("G3_hq", [128, ICT, 512], BF16)
    g1t = ctx.sbuf("G3_g1t", [128, 512], F32)
    for tc in range(T // 512):
        for sub in range(4):
            s0_ = tc * 512 + sub * 128
            ctx.dma("sync", ht, ht.ap, aps["hT"], hT_v[:, :, s0_:s0_ + 128])
            for kc in range(ICT):
                P = ctx.bank(kc % 2)
                ctx.rop("tensor", lambda e, kc=kc, P=P: e.transpose(
                    P.ap[:, 0:128], ht.ap[:, kc, :], ctx.ident_f.ap),
                    reads=[ht, ctx.ident_f], writes=[P])
                ctx.rop("vector", lambda e, P=P, kc=kc: e.tensor_copy(
                    htok.ap[:, kc * 128:(kc + 1) * 128], P.ap[:, 0:128]),
                    reads=[P], writes=[htok])
            emit_fp4_quant(ctx, htok, htok.ap, hqt, hqt.ap, IC, bufs)
            for kc in range(ICT):
                P = ctx.bank(2 + kc % 2)
                pv = P.ap.bitcast(BF16)[:, 0:128]
                ctx.rop("tensor", lambda e, kc=kc, pv=pv: e.transpose(
                    pv, hqt.ap[:, kc * 128:(kc + 1) * 128], ctx.ident_b.ap),
                    reads=[hqt, ctx.ident_b], writes=[P])
                ctx.rop("vector", lambda e, pv=pv, kc=kc, sub=sub: e.tensor_copy(
                    hq.ap[:, kc, sub * 128:(sub + 1) * 128], pv),
                    reads=[P], writes=[hq])
        for mc in range(NC):
            P = ctx.bank(4 + mc % 2)
            for kc in range(ICT):
                ctx.rop("tensor", lambda e, P=P, mc=mc, kc=kc: e.matmul(
                    P.ap, wd.ap[:, kc, mc, :], hq.ap[:, kc, :],
                    start=(kc == 0), stop=(kc == ICT - 1)),
                    reads=[wd, hq], writes=[P])
            ctx.rop("vector", lambda e, P=P: e.tensor_copy(g1t.ap, P.ap),
                    reads=[P], writes=[g1t])
            ctx.dma("sync", aps["g1_fm"], g1_v[:, mc, tc * 512:(tc + 1) * 512], g1t, g1t.ap)


def launch_G3():
    IC = 2816
    return Launch(
        "G3",
        inputs=[
            ("hT", (IC, T), F32, lambda st: st["hT"]),
            ("wdT", (IC, D), BF16, lambda st: st["wdT"]),
        ],
        outputs=[("g1_fm", (D, T), F32)],
        emit=lambda ctx, aps: emit_stage_G3(ctx, aps),
    )


def emit_stage_H(ctx, aps):
    x1_v = aps["x1_fm"].ap.rearrange("(c p) t -> p c t", p=128)
    s1_v = aps["s1_fm"].ap.rearrange("(c p) t -> p c t", p=128)
    g1_v = aps["g1_fm"].ap.rearrange("(c p) t -> p c t", p=128)
    xt = ctx.sbuf("H_xt", [128, 512], F32)
    st_ = ctx.sbuf("H_st", [128, 512], F32)
    gt = ctx.sbuf("H_gt", [128, 512], F32)
    ofm = ctx.sbuf("H_ofm", [128, NC, 512], F32)
    otok = [ctx.sbuf(f"H_otok{i}", [128, D], F32) for i in range(2)]
    for tc in range(T // 512):
        for kc in range(NC):
            ctx.dma("sync", xt, xt.ap, aps["x1_fm"], x1_v[:, kc, tc * 512:(tc + 1) * 512])
            ctx.dma("sync", st_, st_.ap, aps["s1_fm"], s1_v[:, kc, tc * 512:(tc + 1) * 512])
            ctx.dma("sync", gt, gt.ap, aps["g1_fm"], g1_v[:, kc, tc * 512:(tc + 1) * 512])
            ctx.rop("scalar", lambda e: e.activation(gt.ap, gt.ap, AF.Silu),
                    reads=[gt], writes=[gt])
            ctx.rop("vector", lambda e: e.tensor_mul(gt.ap, gt.ap, st_.ap),
                    reads=[gt, st_], writes=[gt])
            ctx.rop("vector", lambda e, kc=kc: e.tensor_add(ofm.ap[:, kc, :], xt.ap, gt.ap),
                    reads=[xt, gt], writes=[ofm])
        for sub in range(4):
            O = otok[sub % 2]
            for kc in range(NC):
                P = ctx.bank(kc % 2)
                ctx.rop("tensor", lambda e, P=P, kc=kc, sub=sub: e.transpose(
                    P.ap[:, 0:128], ofm.ap[:, kc, sub * 128:(sub + 1) * 128], ctx.ident_f.ap),
                    reads=[ofm, ctx.ident_f], writes=[P])
                ctx.rop("vector", lambda e, P=P, kc=kc, O=O: e.tensor_copy(
                    O.ap[:, kc * 128:(kc + 1) * 128], P.ap[:, 0:128]),
                    reads=[P], writes=[O])
            r0 = tc * 512 + sub * 128
            ctx.dma("sync", aps["out"], aps["out"].ap[r0:r0 + 128, :], O, O.ap)


def launch_H():
    return Launch(
        "H",
        inputs=[
            ("x1_fm", (D, T), F32, lambda st: st["x1_fm"]),
            ("s1_fm", (D, T), F32, lambda st: st["s1_fm"]),
            ("g1_fm", (D, T), F32, lambda st: st["g1_fm"]),
        ],
        outputs=[("out", (T, D), F32)],
        emit=lambda ctx, aps: emit_stage_H(ctx, aps),
    )


def kernel_chain(**inputs):
    states = shard_inputs(inputs)
    for mk in (launch_A, launch_B, launch_C, launch_C2, launch_D1, launch_D2,
               launch_E, launch_F, launch_F2, launch_G0, launch_G1, launch_G3,
               launch_H):
        mk().run(states)
    out = np.stack([st["out"].reshape(NSEQ, L, D) for st in states])
    return np.ascontiguousarray(out.reshape(16, L, D))


def emit_fused(ctx, aps):
    IC = 2816
    inter = {}
    for nm, shape, dt in (
        ("x_fm", (D, T), F32), ("xn0_fm", (D, T), F32), ("s0_fm", (D, T), F32),
        ("qkvT", (3 * D, T), BF16), ("x2_fm", (D, T), F32),
        ("h2T", (D, T), BF16), ("ewT", (8, T), BF16), ("ffn0_fm", (D, T), F32),
        ("x1_fm", (D, T), F32), ("xn1_fm", (D, T), F32),
        ("qkvT1", (3 * D, T), BF16), ("s1_fm", (D, T), F32),
        ("wgT", (D, IC), BF16), ("wuT", (D, IC), BF16), ("wdT", (IC, D), BF16),
        ("hT", (IC, T), F32), ("g1_fm", (D, T), F32),
    ):
        inter[nm] = ctx.dram("i_" + nm, list(shape), dt, kind="Internal")
    env = dict(aps)
    env.update(inter)
    make_ones(ctx)
    ctx.bank(0)
    ctx.vfence(ctx.bank(0))

    stages = [
        lambda: emit_stage_A(ctx, env["xin"], env["x_fm"], env["xn0_fm"]),
        lambda: emit_stage_B(ctx, env),
        lambda: emit_stage_C(ctx, env, env["qkvT"]),
        lambda: emit_stage_C2(ctx, env),
        lambda: emit_stage_D1(ctx, env),
        lambda: emit_stage_D2(ctx, env),
        lambda: emit_stage_E(ctx, env),
        lambda: emit_stage_F(ctx, env),
        lambda: emit_stage_F2(ctx, env),
        lambda: emit_stage_G0(ctx, env),
        lambda: emit_stage_G1(ctx, env),
        lambda: emit_stage_G3(ctx, env),
        lambda: emit_stage_H(ctx, env),
    ]
    for st in stages:
        with ctx.scope():
            st()
        ctx.barrier()


def launch_FUSED():
    W = lambda name: (lambda st: np.ascontiguousarray(st["weights"][name]))
    IC = 2816
    return Launch(
        "FUSED",
        inputs=[
            ("xin", (T, D), F32, lambda st: st["xin"]),
            ("conv_w", (D, 4), F32, lambda st: np.ascontiguousarray(
                st["weights"]["L0_attn_conv_w"].reshape(D, 4))),
            ("conv_b", (D,), F32, W("L0_attn_conv_b")),
            ("W_r", (D, D), F32, W("L0_attn_W_r")),
            ("b_r", (D,), F32, W("L0_attn_b_r")),
            ("W_i", (D, D), F32, W("L0_attn_W_i")),
            ("b_i", (D,), F32, W("L0_attn_b_i")),
            ("log_a", (D,), F32, W("L0_attn_log_a")),
            ("W_out", (D, D), F32, W("L0_attn_W_out")),
            ("gamma", (D,), F32, W("L0_attn_gamma")),
            ("ln1_g", (D,), F32, W("L0_ffn_ln1_g")),
            ("ln1_b", (D,), F32, W("L0_ffn_ln1_b")),
            ("w_qkv", (D, 3 * D), F32, W("L0_ffn_w_qkv")),
            ("b_qkv", (3 * D,), F32, W("L0_ffn_b_qkv")),
            ("w_o", (D, D), F32, W("L0_ffn_w_o")),
            ("b_o", (D,), F32, W("L0_ffn_b_o")),
            ("ln2_g", (D,), F32, W("L0_ffn_ln2_g")),
            ("ln2_b", (D,), F32, W("L0_ffn_ln2_b")),
            ("gate_w", (D, 8), F32, W("L0_ffn_gate_w")),
            ("gate_b", (8,), F32, W("L0_ffn_gate_b")),
            ("W1", (8, D, 4096), F32, W("L0_ffn_W1")),
            ("b1", (8, 4096), F32, W("L0_ffn_b1")),
            ("W2", (8, 4096, D), F32, W("L0_ffn_W2")),
            ("b2", (8, D), F32, W("L0_ffn_b2")),
            ("qkv_w", (3 * D, D), F32, W("L1_attn_qkv_w")),
            ("q_norm_w", (64,), F32, W("L1_attn_q_norm_w")),
            ("q_norm_b", (64,), F32, W("L1_attn_q_norm_b")),
            ("k_norm_w", (64,), F32, W("L1_attn_k_norm_w")),
            ("k_norm_b", (64,), F32, W("L1_attn_k_norm_b")),
            ("proj_w", (D, D), F32, W("L1_attn_proj_w")),
            ("proj_b", (D,), F32, W("L1_attn_proj_b")),
            ("w_gate", (IC, D), F32, W("L1_ffn_w_gate")),
            ("w_up", (IC, D), F32, W("L1_ffn_w_up")),
            ("w_down", (D, IC), F32, W("L1_ffn_w_down")),
        ],
        outputs=[("out", (T, D), F32)],
        emit=lambda ctx, aps: emit_fused(ctx, aps),
    )


def kernel(**inputs):
    states = shard_inputs(inputs)
    launch_FUSED().run(states)
    out = np.stack([st["out"].reshape(NSEQ, L, D) for st in states])
    return np.ascontiguousarray(out.reshape(16, L, D))
